# version 14
# baseline (speedup 1.0000x reference)
import contextlib
import os
import numpy as np
import concourse.bass as bass
import concourse.mybir as mybir
from concourse.bass_utils import run_bass_kernel_spmd

F32 = mybir.dt.float32
BF16 = mybir.dt.bfloat16
I32 = mybir.dt.int32
AF = mybir.ActivationFunctionType
ALU = mybir.AluOpType
AX = mybir.AxisListType

ENGINES = ("tensor", "vector", "scalar", "gpsimd", "sync")


class ChanTotal:
    __slots__ = ("eng", "ch")

    def __init__(self, eng, ch):
        self.eng, self.ch = eng, ch


class Handle:
    __slots__ = ("eng", "idx")

    def __init__(self, eng, idx):
        self.eng, self.idx = eng, idx


class _Op:
    __slots__ = ("emit", "waits", "sig", "ch", "tag")

    def __init__(self, emit, waits, ch, tag):
        self.emit = emit
        self.waits = list(waits)
        self.sig = None
        self.ch = ch
        self.tag = tag


class _EngineQ:
    def __init__(self, name):
        self.name = name
        self.ops = []

    def op(self, method, *args, wait=(), tag=None, **kwargs):
        def emit(eng):
            return getattr(eng, method)(*args, **kwargs)
        self.ops.append(_Op(emit, wait, None, tag or method))
        return Handle(self.name, len(self.ops) - 1)

    def raw(self, fn, wait=(), tag="raw"):
        self.ops.append(_Op(fn, wait, None, tag))
        return Handle(self.name, len(self.ops) - 1)

    def dma(self, out, in_, wait=(), ch="m", tag="dma", **kw):
        def emit(eng):
            return eng.dma_start(out=out, in_=in_, **kw)
        o = _Op(emit, wait, ch, tag)
        o.sig = True
        self.ops.append(o)
        return Handle(self.name, len(self.ops) - 1)

    def __getattr__(self, method):
        def f(*args, wait=(), tag=None, **kwargs):
            return self.op(method, *args, wait=wait, tag=tag, **kwargs)
        return f


class Sched:
    def __init__(self, nc):
        self.nc = nc
        self.q = {e: _EngineQ(e) for e in ENGINES}
        self.tensor, self.vector, self.scalar, self.gpsimd, self.sync = (
            self.q[e] for e in ENGINES)
        self._stack = contextlib.ExitStack()
        self._sems = {}

    def _sem(self, key):
        if key not in self._sems:
            self._sems[key] = self._stack.enter_context(
                self.nc.semaphore("s_" + "_".join(str(k) for k in key)))
        return self._sems[key]

    def finalize(self):
        for e in ENGINES:
            for o in self.q[e].ops:
                for h in o.waits:
                    if isinstance(h, Handle):
                        self.q[h.eng].ops[h.idx].sig = True
        for e in ENGINES:
            csig, dsig = 0, {}
            for o in self.q[e].ops:
                if o.sig:
                    if o.ch is not None:
                        dsig[o.ch] = dsig.get(o.ch, 0) + 16
                        o.sig = dsig[o.ch]
                    else:
                        csig += 1
                        o.sig = csig
        totals = {}
        for e in ENGINES:
            for o in self.q[e].ops:
                if o.ch is not None and isinstance(o.sig, int):
                    totals[(e, o.ch)] = max(totals.get((e, o.ch), 0), o.sig)

        def rtok(h):
            if isinstance(h, Handle):
                o = self.q[h.eng].ops[h.idx]
                if o.ch is not None:
                    return ((h.eng, "d", o.ch), o.sig)
                return ((h.eng, "c"), o.sig)
            if isinstance(h, ChanTotal):
                return ((h.eng, "d", h.ch), totals.get((h.eng, h.ch), 0))
            return h
        for e in ENGINES:
            for o in self.q[e].ops:
                o.waits = [rtok(h) for h in o.waits]
        with self.nc.Block() as block:
            for e in ENGINES:
                ops = self.q[e].ops
                if not ops:
                    continue

                def body(eng, _ops=ops, _e=e):
                    water = {}
                    for o in _ops:
                        for (sk, sv) in o.waits:
                            if water.get(sk, 0) >= sv:
                                continue
                            water[sk] = sv
                            eng.wait_ge(self._sem(sk), sv)
                        ins = o.emit(eng)
                        if o.sig is not None:
                            if o.ch is not None:
                                ins.then_inc(self._sem((_e, "d", o.ch)), 16)
                            else:
                                if ins is None:
                                    raise RuntimeError(f"op {o.tag} on {_e} returned None")
                                ins.then_inc(self._sem((_e, "c")), 1)
                getattr(block, e)(body)
        self._stack.close()


S = 4096
D = 1024
CT = 8
TT = 32
NH = 8
DH = 64
W = 512
GD = 2730
NQH, NKV = 16, 4
BIGNEG = -1e9


def build(stage="full"):
    nc = bass.Bass("TRN2", target_bir_lowering=False, debug=False)

    hs = nc.dram_tensor("hs", [S, D], F32, kind="ExternalInput").ap()
    wq = nc.dram_tensor("L0_attn_Wq", [1024, 1024], F32, kind="ExternalInput").ap()
    wk = nc.dram_tensor("L0_attn_Wk", [256, 1024], F32, kind="ExternalInput").ap()
    wv = nc.dram_tensor("L0_attn_Wv", [256, 1024], F32, kind="ExternalInput").ap()
    wo_big = nc.dram_tensor("L0_attn_Wo", [1024, 1024], F32, kind="ExternalInput").ap()
    wqkv = nc.dram_tensor("L0_ffn_wqkv", [2, 1024, 1536], F32, kind="ExternalInput").ap()
    qs_in = nc.dram_tensor("L0_ffn_q_scale", [2, 64], F32, kind="ExternalInput").ap()
    ks_in = nc.dram_tensor("L0_ffn_k_scale", [2, 64], F32, kind="ExternalInput").ap()
    wo_ffn = nc.dram_tensor("L0_ffn_wo", [2, 512, 1024], F32, kind="ExternalInput").ap()
    lng = nc.dram_tensor("L0_ffn_ln_g", [2, 1024], F32, kind="ExternalInput").ap()
    lnb = nc.dram_tensor("L0_ffn_ln_b", [2, 1024], F32, kind="ExternalInput").ap()
    w1 = nc.dram_tensor("L0_ffn_w1", [2, 1024, 5460], F32, kind="ExternalInput").ap()
    b1 = nc.dram_tensor("L0_ffn_b1", [2, 5460], F32, kind="ExternalInput").ap()
    w2 = nc.dram_tensor("L0_ffn_w2", [2, 2730, 1024], F32, kind="ExternalInput").ap()
    b2 = nc.dram_tensor("L0_ffn_b2", [2, 1024], F32, kind="ExternalInput").ap()
    out = nc.dram_tensor("out", [S, D], F32, kind="ExternalOutput").ap()
    if os.environ.get("KDBG") or os.environ.get("KDBGB"):
        dbg_q = nc.dram_tensor("dbg_q", [128, 4, 1024], BF16, kind="ExternalOutput").ap()
        dbg_k = nc.dram_tensor("dbg_k", [128, 4, 1536], BF16, kind="ExternalOutput").ap()
        dbg_v = nc.dram_tensor("dbg_v", [128, 12, 528], BF16, kind="ExternalOutput").ap()
        dbg_o = nc.dram_tensor("dbg_o", [128, 8, 512], BF16, kind="ExternalOutput").ap()
        dbg_t = nc.dram_tensor("dbg_t", [4, S, DH], F32, kind="ExternalOutput").ap()
    xscr = nc.dram_tensor("xscr", [CT * 128, S], F32, kind="Internal").ap()
    tabscr = nc.dram_tensor("tabscr", [2, 4, S, DH], F32, kind="Internal").ap()

    es = contextlib.ExitStack()

    def sb(name, shape, dt=F32):
        return es.enter_context(nc.sbuf_tensor(name, shape, dt))

    ST = sb("ST", [128, CT * S])
    idf = sb("idf", [128, 128], F32)
    idb = sb("idb", [128, 128], BF16)
    CONST = sb("CONST", [128, 512], F32)
    ROWS = sb("ROWS", [1, 704], F32)
    lngt = sb("lngt", [128, 16], F32)
    lnbt = sb("lnbt", [128, 16], F32)
    b1at = sb("b1at", [128, 2 * 22], F32)
    b1gt = sb("b1gt", [128, 2 * 22], F32)
    b2t = sb("b2t", [128, 16], F32)
    MLOC = sb("MLOC", [128, 256], F32)
    MBIG = sb("MBIG", [128, 512], F32)
    STASH = sb("STASH", [128, CT * 4], F32)
    BIAS = sb("BIAS", [128, 64], F32)

    ARENA_BYTES = 68608
    AR = sb("AR", [128, ARENA_BYTES // 4], F32)
    TOKST = AR[:, 0:4 * D]

    class Arena:
        def __init__(self, ap):
            self.ap = ap
            self.off = 0

        def take(self, n_elems_f32, align=16):
            o = (self.off + align - 1) // align * align
            self.off = o + n_elems_f32
            assert self.off <= ARENA_BYTES // 4, f"arena overflow {self.off * 4}B"
            return o

        def f32(self, n):
            o = self.take(n)
            return self.ap[:, o:o + n]

        def bf16(self, n):
            o = self.take((n + 1) // 2)
            return self.ap[:, o:o + (n + 1) // 2].bitcast(BF16)

        def reset(self):
            self.off = 0

    PB = [es.enter_context(nc.psum_tensor(f"pb{i}", [128, 512], F32)) for i in range(8)]

    Sd = Sched(nc)
    t, v, s, g, sy = Sd.tensor, Sd.vector, Sd.scalar, Sd.gpsimd, Sd.sync

    _cc = {}

    def ccol(val):
        if val not in _cc:
            i = len(_cc)
            assert i < 16
            ap = CONST[:, i:i + 1]
            h = g.memset(ap, float(val), tag="ccol")
            _cc[val] = (ap, h)
        return _cc[val]

    inits = []

    h_i1 = g.iota(idf[:, :], pattern=[[1, 128]], base=0, channel_multiplier=-1,
                  allow_small_or_imprecise_dtypes=True)
    h_i2 = g.memset(CONST[:, 0:1], 1.0)
    h_i3 = v.tensor_scalar(out=idf[:, :], in0=idf[:, :], scalar1=0.0, scalar2=None,
                           op0=ALU.is_equal, wait=[h_i1])
    h_i4 = v.tensor_copy(idb[:, :], idf[:, :], wait=[h_i3])
    inits += [h_i4]
    piota = CONST[:, 50:51]
    h_pio = g.iota(piota, pattern=[[1, 1]], base=0, channel_multiplier=1,
                   allow_small_or_imprecise_dtypes=True, tag="piota")

    h_m1 = g.iota(MLOC[:, 0:128], pattern=[[1, 128]], base=0, channel_multiplier=-1,
                  allow_small_or_imprecise_dtypes=True)
    h_m2 = v.tensor_scalar(out=MLOC[:, 0:128], in0=MLOC[:, 0:128], scalar1=0.0, scalar2=float(BIGNEG),
                           op0=ALU.is_lt, op1=ALU.mult, wait=[h_m1])
    h_m3 = g.iota(MLOC[:, 128:256], pattern=[[1, 128]], base=0, channel_multiplier=-1,
                  allow_small_or_imprecise_dtypes=True, wait=[h_m1])
    h_m4 = v.tensor_scalar(out=MLOC[:, 128:256], in0=MLOC[:, 128:256], scalar1=0.0, scalar2=float(BIGNEG),
                           op0=ALU.is_gt, op1=ALU.mult, wait=[h_m3])
    h_m5 = g.iota(MBIG[:, :], pattern=[[1, 512]], base=0, channel_multiplier=-1,
                  allow_small_or_imprecise_dtypes=True, wait=[h_m3])
    h_m6 = v.tensor_scalar(out=MBIG[:, :], in0=MBIG[:, :], scalar1=0.0, scalar2=float(BIGNEG),
                           op0=ALU.is_lt, op1=ALU.mult, wait=[h_m5])
    inits += [h_m2, h_m4, h_m6]

    for l in (0, 1):
        for c in range(8):
            inits.append(sy.dma(lngt[:, l * 8 + c:l * 8 + c + 1],
                                lng[l, c * 128:(c + 1) * 128].unsqueeze(1), ch="prm"))
            inits.append(sy.dma(lnbt[:, l * 8 + c:l * 8 + c + 1],
                                lnb[l, c * 128:(c + 1) * 128].unsqueeze(1), ch="prm"))
            inits.append(sy.dma(b2t[:, l * 8 + c:l * 8 + c + 1],
                                b2[l, c * 128:(c + 1) * 128].unsqueeze(1), ch="prm"))
        for j in range(22):
            cw = 128 if j < 21 else 42
            inits.append(sy.dma(b1at[0:cw, l * 22 + j:l * 22 + j + 1],
                                b1[l, 128 * j:128 * j + cw].unsqueeze(1), ch="prm"))
            inits.append(sy.dma(b1gt[0:cw, l * 22 + j:l * 22 + j + 1],
                                b1[l, 2730 + 128 * j:2730 + 128 * j + cw].unsqueeze(1), ch="prm"))

    def ST3(c0, c1, t0, t1):
        return ST[:, :].rearrange("p (c t) -> p c t", c=CT)[:, c0:c1, t0:t1]

    def emit_ln_load_transpose():
        slot = [dict(readers=[], evac=[None, None]) for _ in range(4)]
        bank_guard = [dict(evac=[None, None]), dict(evac=[None, None])]
        lastv = []
        for i in range(TT):
            sl = i % 4
            st = slot[sl]
            xt = TOKST[:, sl * D:(sl + 1) * D]
            hld = sy.dma(xt, hs[i * 128:(i + 1) * 128, :], wait=st["readers"], ch=f"in{sl}")
            cb = 64 + sl * 8
            mu = CONST[:, cb:cb + 1]
            ssq = CONST[:, cb + 1:cb + 2]
            var = CONST[:, cb + 2:cb + 3]
            rst = CONST[:, cb + 3:cb + 4]
            nmr = CONST[:, cb + 4:cb + 5]
            sqt = AR[:, 4 * D + (sl % 2) * 1024: 4 * D + (sl % 2 + 1) * 1024]
            pr = st.get("stats", [])
            h1 = v.reduce_sum(out=mu, in_=xt, axis=AX.X, wait=[hld] + pr)
            h2 = s.activation(out=sqt, in_=xt, func=AF.Square, wait=[hld] + pr)
            h3 = v.reduce_sum(out=ssq, in_=sqt, axis=AX.X, wait=[h2])
            h4 = v.tensor_scalar(out=mu, in0=mu, scalar1=1.0 / D, scalar2=None,
                                 op0=ALU.mult, wait=[h1])
            h5 = v.tensor_scalar(out=var, in0=mu, scalar1=mu, scalar2=None,
                                 op0=ALU.mult, wait=[h4])
            h6 = v.tensor_scalar(out=var, in0=ssq, scalar1=1.0 / D, scalar2=var,
                                 op0=ALU.mult, op1=ALU.subtract, wait=[h3, h5])
            epsc, heps = ccol(1e-5)
            h7 = s.activation(out=rst, in_=var, func=AF.Ln, bias=epsc, wait=[h6, heps])
            h8 = s.activation(out=rst, in_=rst, func=AF.Exp, scale=-0.5, wait=[h7])
            h9 = v.tensor_scalar(out=nmr, in0=mu, scalar1=rst, scalar2=-1.0,
                                 op0=ALU.mult, op1=ALU.mult, wait=[h8])
            h10 = s.activation(out=xt, in_=xt, func=AF.Identity, scale=rst, bias=nmr,
                               wait=[h9, h3])
            st["stats"] = [h9, h10]
            pbank = (PB[(sl % 2) * 2], PB[(sl % 2) * 2 + 1])
            bg = bank_guard[sl % 2]
            evs = []
            for c in range(CT):
                pb = pbank[c // 4]
                po = (c % 4) * 128
                w = [h10]
                if bg["evac"][c // 4] is not None:
                    w.append(bg["evac"][c // 4])
                evs.append(t.transpose(pb[:, po:po + 128], xt[:, c * 128:(c + 1) * 128],
                                       idf[:, :], wait=w, tag="trT"))
            hv = []
            for half in range(2):
                hv.append(v.tensor_copy(
                    ST3(half * 4, (half + 1) * 4, i * 128, (i + 1) * 128),
                    pbank[half][:, :].rearrange("p (c t) -> p c t", c=4),
                    wait=evs[half * 4:half * 4 + 4], tag="evT"))
                bg["evac"][half] = hv[half]
            st["readers"] = [h10] + evs
            lastv = hv
        return lastv

    def emit_store_stream(extra_wait=(), combine=None):
        slot = [dict(readers=[], evac=[None, None]), dict(readers=[], evac=[None, None])]
        for i in range(TT):
            sl = i % 2
            st = slot[sl]
            xt = TOKST[:, sl * D:(sl + 1) * D]
            pbank = (PB[4 + sl * 2], PB[5 + sl * 2])
            evs = []
            for c in range(CT):
                pb = pbank[c // 4]
                po = (c % 4) * 128
                w = list(extra_wait) if i < 2 else []
                if st["evac"][c // 4] is not None:
                    w.append(st["evac"][c // 4])
                evs.append(t.transpose(pb[:, po:po + 128],
                                       ST[:, c * S + i * 128: c * S + (i + 1) * 128],
                                       idf[:, :], wait=w, tag="trS"))
            hst = []
            for half in range(2):
                hst.append(v.tensor_copy(
                    xt.rearrange("p (c q) -> p c q", c=CT)[:, half * 4:(half + 1) * 4, :],
                    pbank[half][:, :].rearrange("p (c t) -> p c t", c=4),
                    wait=evs[half * 4:half * 4 + 4] + st["readers"], tag="evS"))
                st["evac"][half] = hst[half]
            if combine is not None:
                hst = combine(i, sl, xt, hst)
            hd = sy.dma(out[i * 128:(i + 1) * 128, :], xt, wait=hst, ch=f"out{sl}")
            st["readers"] = [hd]
        return [ChanTotal("sync", "out0"), ChanTotal("sync", "out1")]

    onesrow = CONST[0:1, 128:256]
    h_onesrow = g.memset(onesrow, 1.0, tag="onesrow")
    PRM = ChanTotal("sync", "prm")

    w1ascr = nc.dram_tensor("w1ascr", [2, 22, 8, 128, 128], BF16, kind="Internal").ap()
    w1gscr = nc.dram_tensor("w1gscr", [2, 22, 8, 128, 128], BF16, kind="Internal").ap()
    w2scr = nc.dram_tensor("w2scr", [2, GD, 1024], BF16, kind="Internal").ap()

    def emit_prestage_mlp(l):
        hs_ = []
        for (scr, c0) in ((w1ascr, 0), (w1gscr, GD)):
            for j in range(22):
                cw = 128 if j < 21 else 42
                srcj = w1[l, :, c0 + 128 * j:c0 + 128 * j + cw].rearrange(
                    "(kt p) c -> kt p c", p=128)
                hs_.append(g.dma(scr[l, j, :, :, 0:cw], srcj, ch=f"pm{l}", tag="presw1"))
        hs_.append(g.dma(w2scr[l], w2[l], ch=f"pm{l}", tag="presw2"))
        return hs_

    def emit_mlp(l, phase_wait=(), dbg=False):
        TB = 512
        NB = S // TB
        NCK = 22
        a_h2lo = AR[:, 0:2048].rearrange("p (c t) -> p c t", c=4)
        a_y2 = AR[:, 2048:4096].bitcast(BF16).rearrange("p (c t) -> p c t", c=8)
        a_hg = AR[:, 4096:9728].bitcast(BF16).rearrange("p (c t) -> p c t", c=22)
        a_sq = [AR[:, 9728 + i * 512: 9728 + (i + 1) * 512] for i in range(2)]
        a_gg = [AR[:, 10752 + i * 512: 10752 + (i + 1) * 512] for i in range(2)]
        a_w1 = [AR[:, 11776 + i * 1024: 11776 + (i + 1) * 1024].bitcast(BF16)
                .rearrange("p (kt c) -> p kt c", kt=8) for i in range(2)]
        a_w2 = [AR[:, 13824 + i * 512: 13824 + (i + 1) * 512].bitcast(BF16) for i in range(2)]
        r_mu = AR[0:1, 14848:15360]
        r_var = AR[0:1, 15360:15872]
        r_rst = AR[0:1, 15872:16384]

        ones_col, h_oc = ccol(1.0)
        eps_col, h_eps = ccol(1e-5)
        h_or = h_onesrow
        PRES = ChanTotal("gpsimd", f"pm{l}")
        PRESG = PRES

        h_stash = v.tensor_copy(
            STASH[:, 0:28].rearrange("p (c b) -> p c b", c=4),
            ST3(4, 8, 0, S)[:, :, 511:S - 1:512],
            wait=list(phase_wait), tag="stashB")

        hw1 = [None, None]
        hw2 = [None, None]
        ph2_res = [None] * 8
        ph1_free = [None] * 4
        last_apply_readers = []
        dbgtok = []
        h_h2lo_prev = []
        for B in range(NB):
            t0 = B * TB
            wz = [h_stash] + h_h2lo_prev
            if B == 0:
                h1 = v.memset(a_h2lo[:, :, 0:1], 0.0, wait=wz, tag="h2lo0")
                h2 = v.tensor_copy(a_h2lo[:, :, 1:TB], ST3(4, 8, 0, TB - 1), wait=wz, tag="h2lo")
                h_h2lo = [h1, h2]
            else:
                h2 = v.tensor_copy(a_h2lo[:, :, 1:TB], ST3(4, 8, t0, t0 + TB - 1), wait=wz, tag="h2lo")
                h1 = v.tensor_copy(a_h2lo[:, :, 0:1],
                                   STASH[:, 0:28].rearrange("p (c b) -> p c b", c=4)[:, :, B - 1:B],
                                   wait=wz, tag="h2lob")
                h_h2lo = [h1, h2]

            def h2tile(c):
                if c < 4:
                    return ST[:, c * S + t0: c * S + t0 + TB]
                return a_h2lo[:, c - 4, :]

            hsum = []
            hsq = []
            sq_rot = [None, None]
            for c in range(CT):
                w = h_h2lo if c >= 4 else list(phase_wait)
                hsum.append(t.matmul(PB[6][0:1, 0:TB], ones_col, h2tile(c),
                                     start=(c == 0), stop=(c == CT - 1),
                                     wait=w + [h_oc] + ([ph2_res[6]] if c == 0 and ph2_res[6] else []),
                                     tag="lnsum"))
                wsq = w + ([sq_rot[c % 2]] if sq_rot[c % 2] else [])
                hq = s.activation(out=a_sq[c % 2], in_=h2tile(c), func=AF.Square, wait=wsq, tag="lnsq")
                hm = t.matmul(PB[7][0:1, 0:TB], ones_col, a_sq[c % 2],
                              start=(c == 0), stop=(c == CT - 1),
                              wait=[hq, h_oc] + ([ph2_res[7]] if c == 0 and ph2_res[7] else []),
                              tag="lnssq")
                sq_rot[c % 2] = hm
                hsq.append(hm)
            hmu = v.tensor_scalar(out=r_mu, in0=PB[6][0:1, 0:TB], scalar1=1.0 / D, scalar2=None,
                                  op0=ALU.mult, wait=[hsum[-1]], tag="mu")
            hv1 = v.tensor_tensor(out=r_var, in0=r_mu, in1=r_mu, op=ALU.mult, wait=[hmu], tag="mu2")
            hv2 = v.scalar_tensor_tensor(out=r_var, in0=PB[7][0:1, 0:TB], scalar=1.0 / D,
                                         in1=r_var, op0=ALU.mult, op1=ALU.subtract,
                                         wait=[hsq[-1], hv1], tag="var")
            hr1 = s.activation(out=r_rst, in_=r_var, func=AF.Ln, bias=eps_col[0:1, :],
                               wait=[hv2, h_eps], tag="lnr")
            hr2 = s.activation(out=r_rst, in_=r_rst, func=AF.Exp, scale=-0.5, wait=[hr1], tag="rstd")
            hbc1 = t.matmul(PB[6][:, 0:TB], onesrow, r_mu, start=True, stop=True,
                            wait=[hmu, hr2, h_or, hv2], tag="bcmu")
            hbc2 = t.matmul(PB[7][:, 0:TB], onesrow, r_rst, start=True, stop=True,
                            wait=[hr2, h_or], tag="bcr")
            happly = []
            ap_rot = [None, None]
            for c in range(CT):
                w0 = (h_h2lo if c >= 4 else []) + [hbc1] + dbgtok + ([ap_rot[c % 2]] if ap_rot[c % 2] else [])
                ht1 = v.tensor_tensor(out=a_sq[c % 2], in0=h2tile(c), in1=PB[6][:, 0:TB],
                                      op=ALU.subtract, wait=w0, tag="apsub")
                ht2 = v.tensor_tensor(out=a_sq[c % 2], in0=a_sq[c % 2], in1=PB[7][:, 0:TB],
                                      op=ALU.mult, wait=[ht1, hbc2], tag="apmul")
                ht3 = s.activation(out=a_y2[:, c, :], in_=a_sq[c % 2], func=AF.Identity,
                                   scale=lngt[:, l * 8 + c:l * 8 + c + 1],
                                   bias=lnbt[:, l * 8 + c:l * 8 + c + 1],
                                   wait=[ht2, PRM], tag="apaff")
                ap_rot[c % 2] = ht3
                happly.append(ht3)

            if dbg and B == 0:
                sy.dma(dbg_o, a_y2, wait=happly, ch="dbg")
                dbgtok.append(ChanTotal("sync", "dbg"))
            hstt_all = []
            gg_rot = [None, None]
            for j in range(NCK):
                cw = 128 if j < 21 else 42
                sl = j % 2
                wd = [hw1[sl]] if hw1[sl] else list(phase_wait)
                hda = sy.dma(a_w1[sl][:, :, 0:cw], w1ascr[l, j, :, :, 0:cw].rearrange("kt p c -> p kt c"),
                             wait=wd + [PRES], ch=f"w1{sl}", tag="w1a")
                hdg = sy.dma(a_w1[sl][:, :, 128:128 + cw], w1gscr[l, j, :, :, 0:cw].rearrange("kt p c -> p kt c"),
                             wait=wd + [PRESG], ch=f"w1g{sl}", tag="w1g")
                hw1[sl] = None
                pa, pg = PB[0 + 2 * (j % 2)], PB[1 + 2 * (j % 2)]
                hmma = []
                hmmg = []
                for kt in range(CT):
                    wm = [hda] + happly + ([ph1_free[0 + 2 * (j % 2)]] if kt == 0 and ph1_free[0 + 2 * (j % 2)] else [])
                    hmma.append(t.matmul(pa[0:cw, 0:TB], a_w1[sl][:, kt, 0:cw], a_y2[:, kt, :],
                                         start=(kt == 0), stop=(kt == CT - 1), wait=wm, tag="w1amm"))
                    wg = [hdg] + happly + ([ph1_free[1 + 2 * (j % 2)]] if kt == 0 and ph1_free[1 + 2 * (j % 2)] else [])
                    hmmg.append(t.matmul(pg[0:cw, 0:TB], a_w1[sl][:, kt, 128:128 + cw], a_y2[:, kt, :],
                                         start=(kt == 0), stop=(kt == CT - 1), wait=wg, tag="w1gmm"))
                wgg = [hmmg[-1]] + ([gg_rot[sl]] if gg_rot[sl] else [])
                hgel = s.activation(out=a_gg[sl].unsqueeze(0)[0][0:cw, :], in_=pg[0:cw, 0:TB],
                                    func=AF.Gelu, bias=b1gt[0:cw, l * 22 + j:l * 22 + j + 1],
                                    wait=wgg, tag="gelu")
                hstt = v.scalar_tensor_tensor(
                    out=a_hg[0:cw, j, :],
                    in0=pa[0:cw, 0:TB],
                    scalar=b1at[0:cw, l * 22 + j:l * 22 + j + 1],
                    in1=a_gg[sl].unsqueeze(0)[0][0:cw, :],
                    op0=ALU.add, op1=ALU.mult, wait=[hgel, hmma[-1], PRM], tag="hg")
                gg_rot[sl] = hstt
                ph1_free[0 + 2 * (j % 2)] = hstt
                ph1_free[1 + 2 * (j % 2)] = hstt
                hw1[sl] = hmmg[-1]
                hstt_all.append(hstt)

            hmm2 = [[None] * 8 for _ in range(NCK)]
            for kt in range(NCK):
                kw = 128 if kt < 21 else 42
                sl = kt % 2
                wd = [hw2[sl]] if hw2[sl] else list(phase_wait)
                hd2 = s.dma(a_w2[sl][0:kw, :], w2scr[l, kt * 128:kt * 128 + kw],
                            wait=wd + [PRESG], ch=f"w2{sl}", tag="w2")
                for m in range(8):
                    wm = [hd2, hstt_all[kt]]
                    if kt == 0:
                        wm += [ph1_free[m]] if m < 4 and ph1_free[m] else []
                        wm += [ph2_res[m]] if ph2_res[m] else []
                        if m in (6, 7):
                            wm += [hbc1, hbc2, happly[-1]]
                    hmm2[kt][m] = t.matmul(PB[m][:, 0:TB],
                                           a_w2[sl][0:kw, m * 128:(m + 1) * 128],
                                           a_hg[0:kw, kt, :],
                                           start=(kt == 0), stop=(kt == NCK - 1), wait=wm, tag="w2mm")
                hw2[sl] = hmm2[kt][7]
            hres = []
            for m in range(8):
                hres.append(v.scalar_tensor_tensor(
                    out=ST[:, m * S + t0: m * S + t0 + TB],
                    in0=PB[m][:, 0:TB], scalar=b2t[:, l * 8 + m:l * 8 + m + 1],
                    in1=ST[:, m * S + t0: m * S + t0 + TB],
                    op0=ALU.add, op1=ALU.add,
                    wait=[hmm2[NCK - 1][m], h_stash], tag="zres"))
                ph2_res[m] = hres[m]
            h_h2lo_prev = hres
        return [x for x in ph2_res if x]

    def emit_tables():
        A0 = 8192
        b_i2p = AR[:, A0:A0 + 32]
        b_lnb = AR[:, A0 + 32:A0 + 64]
        b_qs = AR[:, A0 + 64:A0 + 128]
        b_ks = AR[:, A0 + 128:A0 + 192]
        r_half = AR[0:1, A0 + 192:A0 + 224]
        r_tmp = AR[0:1, A0 + 224:A0 + 288]
        stg = [AR[:, A0 + 512 + i * 1024:A0 + 512 + (i + 1) * 1024] for i in range(2)]
        scr = [AR[:, A0 + 2560 + i * 256:A0 + 2560 + (i + 1) * 256] for i in range(6)]
        c_mh = ccol(-0.5)
        c_p2 = ccol(np.pi / 2)
        c_l2p = ccol(float(np.log(1.0 / (2 * np.pi))))
        c_b = ccol(25.6 / 89.6)
        c_m8 = ccol(-8.0)
        hh = g.iota(r_half, pattern=[[2, 32]], base=0, channel_multiplier=0,
                    allow_small_or_imprecise_dtypes=True, tag="iot_half")
        h1 = s.activation(out=r_tmp[0:1, 0:32], in_=r_half, func=AF.Exp,
                          scale=float(-np.log(10000.0) / 64.0), bias=c_l2p[0][0:1, :],
                          wait=[hh, c_l2p[1]], tag="invf")
        hb1 = t.matmul(PB[6][:, 0:32], onesrow, r_tmp[0:1, 0:32], start=True, stop=True,
                       wait=[h1, h_onesrow], tag="bci2p")
        hb1v = v.tensor_copy(b_i2p, PB[6][:, 0:32], wait=[hb1], tag="evb")
        h2 = s.activation(out=r_tmp[0:1, 32:64], in_=r_half, func=AF.Identity,
                          scale=float(1.0 / 89.6), bias=c_b[0][0:1, :], wait=[hh, c_b[1]], tag="base")
        h3 = s.activation(out=r_tmp[0:1, 32:64], in_=r_tmp[0:1, 32:64], func=AF.Ln,
                          wait=[h2], tag="lnbase")
        hb2 = t.matmul(PB[7][:, 0:32], onesrow, r_tmp[0:1, 32:64], start=True, stop=True,
                       wait=[h3, h_onesrow, hb1], tag="bclnb")
        hb2v = v.tensor_copy(b_lnb, PB[7][:, 0:32], wait=[hb2], tag="evb2")
        htabs = []
        h_rtmp_free = [hb1, hb2]
        hprev = [None, None]
        for l in (0, 1):
            hq = sy.dma(r_tmp[0:1, 0:64], qs_in[l].unsqueeze(0), wait=h_rtmp_free,
                        ch="rows", tag="qsrow")
            hq2 = v.tensor_scalar(out=r_tmp[0:1, 0:64], in0=r_tmp[0:1, 0:64], scalar1=8.0,
                                  scalar2=None, op0=ALU.mult, wait=[hq, hb1v], tag="qs8")
            hqb = t.matmul(PB[6][:, 0:64], onesrow, r_tmp[0:1, 0:64], start=True, stop=True,
                           wait=[hq2, h_onesrow], tag="bcqs")
            hqv = v.tensor_copy(b_qs, PB[6][:, 0:64], wait=[hqb], tag="evqs")
            hk = sy.dma(r_tmp[0:1, 0:64], ks_in[l].unsqueeze(0), wait=[hqb], ch="rows", tag="ksrow")
            hkb = t.matmul(PB[7][:, 0:64], onesrow, r_tmp[0:1, 0:64], start=True, stop=True,
                           wait=[hk, h_onesrow, hb2v], tag="bcks")
            hkv = v.tensor_copy(b_ks, PB[7][:, 0:64], wait=[hkb], tag="evks")
            h_rtmp_free = [hkb]
            for i in range(TT):
                sl = i % 2
                tcol = CONST[:, 32 + sl * 8:33 + sl * 8]
                pcol = CONST[:, 33 + sl * 8:34 + sl * 8]
                u_, w_, aw_ = scr[sl * 3:sl * 3 + 3]
                u, w, aw = u_[:, 0:32], w_[:, 0:32], aw_[:, 0:32]
                iu = u_[:, 32:64].bitcast(I32)
                sc_, isc_, sin_, cos_ = (w_[:, 32:64], w_[:, 64:96],
                                         aw_[:, 32:64], aw_[:, 64:96])
                st4 = stg[sl]
                wv = [hprev[sl]] if hprev[sl] else [hqv, hkv]
                hio = v.tensor_scalar(out=tcol, in0=piota, scalar1=float(128 * i),
                                      scalar2=None, op0=ALU.add, wait=[h_pio] + wv, tag="tcol")
                h4 = v.tensor_scalar(out=u, in0=b_i2p, scalar1=tcol, scalar2=None,
                                     op0=ALU.mult, wait=[hio, hb1v] + wv, tag="u")
                h5 = v.tensor_copy(iu, u, wait=[h4], tag="cvt")
                h6 = v.tensor_copy(w, iu, wait=[h5], tag="back")
                h7 = v.tensor_tensor(out=w, in0=u, in1=w, op=ALU.subtract, wait=[h6], tag="frac")
                h8a = v.tensor_scalar(out=isc_, in0=w, scalar1=0.5, scalar2=None,
                                      op0=ALU.is_gt, wait=[h7], tag="wgt")
                h8 = v.tensor_tensor(out=w, in0=w, in1=isc_, op=ALU.subtract,
                                     wait=[h8a], tag="wrap")
                h9 = s.activation(out=aw, in_=w, func=AF.Abs, wait=[h8], tag="aw")
                h10 = s.activation(out=sin_, in_=w, func=AF.Sin, scale=float(2 * np.pi),
                                   wait=[h9], tag="sin")
                h11 = s.activation(out=cos_, in_=aw, func=AF.Sin, scale=float(-2 * np.pi),
                                   bias=c_p2[0], wait=[h10, c_p2[1]], tag="cos")
                h12 = s.activation(out=pcol, in_=tcol, func=AF.Identity, scale=1.0 / 256.0,
                                   bias=c_m8[0], wait=[hio, c_m8[1]], tag="pcol")
                h13 = v.tensor_scalar(out=u, in0=b_lnb, scalar1=pcol, scalar2=None,
                                      op0=ALU.mult, wait=[h12, h5, hb2v], tag="plnb")
                h14 = s.activation(out=sc_, in_=u, func=AF.Exp, wait=[h13, h11], tag="sc")
                h15 = s.activation(out=isc_, in_=u, func=AF.Exp, scale=-1.0, wait=[h14, h8], tag="isc")
                ha1 = v.tensor_tensor(out=u, in0=cos_, in1=sc_, op=ALU.mult,
                                      wait=[h15, h11], tag="cs")
                ha2 = v.tensor_tensor(out=w, in0=sin_, in1=sc_, op=ALU.mult,
                                      wait=[h14, h10, h8], tag="ss")
                hm = []
                for (dsto, src2, tb, rot) in ((0, u, b_qs, 0), (64, w, b_qs, 32),
                                              (128, None, b_ks, 0), (192, None, b_ks, 32)):
                    pass
                hm.append(v.tensor_tensor(out=st4[:, 0:32], in0=u, in1=b_qs[:, 0:32], op=ALU.mult, wait=[ha1, hqv] + wv, tag="cq0"))
                hm.append(v.tensor_tensor(out=st4[:, 32:64], in0=u, in1=b_qs[:, 32:64], op=ALU.mult, wait=[ha1, hqv], tag="cq1"))
                hm.append(v.scalar_tensor_tensor(out=st4[:, 64:96], in0=w, scalar=-1.0,
                                                 in1=b_qs[:, 32:64], op0=ALU.mult, op1=ALU.mult,
                                                 wait=[ha2, hqv], tag="sq0"))
                hm.append(v.tensor_tensor(out=st4[:, 96:128], in0=w, in1=b_qs[:, 0:32], op=ALU.mult, wait=[ha2, hqv], tag="sq1"))
                ha3 = v.tensor_tensor(out=u, in0=cos_, in1=isc_, op=ALU.mult,
                                      wait=[h15, hm[0], hm[1]], tag="ci")
                ha4 = v.tensor_tensor(out=w, in0=sin_, in1=isc_, op=ALU.mult,
                                      wait=[h15, hm[2], hm[3]], tag="si")
                hm.append(v.tensor_tensor(out=st4[:, 128:160], in0=u, in1=b_ks[:, 0:32], op=ALU.mult, wait=[ha3, hkv], tag="ck0"))
                hm.append(v.tensor_tensor(out=st4[:, 160:192], in0=u, in1=b_ks[:, 32:64], op=ALU.mult, wait=[ha3, hkv], tag="ck1"))
                hm.append(v.scalar_tensor_tensor(out=st4[:, 192:224], in0=w, scalar=-1.0,
                                                 in1=b_ks[:, 32:64], op0=ALU.mult, op1=ALU.mult,
                                                 wait=[ha4, hkv], tag="sk0"))
                hm.append(v.tensor_tensor(out=st4[:, 224:256], in0=w, in1=b_ks[:, 0:32], op=ALU.mult, wait=[ha4, hkv], tag="sk1"))
                hd = g.dma(tabscr[l, :, i * 128:(i + 1) * 128, :].rearrange("f p d -> p f d"),
                            st4[:, 0:256].rearrange("p (f d) -> p f d", f=4),
                            wait=hm, ch=f"tabst{sl}", tag="tabst")
                hprev[sl] = hd
                htabs.append(hd)
        return [ChanTotal("gpsimd", "tabst0"), ChanTotal("gpsimd", "tabst1")]

    wqkvscr = nc.dram_tensor("wqkvscr", [2, 3, 8, 128, 512], BF16, kind="Internal").ap()
    woscr = nc.dram_tensor("woscr", [2, 4, 128, 1024], BF16, kind="Internal").ap()

    def emit_prestage_attn(l):
        hs_ = []
        for part in range(3):
            srcp = wqkv[l, :, part * 512:(part + 1) * 512].rearrange(
                "(kt p) c -> kt p c", p=128)
            hs_.append(g.dma(wqkvscr[l, part], srcp, ch=f"pa{l}", tag="preqkv"))
        hs_.append(g.dma(woscr[l], wo_ffn[l].rearrange("(kt p) c -> kt p c", p=128),
                         ch=f"pa{l}", tag="prewo"))
        return hs_

    def emit_ffn_attn(l, phase_wait=(), dbg=False):
        a_tabs = [AR[:, i * 256:(i + 1) * 256] for i in range(2)]
        a_rsc = [AR[:, 512 + i * 512:512 + (i + 1) * 512] for i in range(2)] + \
                [AR[:, 15968:16480], AR[:, 16640:17152]]
        a_stg = [AR[:, 1536 + i * 256:1536 + (i + 1) * 256].bitcast(BF16) for i in range(2)]
        a_zbc = AR[:, 2048:6144].bitcast(BF16).rearrange("p (c t) -> p c t", c=8)
        a_zbt = AR[:, 6144:6656].bitcast(BF16).rearrange("p (c t) -> p c t", c=8)
        a_slab = [AR[:, 6656 + i * 256:6656 + (i + 1) * 256].bitcast(BF16) for i in range(3)]
        a_PT = AR[:, 0:3200].bitcast(BF16).rearrange("p (s h q) -> p s h q", s=5, h=2)
        a_on = AR[:, 3200:5248].bitcast(BF16).rearrange("p (q c) -> p q c", q=8)
        a_oT = AR[:, 0:2048].bitcast(BF16).rearrange("p (c t) -> p c t", c=4)
        a_woh = [AR[:, 2048:3072].bitcast(BF16).rearrange("p (c t) -> p c t", c=2),
                 AR[:, 5248:6272].bitcast(BF16).rearrange("p (c t) -> p c t", c=2)]
        a_qT = AR[:, 7680:9728].bitcast(BF16).rearrange("p (c t) -> p c t", c=4)
        a_kT = AR[:, 9728:12800].bitcast(BF16).rearrange("p (c t) -> p c t", c=4)
        a_vE = AR[:, 12800:15968].bitcast(BF16).rearrange("p (r t) -> p r t", r=12)
        vE4 = a_vE.rearrange("p r (h w) -> p r h w", h=8)

        PRES = ChanTotal("gpsimd", f"pa{l}")
        TABS = [ChanTotal("gpsimd", "tabst0"), ChanTotal("gpsimd", "tabst1")]
        pw = list(phase_wait)

        hq = sy.dma(ROWS[0:1, 0:64], qs_in[l].unsqueeze(0), wait=pw, ch="rows", tag="qsr")
        hk = sy.dma(ROWS[0:1, 64:128], ks_in[l].unsqueeze(0), wait=[hq], ch="rows", tag="ksr")
        hm1 = v.tensor_reduce(out=ROWS[0:1, 128:129], in_=ROWS[0:1, 0:64], axis=AX.X,
                              op=ALU.max, apply_absolute_value=True, wait=[hk] + pw, tag="mq")
        hm2_ = v.tensor_reduce(out=ROWS[0:1, 129:130], in_=ROWS[0:1, 64:128], axis=AX.X,
                               op=ALU.max, apply_absolute_value=True, wait=[hm1], tag="mk")
        hm3 = v.tensor_tensor(out=ROWS[0:1, 128:129], in0=ROWS[0:1, 128:129],
                              in1=ROWS[0:1, 129:130], op=ALU.mult, wait=[hm2_], tag="mmul")
        hm4_ = v.tensor_scalar(out=ROWS[0:1, 128:129], in0=ROWS[0:1, 128:129], scalar1=-8.0,
                               scalar2=None, op0=ALU.mult, wait=[hm3], tag="m8")
        hm5 = t.matmul(PB[6][:, 0:1], onesrow, ROWS[0:1, 128:129], start=True, stop=True,
                       wait=[hm4_, h_onesrow] + pw, tag="bcm")
        hbias = v.tensor_copy(BIAS[:, l:l + 1], PB[6][:, 0:1], wait=[hm5], tag="evbias")
        h_stash = v.tensor_copy(
            STASH[:, 0:12].rearrange("p (c b) -> p c b", c=4),
            ST3(4, 8, 0, S)[:, :, 1023:S - 1:1024], wait=pw, tag="stashA")
        if dbg:
            hz1 = g.memset(a_vE[:, :, :], 0.0, wait=pw, tag="dbg0")
            hz2 = g.memset(a_kT[:, :, :], 0.0, wait=pw, tag="dbg0")
            pw = pw + [hz1, hz2]
        h_vones = g.memset(vE4[:, :, :, 64:65], 1.0, wait=pw, tag="vones")

        tTE, tV, tS, tG = [hm5] + pw, [h_stash, hbias] + pw, list(pw), [h_vones] + pw
        hslab = [None, None, None]
        htab_rot = [None, None]
        for ch in range(4):
            cs, ce = ch * 8, ch * 8 + 8
            CS, CE = cs * 128, ce * 128
            wz = tV + tTE
            h_zb1 = v.tensor_copy(a_zbc[:, 0:4, :], ST3(0, 4, CS, CE), wait=wz, tag="zbu")
            h_zb2 = v.tensor_copy(a_zbc[:, 4:8, :], ST3(4, 8, CS, CE), wait=wz, tag="zbl")
            if ch == 0:
                hb0 = v.memset(a_zbt[:, 4:8, 0:1], 0.0, wait=wz, tag="zbt0")
            else:
                hb0 = v.tensor_copy(a_zbt[:, 4:8, 0:1],
                                    STASH[:, 0:12].rearrange("p (c b) -> p c b", c=4)[:, :, ch - 1:ch],
                                    wait=wz, tag="zbt0")
            hb1 = v.tensor_copy(a_zbt[:, 4:8, 1:128], ST3(4, 8, CS, CS + 127), wait=wz, tag="zbt1")
            hb2 = v.tensor_copy(a_zbt[:, 0:4, :], ST3(0, 4, CS, CS + 128), wait=wz, tag="zbtu")
            h_zb = [h_zb1, h_zb2, hb0, hb1, hb2]

            def statq(T, c):
                if c < 4:
                    return a_zbc[:, c, T * 128:(T + 1) * 128]
                if T == 0:
                    return a_zbt[:, c, 0:128]
                return a_zbc[:, c, T * 128 - 1:(T + 1) * 128 - 1]

            rotdone = [[None] * 8, [None] * 8, [None] * 8]
            tab_reader = [[], []]
            htr_ev = [[None] * 8, [None] * 8]
            hv_ev = [None] * 8
            tr_bank_guard = [None] * 8
            hrsc_guard = [tTE[0]] * 4
            hstg_guard = [tTE[0], tTE[0]]
            lastG = [tTE[0], tTE[0]]
            for part in range(3):
                hmm_last = [None] * 8
                for kt in range(CT):
                    sl = (part * CT + kt) % 3
                    wd = ([hslab[sl]] if hslab[sl] else pw) + [PRES]
                    hdsl = s.dma(a_slab[sl], wqkvscr[l, part, kt], wait=wd,
                                 ch=f"qkv{sl}", tag="qkvslab")
                    for T in range(8):
                        wm = [hdsl] + h_zb
                        if kt == 0:
                            if part == 0:
                                wm += tTE
                            else:
                                if rotdone[part - 1][T]:
                                    wm.append(rotdone[part - 1][T])
                                if tr_bank_guard[T]:
                                    wm.append(tr_bank_guard[T])
                        hmm_last[T] = t.matmul(
                            PB[T][:, 0:512], statq(T, kt), a_slab[sl],
                            start=(kt == 0), stop=(kt == CT - 1), wait=wm, tag=f"qkv{part}")
                    hslab[sl] = hmm_last[7]
                chst = {}

                def chainA(T, part=part, hmm_last=hmm_last):
                    gt = cs + T
                    sl = T % 2
                    wtb = tab_reader[sl] if tab_reader[sl] else (TABS + tTE)
                    htab = sy.dma(a_tabs[sl].rearrange("p (f d) -> p f d", f=4),
                                  tabscr[l, :, gt * 128:(gt + 1) * 128, :].rearrange("f p d -> p f d"),
                                  wait=wtb, ch=f"tab{sl}", tag="tabtile")
                    htab_rot[sl] = htab
                    tab_reader[sl] = []
                    ps3 = PB[T][:, 0:512].rearrange("p (h d) -> p h d", h=8)
                    nrm = CONST[:, 256 + T * 8:264 + T * 8]
                    rp = 2 * (T % 2)
                    wn = [hmm_last[T]] + ([hrsc_guard[rp]] if hrsc_guard[rp] else [])
                    hsq = s.activation(out=a_rsc[rp], in_=PB[T][:, 0:512], func=AF.Square,
                                       wait=wn, tag="nsq")
                    hrd = v.tensor_reduce(out=nrm, in_=a_rsc[rp].rearrange("p (h d) -> p h d", h=8),
                                          axis=AX.X, op=ALU.add, wait=[hsq], tag="nred")
                    hc1 = v.tensor_scalar(out=nrm, in0=nrm, scalar1=1e-18, scalar2=None,
                                          op0=ALU.max, wait=[hrd], tag="nclip")
                    hl1 = s.activation(out=nrm, in_=nrm, func=AF.Ln, wait=[hc1], tag="nln")
                    hl2 = s.activation(out=nrm, in_=nrm, func=AF.Exp, scale=-0.5, wait=[hl1], tag="nexp")
                    s0 = a_rsc[rp].rearrange("p (h d) -> p h d", h=8)
                    hr0 = v.tensor_tensor(out=s0, in0=ps3,
                                          in1=nrm.unsqueeze(2).broadcast_to([128, 8, 64]),
                                          op=ALU.mult, wait=[hl2], tag="rqn")
                    rotdone[part][T] = hr0
                    chst[T] = (sl, rp, htab, hr0)

                def chainB(T, part=part):
                    gt = cs + T
                    sl, rp, htab, hr0 = chst[T]
                    tb = a_tabs[sl].rearrange("p (f d) -> p f d", f=4)
                    ct_ = tb[:, 2 * part + 0:2 * part + 1, :].broadcast_to([128, 8, 64])
                    st_ = tb[:, 2 * part + 1:2 * part + 2, :].broadcast_to([128, 8, 64])
                    s0 = a_rsc[rp].rearrange("p (h d) -> p h d", h=8)
                    s1 = a_rsc[rp + 1].rearrange("p (h d) -> p h d", h=8)
                    stgv = a_stg[sl].rearrange("p (h d) -> p h d", h=8)
                    g1 = [hrsc_guard[rp + 1]] if hrsc_guard[rp + 1] else []
                    hp0 = v.tensor_tensor(out=stgv, in0=s0, in1=ct_, op=ALU.mult,
                                          wait=[hr0, htab] +
                                          ([hstg_guard[sl]] if hstg_guard[sl] else []), tag="rc")
                    hp1a = g.tensor_tensor(out=s1[:, :, 0:32], in0=s0[:, :, 32:64],
                                           in1=st_[:, :, 0:32], op=ALU.mult,
                                           wait=[hr0, htab] + g1, tag="rsa")
                    hp1b = g.tensor_tensor(out=s1[:, :, 32:64], in0=s0[:, :, 0:32],
                                           in1=st_[:, :, 32:64], op=ALU.mult,
                                           wait=[hr0, htab] + g1, tag="rsb")
                    tab_reader[sl] = tab_reader[sl] + [hp0, hp1a, hp1b]
                    lastG[:] = [hp1a, hp1b]
                    hadd = v.tensor_tensor(out=stgv, in0=stgv, in1=s1, op=ALU.add,
                                           wait=[hp0, hp1a, hp1b], tag="radd")
                    hrsc_guard[rp] = hadd
                    hrsc_guard[rp + 1] = hadd
                    pvt = PB[T].bitcast(BF16)
                    htr = []
                    for c4 in range(4):
                        wtr = [hadd]
                        if c4 == 0 and tr_bank_guard[T]:
                            wtr.append(tr_bank_guard[T])
                        htr.append(t.transpose(pvt[:, c4 * 128:(c4 + 1) * 128],
                                               a_stg[sl][:, c4 * 128:(c4 + 1) * 128],
                                               idb[:, :], wait=wtr, tag="trqk"))
                    if part == 0:
                        dview = a_qT[:, :, T * 128:(T + 1) * 128]
                    else:
                        dview = a_kT[:, :, (gt % 12) * 128:((gt % 12) + 1) * 128]
                    hev = v.tensor_copy(dview, pvt[:, 0:512].rearrange("p (c t) -> p c t", c=4),
                                        wait=htr + tTE, tag="evqk")
                    htr_ev[part][T] = hev
                    hstg_guard[sl] = hev
                    tr_bank_guard[T] = hev

                for T in range(8):
                    if part == 2:
                        gt = cs + T
                        slot = gt % 12
                        wv = [hmm_last[T], h_vones] + tTE
                        hv_ev[T] = v.tensor_copy(
                            vE4[:, slot, :, 0:64],
                            PB[T][:, 0:512].rearrange("p (h d) -> p h d", h=8),
                            wait=wv, tag="vev")
                        rotdone[2][T] = hv_ev[T]
                        continue
                    chainA(T)
                    if T >= 1:
                        chainB(T - 1)
                if part != 2:
                    chainB(7)
            hA1 = htr_ev[0] + htr_ev[1] + hv_ev
            dbg_tok = []
            if dbg and ch == 0:
                sy.dma(dbg_q, a_qT, wait=hA1, ch="dbg")
                sy.dma(dbg_k, a_kT, wait=hA1, ch="dbg")
                sy.dma(dbg_v, a_vE, wait=hA1 + [h_vones], ch="dbg")
                sy.dma(dbg_t, tabscr[l], ch="dbg", wait=TABS)
                dbg_tok = [ChanTotal("sync", "dbg")]
            scb = {(0, 0): (PB[2], PB[4], 0), (0, 1): (PB[3], PB[6], 0),
                   (1, 0): (PB[5], PB[4], 128), (1, 1): (PB[7], PB[6], 128)}
            jlist = list(range(max(cs - 4, 0), ce))
            h_on_done = []
            hlastPV = None
            a1done = [hA1[-1], hv_ev[-1]]
            allexp = []
            pv_guard = [None, None]
            for pr in range(4):
                exp_guard = {}
                sc_guard = {}
                exp_h = {}
                spans = {}
                for ji, J in enumerate(jlist):
                    slot_ = ji % 2
                    span_lo = max(128 * J, CS)
                    span_hi = min(128 * J + 640, CE)
                    N = span_hi - span_lo
                    N1 = min(N, 512)
                    N2 = N - N1
                    jslot = J % 12
                    spans[J] = (span_lo, N1, N2)
                    hsc = {}
                    for hh in range(2):
                        b512, b128, o128 = scb[(slot_, hh)]
                        kst = a_kT[hh * 64:(hh + 1) * 64, pr, jslot * 128:(jslot + 1) * 128]
                        qrh = a_qT[hh * 64:(hh + 1) * 64, pr, span_lo - CS:span_hi - CS]
                        wsc = a1done + ([exp_guard[(slot_, hh)]] if (slot_, hh) in exp_guard else [])
                        hsc[(hh, 0)] = t.matmul(b512[:, 0:N1], kst, qrh[:, 0:N1],
                                                start=True, stop=True, wait=wsc, tag="sc512")
                        if N2 > 0:
                            hsc[(hh, 1)] = t.matmul(b128[:, o128:o128 + N2], kst, qrh[:, N1:N],
                                                    start=True, stop=True, wait=wsc, tag="sc128")
                    hmk = {}
                    allsc = list(hsc.values())
                    for hh in range(2):
                        b512, b128, o128 = scb[(slot_, hh)]
                        ww = list(allsc)
                        hml = []
                        if J >= cs:
                            hml.append(v.tensor_tensor(out=b512[:, 0:128], in0=b512[:, 0:128],
                                                       in1=MLOC[:, 0:128], op=ALU.add,
                                                       wait=ww + [h_m2], tag="mkf"))
                        if J + 4 <= ce - 1:
                            off = 128 * (J + 4) - span_lo
                            tgt = b512[:, off:off + 128] if off < 512 else b128[:, o128:o128 + 128]
                            hml.append(v.tensor_tensor(out=tgt, in0=tgt, in1=MLOC[:, 128:256],
                                                       op=ALU.add, wait=ww + [h_m4], tag="mkl"))
                        hmk[hh] = hml if hml else ww
                    hex_ = {}
                    ptslot = J % 5
                    for hh in range(2):
                        b512, b128, o128 = scb[(slot_, hh)]
                        wex = hmk[hh] + [hsc[(hh, 0)], hbias]
                        if (hh, 1) in hsc:
                            wex.append(hsc[(hh, 1)])
                        if (slot_, hh) in sc_guard:
                            wex.append(sc_guard[(slot_, hh)])
                        if hlastPV is not None:
                            wex.append(hlastPV)
                        hex_[(hh, 0)] = s.activation(out=a_PT[:, ptslot, hh, 0:N1], in_=b512[:, 0:N1],
                                                     func=AF.Exp, bias=BIAS[:, l:l + 1],
                                                     wait=wex, tag="lexp")
                        if N2 > 0:
                            hex_[(hh, 1)] = s.activation(out=a_PT[:, ptslot, hh, N1:N],
                                                         in_=b128[:, o128:o128 + N2],
                                                         func=AF.Exp, bias=BIAS[:, l:l + 1],
                                                         wait=wex, tag="lexp2")
                        exp_guard[(slot_, hh)] = hex_[(hh, 1)] if (hh, 1) in hex_ else hex_[(hh, 0)]
                        allexp.append(exp_guard[(slot_, hh)])
                    exp_h[J] = hex_
                    if J < cs:
                        sc_guard[(slot_, 0)] = exp_guard[(slot_, 0)]
                        sc_guard[(slot_, 1)] = exp_guard[(slot_, 1)]
                        continue
                    Q = J
                    for hh in range(2):
                        head = 2 * pr + hh
                        js = [jj for jj in range(max(Q - 4, 0), Q + 1)]
                        hpv = None
                        for idx2, J2 in enumerate(js):
                            splo2, N1b, N2b = spans[J2]
                            qoff = 128 * Q - splo2
                            wpv = []
                            he = exp_h[J2]
                            if qoff + 128 <= N1b:
                                wpv.append(he[(hh, 0)])
                            else:
                                wpv.append(he[(hh, 1)])
                                if qoff < N1b:
                                    wpv.append(he[(hh, 0)])
                            first = (idx2 == 0)
                            if first and pv_guard[hh]:
                                wpv.append(pv_guard[hh])
                            if first:
                                wpv += a1done
                            hpv = t.matmul(PB[hh][:, 0:65], a_PT[:, J2 % 5, hh, qoff:qoff + 128],
                                           vE4[:, J2 % 12, head, 0:65],
                                           start=first, stop=(idx2 == len(js) - 1),
                                           wait=wpv, tag="pv")
                        hlastPV = hpv
                        sc_guard[(slot_, hh)] = hpv
                        rc = CONST[:, 48 + hh:49 + hh]
                        hrc = v.reciprocal(out=rc, in_=PB[hh][:, 64:65], wait=[hpv], tag="rcp")
                        hno = v.tensor_scalar(out=a_on[:, Q - cs, head * 64:(head + 1) * 64],
                                              in0=PB[hh][:, 0:64], scalar1=rc, scalar2=None,
                                              op0=ALU.mult, wait=[hrc], tag="onorm")
                        pv_guard[hh] = hno
                        h_on_done.append(hno)
            if dbg and ch == 0:
                sy.dma(dbg_o, a_on, wait=[h_on_done[-1]], ch="dbg")
            hwod1 = sy.dma(a_woh[0], woscr[l, 0:2].rearrange("c p t -> p c t"),
                           wait=[h for h in hslab if h] + [PRES, hlastPV, h_on_done[-1], allexp[-1]] + tTE,
                           ch="wo", tag="wodma")
            hwod = sy.dma(a_woh[1], woscr[l, 2:4].rearrange("c p t -> p c t"),
                          wait=[hwod1], ch="wo", tag="wodma2")
            hev3 = []
            allexp = list(exp_guard.values())
            for Q in range(8):
                pvt = PB[2].bitcast(BF16)
                htrq = []
                for c4 in range(4):
                    wtr = [h_on_done[-1]] + ([hev3[-1]] if hev3 else allexp + [hlastPV])
                    htrq.append(t.transpose(pvt[:, c4 * 128:(c4 + 1) * 128],
                                            a_on[:, Q, c4 * 128:(c4 + 1) * 128], idb[:, :],
                                            wait=wtr, tag="tro"))
                hev3.append(v.tensor_copy(a_oT[:, :, Q * 128:(Q + 1) * 128],
                                          pvt[:, 0:512].rearrange("p (c t) -> p c t", c=4),
                                          wait=htrq, tag="evo"))
            hres = [None] * 8
            hmmw = None
            for half in range(2):
                for m in range(8):
                    for kc in range(4):
                        wmw = [hwod, hev3[-1]]
                        if kc == 0 and half == 0:
                            wmw += [h_on_done[-1], hlastPV, allexp[-1]]
                        if kc == 0 and half == 1:
                            wmw.append(hres[m])
                        hmmw = t.matmul(PB[m][:, 0:512], a_woh[kc // 2][:, kc % 2, m * 128:(m + 1) * 128],
                                        a_oT[:, kc, half * 512:half * 512 + 512],
                                        start=(kc == 0), stop=(kc == 3), wait=wmw, tag="womm")
                    hres[m] = v.tensor_tensor(
                        out=ST[:, m * S + CS + half * 512: m * S + CS + half * 512 + 512],
                        in0=ST[:, m * S + CS + half * 512: m * S + CS + half * 512 + 512],
                        in1=PB[m][:, 0:512], op=ALU.add, wait=[hmmw, h_stash], tag="zatt")
            tTE = [hmmw]
            tV = list(hres) + dbg_tok
            tS = [allexp[-1]]
            tG = list(lastG)
        return tV + tTE



    wqTscr = nc.dram_tensor("wqTscr", [8, 128, 1024], BF16, kind="Internal").ap()
    wkTscr = nc.dram_tensor("wkTscr", [8, 128, 256], BF16, kind="Internal").ap()
    wvTscr = nc.dram_tensor("wvTscr", [8, 128, 256], BF16, kind="Internal").ap()
    woTscr = nc.dram_tensor("woTscr", [8, 128, 1024], BF16, kind="Internal").ap()

    def emit_prestage_big(phase_wait=()):
        stg_in = [AR[:, 6144 + i * 128:6144 + (i + 1) * 128].bitcast(BF16) for i in range(4)]
        stg_out = [AR[:, 6656 + i * 512:6656 + (i + 1) * 512].bitcast(BF16) for i in range(2)]
        pwl = list(phase_wait)
        dst64_q = lambda b: 128 * (b % 8) + 64 * (b // 8)
        dst64_k = lambda b: {0: 0, 2: 64, 1: 128, 3: 192}[b]
        dst64_n = lambda b: 64 * b
        jobs = [(wq, wqTscr, 8, dst64_q), (wk, wkTscr, 2, dst64_k),
                (wv, wvTscr, 2, dst64_n), (wo_big, woTscr, 8, dst64_n)]
        hld_g = [None] * 4
        htr_g = [None, None]
        hst_prev = []
        it = 0
        tc = 0
        for (wsrc, wdst, nrt, dmap) in jobs:
            for ktg in range(4):
                hev_all = [[], []]
                for r in range(nrt):
                    sl = it % 4
                    wl = ([hld_g[sl]] if hld_g[sl] else pwl)
                    hld = g.dma(stg_in[sl], wsrc[r * 128:(r + 1) * 128, ktg * 256:(ktg + 1) * 256],
                                wait=wl, ch=f"bw{sl}", tag="bw")
                    it += 1
                    last_tr = None
                    for kk in range(2):
                        bk = tc % 2
                        tc += 1
                        htr = t.transpose(PB[4 + bk].bitcast(BF16)[:, 0:128],
                                          stg_in[sl][:, kk * 128:(kk + 1) * 128], idb[:, :],
                                          wait=[hld] + ([htr_g[bk]] if htr_g[bk] else []), tag="bwtr")
                        last_tr = htr
                        for hf in range(2):
                            b = 2 * r + hf
                            hev_all[kk].append(v.tensor_copy(
                                stg_out[kk][:, dmap(b):dmap(b) + 64],
                                PB[4 + bk].bitcast(BF16)[:, hf * 64:(hf + 1) * 64],
                                wait=[htr] + hst_prev, tag="bwev"))
                        htr_g[bk] = hev_all[kk][-1]
                    hld_g[sl] = last_tr
                hst_prev = [sy.dma(wdst[ktg * 2 + kk], stg_out[kk][:, 0:nrt * 128],
                                   wait=hev_all[kk], ch="bigs", tag="bws") for kk in range(2)][-1:]
        return [ChanTotal("sync", "bigs")]

    def emit_bigattn(phase_wait=(), dbg=False):
        kTA = AR[:, 0:2048].bitcast(BF16)
        kTB = AR[:, 2048:4096].bitcast(BF16)
        a_vE = AR[:, 4096:8256].bitcast(BF16).rearrange("p (r w) -> p r w", r=32)
        a_y1 = AR[:, 8256:9280].bitcast(BF16).rearrange("p (c t) -> p c t", c=8)
        a_qT = AR[:, 9280:11328].bitcast(BF16).rearrange("p (m t) -> p m t", m=8)
        a_PT = AR[:, 11328:13376].bitcast(BF16).rearrange("p (s h q) -> p s h q", s=4, h=2)
        a_oT = AR[:, 13376:15424].bitcast(BF16).rearrange("p (c t) -> p c t", c=8)
        a_ws = [AR[:, 15424 + i * 512:15424 + (i + 1) * 512].bitcast(BF16) for i in range(2)]
        a_wkv = AR[:, 9280:11328].bitcast(BF16)
        a_sq = AR[:, 11328:11840]
        a_rows = AR[0:1, 11840:12352]
        a_nrows = AR[0:1, 8256:8768]
        a_obn = AR[:, 8768:9024].bitcast(BF16)
        a_ob = AR[:, 12352:12864].bitcast(BF16)
        BIGS = ChanTotal("sync", "bigs")
        pw = list(phase_wait)
        eps_col, h_eps = ccol(1e-5)
        ones_col, h_oc = ccol(1.0)
        l8_col, h_l8 = ccol(float(np.log(0.125)))
        h_it = g.iota(CONST[:, 100:101], pattern=[[1, 1]], base=0, channel_multiplier=1,
                      allow_small_or_imprecise_dtypes=True, wait=pw, tag="iotp")
        h_t1 = v.tensor_scalar(out=CONST[:, 101:102], in0=CONST[:, 100:101], scalar1=64.0,
                               scalar2=None, op0=ALU.is_lt, wait=[h_it], tag="otop")
        h_t2 = v.tensor_scalar(out=CONST[:, 102:103], in0=CONST[:, 100:101], scalar1=64.0,
                               scalar2=None, op0=ALU.is_ge, wait=[h_it], tag="obot")
        ones_top, ones_bot = CONST[:, 101:102], CONST[:, 102:103]
        h_vone = g.memset(a_vE.rearrange("p r (k w) -> p r k w", k=4)[:, :, :, 64:65], 1.0,
                          wait=pw, tag="bvones")

        asq_g = [[]]

        def chan_ln_chunk(t0, TBc, dst, wl, pg):
            b0, b1 = pg
            hsum = []
            hsq = []
            for c in range(CT):
                w = list(wl) if c == 0 else []
                hsum.append(t.matmul(b0[0:1, 0:TBc], ones_col, ST[:, c * S + t0:c * S + t0 + TBc],
                                     start=(c == 0), stop=(c == CT - 1), wait=w + [h_oc], tag="bln1"))
                hq = s.activation(out=a_sq[:, 0:TBc], in_=ST[:, c * S + t0:c * S + t0 + TBc],
                                  func=AF.Square,
                                  wait=w + ([hsq[-1]] if hsq else asq_g[0]), tag="bln2")
                hsq.append(t.matmul(b1[0:1, 0:TBc], ones_col, a_sq[:, 0:TBc],
                                    start=(c == 0), stop=(c == CT - 1), wait=[hq, h_oc] + w, tag="bln3"))
            r_mu = a_rows[0:1, 0:256]
            r_var = a_rows[0:1, 256:512]
            hmu = v.tensor_scalar(out=r_mu[0:1, 0:TBc], in0=b0[0:1, 0:TBc], scalar1=1.0 / D,
                                  scalar2=None, op0=ALU.mult, wait=[hsum[-1]], tag="bln4")
            hv1 = v.tensor_tensor(out=r_var[0:1, 0:TBc], in0=r_mu[0:1, 0:TBc], in1=r_mu[0:1, 0:TBc],
                                  op=ALU.mult, wait=[hmu], tag="bln5")
            hv2 = v.scalar_tensor_tensor(out=r_var[0:1, 0:TBc], in0=b1[0:1, 0:TBc], scalar=1.0 / D,
                                         in1=r_var[0:1, 0:TBc], op0=ALU.mult, op1=ALU.subtract,
                                         wait=[hsq[-1], hv1], tag="bln6")
            hr1 = s.activation(out=r_var[0:1, 0:TBc], in_=r_var[0:1, 0:TBc], func=AF.Ln,
                               bias=eps_col[0:1, :], wait=[hv2, h_eps], tag="bln7")
            hr2 = s.activation(out=r_var[0:1, 0:TBc], in_=r_var[0:1, 0:TBc], func=AF.Exp,
                               scale=-0.5, wait=[hr1], tag="bln8")
            hb0 = t.matmul(b0[:, 0:TBc], onesrow, r_mu[0:1, 0:TBc], start=True, stop=True,
                           wait=[hmu, hr2, h_onesrow], tag="bln9")
            hb1 = t.matmul(b1[:, 0:TBc], onesrow, r_var[0:1, 0:TBc], start=True, stop=True,
                           wait=[hr2, h_onesrow], tag="bln10")
            happ = []
            for c in range(CT):
                ha1 = v.tensor_tensor(out=a_sq[:, 0:TBc], in0=ST[:, c * S + t0:c * S + t0 + TBc],
                                      in1=b0[:, 0:TBc], op=ALU.subtract,
                                      wait=[hb0, hsq[-1]] + ([happ[-1]] if happ else []), tag="bln11")
                ha2 = v.tensor_tensor(out=dst[:, c, :], in0=a_sq[:, 0:TBc], in1=b1[:, 0:TBc],
                                      op=ALU.mult, wait=[ha1, hb1, hsq[-1]], tag="bln12")
                happ.append(ha2)
            asq_g[0] = [happ[-1]]
            return happ

        hkv = []
        hwk = sy.dma(a_wkv[:, 0:2048].rearrange("p (kt c) -> p kt c", kt=8),
                     wkTscr.rearrange("kt p c -> p kt c"), wait=pw + [BIGS], ch="bkv", tag="wkld")
        hwv = sy.dma(a_wkv[:, 2048:4096].rearrange("p (kt c) -> p kt c", kt=8),
                     wvTscr.rearrange("kt p c -> p kt c"), wait=[hwk], ch="bkv", tag="wvld")
        wkv3 = a_wkv.rearrange("p (x kt c) -> p x kt c", x=2, kt=8)
        hkmax = [None, None]
        hln_g = []
        hev_prev = []
        for cc in range(16):
            t0 = cc * 256
            hln = chan_ln_chunk(t0, 256, a_y1, pw + hln_g, (PB[6], PB[7]))
            hln_g = [hln[-1]]
            hk2 = {}
            for (xi, dstk) in ((0, kTA), (1, kTB)):
                hmmk = None
                for kt in range(CT):
                    wm = [hwk, hln[-1]]
                    if kt == 0:
                        wm += hev_prev
                    hmmk = t.matmul(PB[xi][:, 0:256], wkv3[:, 0, kt, 0 + 128 * xi:128 + 128 * xi],
                                    a_y1[:, kt, :], start=(kt == 0), stop=(kt == CT - 1),
                                    wait=wm, tag="kproj")
                hk2[xi] = hmmk
            hv2_ = {}
            for tt in range(2):
                hmmv = None
                for kt in range(CT):
                    wm = [hwv, hln[-1]]
                    if kt == 0:
                        wm += hev_prev
                    hmmv = t.matmul(PB[2 + tt][:, 0:256], a_y1[:, kt, tt * 128:(tt + 1) * 128],
                                    wkv3[:, 1, kt, :], start=(kt == 0), stop=(kt == CT - 1),
                                    wait=wm, tag="vproj")
                hv2_[tt] = hmmv
            hev = []
            hkn_prev = []
            for (xi, dstk) in ((0, kTA), (1, kTB)):
                hq = s.activation(out=a_sq[:, 0:256], in_=PB[xi][:, 0:256], func=AF.Square,
                                  wait=[hk2[xi], hln[-1]] + hkn_prev + asq_g[0], tag="ksq")
                wkn = [hq] + ([hkmax[0]] if hkmax[0] else [])
                hn1 = t.matmul(PB[4][0:1, xi * 256:xi * 256 + 256], ones_top, a_sq[:, 0:256],
                               start=True, stop=True, wait=wkn + [h_t1], tag="kn1")
                hn2 = t.matmul(PB[5][0:1, xi * 256:xi * 256 + 256], ones_bot, a_sq[:, 0:256],
                               start=True, stop=True, wait=wkn + [h_t2], tag="kn2")
                for (pbn, hn, kv) in ((PB[4], hn1, 0 + xi), (PB[5], hn2, 2 + xi)):
                    hr = v.tensor_reduce(out=a_rows[0:1, 320 + kv:321 + kv],
                                         in_=pbn[0:1, xi * 256:xi * 256 + 256],
                                         axis=AX.X, op=ALU.max, wait=[hn] + ([hkmax[0]] if hkmax[0] else []),
                                         tag="kred")
                    if cc == 0:
                        hm_ = v.tensor_copy(ROWS[0:1, 296 + kv:297 + kv], a_rows[0:1, 320 + kv:321 + kv],
                                            wait=[hr], tag="kmax0")
                    else:
                        hm_ = v.tensor_tensor(out=ROWS[0:1, 296 + kv:297 + kv],
                                              in0=ROWS[0:1, 296 + kv:297 + kv],
                                              in1=a_rows[0:1, 320 + kv:321 + kv], op=ALU.max,
                                              wait=[hr, hkmax[1]] if hkmax[1] else [hr], tag="kmax")
                    hkmax = [hm_, hm_]
                hkn_prev = [hn1, hn2]
                asq_g[0] = [hn1, hn2]
                hev.append(v.tensor_copy(dstk[:, t0:t0 + 256], PB[xi][:, 0:256],
                                         wait=[hk2[xi], hq], tag="kev"))
            for tt in range(2):
                gt = cc * 2 + tt
                hev.append(v.tensor_copy(
                    a_vE.rearrange("p r (k w) -> p r k w", k=4)[:, gt, :, 0:64],
                    PB[2 + tt][:, 0:256].rearrange("p (k d) -> p k d", k=4),
                    wait=[hv2_[tt], h_vone], tag="vev"))
            hev_prev = hev
            hln_g = [hln[-1]] + hev
        if dbg:
            sy.dma(dbg_k[:, 0:2, :].rearrange("p a t -> p (a t)")[:, 0:3072],
                   AR[:, 0:3072].bitcast(BF16)[:, 0:3072], wait=hev_prev + [hkmax[0]], ch="dbg")
            sy.dma(dbg_v[:, :, 0:260], a_vE[:, 0:12, :], wait=hev_prev, ch="dbg")
        tTE = hev_prev[:1]
        tV = hev_prev
        hwslab = [None, None]
        prev_wo = []
        prev_norm = []
        BIGW = [ChanTotal("sync", "bigs")]
        hexp_all = []
        dbg_gtok = []
        for gq in range(8):
            G0 = gq * 512
            hlnq_g = [tV[-1], hkmax[0]] + prev_wo + prev_norm
            hqmm = [None] * 8
            hqev = []
            for hf in range(2):
                hln = chan_ln_chunk(G0 + hf * 256, 256, a_y1, hlnq_g, (PB[6], PB[7]))
                hlnq_g = [hln[-1]]
                for kt in range(CT):
                    sl = kt % 2
                    wd = [hwslab[sl]] if hwslab[sl] else (pw + BIGW)
                    hdw = s.dma(a_ws[sl], wqTscr[kt], wait=wd, ch=f"bws{sl}", tag="wqld")
                    for M in range(8):
                        wm = [hdw, hln[-1]]
                        if kt == 0:
                            wm += [hqev[M]] if hf == 1 else prev_wo
                            if hf == 0:
                                wm += prev_norm
                        hqmm[M] = t.matmul(PB[M][:, hf * 256:hf * 256 + 256],
                                           a_ws[sl][:, M * 128:(M + 1) * 128], a_y1[:, kt, :],
                                           start=(kt == 0), stop=(kt == CT - 1), wait=wm,
                                           tag="qproj")
                    hwslab[sl] = hqmm[7]
                for M in range(8):
                    hqev.append(v.tensor_copy(a_qT[:, M, hf * 256:hf * 256 + 256],
                                              PB[M][:, hf * 256:hf * 256 + 256],
                                              wait=[hqmm[M]] + prev_norm, tag="qev"))
                hlnq_g = hlnq_g + hqev[-2:]
            hnorm = []
            asql = [a_sq, AR[:, 11840:12352]]
            qbg = [list(asq_g[0]), []]
            hqs = {}

            def qnA(M):
                b = M % 2
                hqs[M] = s.activation(out=asql[b][:, 0:512], in_=a_qT[:, M, :], func=AF.Square,
                                      wait=[hqev[M], hqev[8 + M]] + qbg[b], tag="qsq")

            def qnB(M):
                b = M % 2
                hn1 = t.matmul(PB[0][0:1, 0:512], ones_top, asql[b][:, 0:512], start=True,
                               stop=True, wait=[hqs[M], h_t1, hqev[0], hqev[8]] + ([hnorm[-2]] if hnorm else []), tag="qn1")
                hn2 = t.matmul(PB[1][0:1, 0:512], ones_bot, asql[b][:, 0:512], start=True,
                               stop=True, wait=[hqs[M], h_t2, hqev[1], hqev[9]] + ([hnorm[-1]] if hnorm else []), tag="qn2")
                hr1 = v.tensor_reduce(out=ROWS[0:1, 300 + M:301 + M], in_=PB[0][0:1, 0:512],
                                      axis=AX.X, op=ALU.max, wait=[hn1], tag="qred1")
                hr2 = v.tensor_reduce(out=ROWS[0:1, 308 + M:309 + M], in_=PB[1][0:1, 0:512],
                                      axis=AX.X, op=ALU.max, wait=[hn2], tag="qred2")
                hnorm.extend([hr1, hr2])
                qbg[b] = [hn1, hn2]

            for M in range(8):
                qnA(M)
                if M >= 1:
                    qnB(M - 1)
            qnB(7)
            asq_g[0] = qbg[0]
            hb1_ = v.tensor_copy(ROWS[0:1, 316:332].rearrange("p (a b) -> p a b", a=4),
                                 ROWS[0:1, 296:300].unsqueeze(2).broadcast_to([1, 4, 4]),
                                 wait=[hkmax[0]] + hnorm[-2:], tag="kkrow")
            hb2_ = v.tensor_tensor(out=ROWS[0:1, 300:316], in0=ROWS[0:1, 300:316],
                                   in1=ROWS[0:1, 316:332], op=ALU.mult, wait=[hb1_] + hnorm, tag="qk")
            hb3_ = s.activation(out=ROWS[0:1, 300:316], in_=ROWS[0:1, 300:316], func=AF.Ln,
                                wait=[hb2_], tag="lnqk")
            hb4_ = s.activation(out=ROWS[0:1, 300:316], in_=ROWS[0:1, 300:316], func=AF.Exp,
                                scale=0.5, bias=l8_col[0:1, :], wait=[hb3_, h_l8], tag="mrow")
            hb5_ = t.matmul(PB[2][:, 0:16], onesrow, ROWS[0:1, 300:316], start=True, stop=True,
                            wait=[hb4_, h_onesrow] + prev_wo, tag="bcm16")
            hbias = v.tensor_scalar(out=BIAS[:, 8:24], in0=PB[2][:, 0:16], scalar1=-1.0,
                                    scalar2=None, op0=ALU.mult, wait=[hb5_] + hexp_all[-1:], tag="nbias")
            if dbg and gq == 0:
                sy.dma(dbg_q.rearrange("p a b -> p (a b)"), a_qT.rearrange("p m t -> p (m t)"),
                       wait=hqev + [hbias], ch="dbg")
                dbg_gtok = [ChanTotal("sync", "dbg")]
            allexp = []
            hno = []
            norm_q = []
            for M in range(8):
                heads = (M, M + 8)
                pvb = 6 if M % 2 == 0 else 0
                kts = kTA if M < 4 else kTB
                njt = 4 * gq + 4
                hpv = [None, None]
                hexp_slot = {}
                scg = {}
                for bt0 in range(0, njt, 4):
                    jts = list(range(bt0, min(bt0 + 4, njt)))
                    hexp_b = {}
                    for jt in jts:
                        r = jt - 4 * gq
                        qlo = max(0, r) * 128 if r >= 0 else 0
                        N = 512 - (qlo)
                        sb0 = PB[2 + 2 * (jt % 2)]
                        sb1 = PB[3 + 2 * (jt % 2)]
                        for hh in range(2):
                            bnk = (sb0, sb1)[hh]
                            wsc = [hqev[M], hbias] + (hno[-1:] if hno else [])
                            if (jt % 2, hh) in scg:
                                wsc.append(scg[(jt % 2, hh)])
                            kst = kts[hh * 64:(hh + 1) * 64, jt * 128:(jt + 1) * 128]
                            hs_ = t.matmul(bnk[:, 0:N], kst,
                                           a_qT[hh * 64:(hh + 1) * 64, M, qlo:512],
                                           start=True, stop=True, wait=wsc + tTE, tag="bsc")
                            if r >= 0:
                                hs_ = v.tensor_tensor(out=bnk[:, 0:128], in0=bnk[:, 0:128],
                                                      in1=MLOC[:, 0:128], op=ALU.add,
                                                      wait=[hs_, h_m2], tag="bmask")
                            wex = [hs_] + ([hexp_slot.get((jt % 4, hh))] if hexp_slot.get((jt % 4, hh)) else [])
                            if hpv[hh]:
                                wex.append(hpv[hh])
                            he = s.activation(out=a_PT[:, jt % 4, hh, qlo:512], in_=bnk[:, 0:N],
                                              func=AF.Exp, scale=0.125,
                                              bias=BIAS[:, 8 + heads[hh]:9 + heads[hh]],
                                              wait=wex, tag="bexp")
                            hexp_b[(jt, hh)] = he
                            hexp_slot[(jt % 4, hh)] = he
                            scg[(jt % 2, hh)] = he
                            allexp.append(he)
                            hexp_all.append(he)
                    for jt in jts:
                        r = jt - 4 * gq
                        qlo = max(0, r) * 128 if r >= 0 else 0
                        N = 512 - qlo
                        kv = M // 4 if False else None
                        for hh in range(2):
                            head = heads[hh]
                            kvh = head // 4
                            wpv = [hexp_b[(jt, hh)], h_vone]
                            if jt == 0:
                                wpv += [hno[-1]] if hno else prev_norm + tTE
                            hpv[hh] = t.matmul(PB[pvb + hh][0:65, qlo:512],
                                               a_vE[:, jt, kvh * 65:kvh * 65 + 65],
                                               a_PT[:, jt % 4, hh, qlo:512],
                                               start=(jt == 0), stop=(jt == njt - 1),
                                               wait=wpv, tag="bpv")
                norm_q.append((M, heads, pvb, list(hpv)))
                while len(norm_q) > (1 if M < 7 else 0):
                    Mn, headsn, pvbn, hpvn = norm_q.pop(0)
                    for hh in range(2):
                        head = headsn[hh]
                        hrc = v.reciprocal(out=a_nrows[0:1, 0:512], in_=PB[pvbn + hh][64:65, 0:512],
                                           wait=[hpvn[hh]] + ([hno[-1]] if hno else []), tag="brcp")
                        hoev = v.tensor_copy(a_obn[0:64, 0:512], PB[pvbn + hh][0:64, 0:512],
                                             wait=[hpvn[hh], hrc], tag="boev")
                        hbc = t.matmul(PB[pvbn + hh][:, 0:512], onesrow, a_nrows[0:1, 0:512],
                                       start=True, stop=True, wait=[hrc, hoev], tag="brbc")
                        hmul = v.tensor_tensor(
                            out=a_oT[64 * (head % 2):64 * (head % 2) + 64, head // 2, :],
                            in0=a_obn[0:64, 0:512], in1=PB[pvbn + hh][64:128, 0:512], op=ALU.mult,
                            wait=[hbc, hoev], tag="bnorm")
                        hno.append(hmul)
            if dbg and gq == 0:
                sy.dma(dbg_o, a_oT, wait=[hno[-1]], ch="dbg")
            hwo = [None] * 8
            hres = []
            for kt in range(CT):
                sl = kt % 2
                hdw = s.dma(a_ws[sl], woTscr[kt], wait=[hwslab[sl]], ch=f"bws{sl}", tag="wold")
                for m in range(8):
                    wm = [hdw, hno[-1]]
                    if kt == 0:
                        wm += [allexp[-1], hnorm[-1], hno[-1]]
                    hwo[m] = t.matmul(PB[m][:, 0:512], a_ws[sl][:, m * 128:(m + 1) * 128],
                                      a_oT[:, kt, :], start=(kt == 0), stop=(kt == CT - 1),
                                      wait=wm, tag="bwomm")
                hwslab[sl] = hwo[7]
            for m in range(8):
                hres.append(v.tensor_tensor(out=ST[:, m * S + G0:m * S + G0 + 512],
                                            in0=ST[:, m * S + G0:m * S + G0 + 512],
                                            in1=PB[m][:, 0:512], op=ALU.add,
                                            wait=[hwo[m]], tag="bres"))
            prev_wo = [hwo[7]]
            prev_norm = list(hres) + dbg_gtok
            tTE = [hwo[7]]
            tV = list(hres)
        return tV + tTE

    def emit_resid_combine(scale, phase_wait=()):
        slot = [dict(evac=[None, None], rd=[]), dict(evac=[None, None], rd=[])]
        hout = []
        for i in range(TT):
            sl = i % 2
            stt = slot[sl]
            xt = AR[:, sl * D:(sl + 1) * D]
            hld = sy.dma(xt, hs[i * 128:(i + 1) * 128, :],
                         wait=stt["rd"] + (list(phase_wait) if i < 2 else []), ch=f"rc{sl}", tag="rcld")
            pbank = (PB[sl * 2], PB[sl * 2 + 1])
            evs = []
            for c in range(CT):
                pb = pbank[c // 4]
                po = (c % 4) * 128
                w = [hld] + (list(phase_wait) if i < 2 else [])
                if stt["evac"][c // 4] is not None:
                    w.append(stt["evac"][c // 4])
                evs.append(t.transpose(pb[:, po:po + 128], xt[:, c * 128:(c + 1) * 128],
                                       idf[:, :], wait=w, tag="trR"))
            for half in range(2):
                hv = v.scalar_tensor_tensor(
                    out=ST3(half * 4, (half + 1) * 4, i * 128, (i + 1) * 128),
                    in0=ST3(half * 4, (half + 1) * 4, i * 128, (i + 1) * 128),
                    scalar=float(scale),
                    in1=pbank[half][:, :].rearrange("p (c t) -> p c t", c=4),
                    op0=ALU.mult, op1=ALU.add,
                    wait=evs[half * 4:half * 4 + 4] + (list(phase_wait) if i < 2 else []),
                    tag="rcomb")
                stt["evac"][half] = hv
                hout.append(hv)
            stt["rd"] = evs
        return hout[-4:]

    def emit_chan_ln(phase_wait=()):
        hout = []
        prev = {}
        eps_col, h_eps = ccol(1.0e-5)
        ones_col, h_oc = ccol(1.0)
        for i in range(8):
            t0 = i * 512
            r_mu = ROWS[0:1, 130:642] if False else AR[0:1, 2048 + 0:2048 + 512]
            r_mu = AR[0:1, 2048:2560]
            r_var = AR[0:1, 2560:3072]
            r_rst = AR[0:1, 3072:3584]
            sq = AR[:, 3584 + (i % 2) * 512: 3584 + (i % 2 + 1) * 512]
            hsum = []
            hsq = []
            sqg = prev.get(("sq", i % 2))
            for c in range(CT):
                w = list(phase_wait) if i < 2 else []
                if c == 0 and ("b6" in prev):
                    w.append(prev["b6"])
                hsum.append(t.matmul(PB[6][0:1, 0:512], ones_col, ST[:, c * S + t0:c * S + t0 + 512],
                                     start=(c == 0), stop=(c == CT - 1),
                                     wait=w + [h_oc], tag="clnsum"))
                wq = w + ([sqg] if sqg and c < 2 else [])
                hq = s.activation(out=sq, in_=ST[:, c * S + t0:c * S + t0 + 512], func=AF.Square,
                                  wait=wq + ([hsq[-1]] if hsq else []), tag="clnsq")
                w7 = [hq] + ([prev["b7"]] if c == 0 and "b7" in prev else [])
                hsq.append(t.matmul(PB[7][0:1, 0:512], ones_col, sq,
                                    start=(c == 0), stop=(c == CT - 1), wait=w7 + [h_oc], tag="clnssq"))
            hmu = v.tensor_scalar(out=r_mu, in0=PB[6][0:1, 0:512], scalar1=1.0 / D, scalar2=None,
                                  op0=ALU.mult, wait=[hsum[-1]] + ([prev["mu"]] if "mu" in prev else []),
                                  tag="cmu")
            hv1 = v.tensor_tensor(out=r_var, in0=r_mu, in1=r_mu, op=ALU.mult,
                                  wait=[hmu] + ([prev["var"]] if "var" in prev else []), tag="cmu2")
            hv2 = v.scalar_tensor_tensor(out=r_var, in0=PB[7][0:1, 0:512], scalar=1.0 / D,
                                         in1=r_var, op0=ALU.mult, op1=ALU.subtract,
                                         wait=[hsq[-1], hv1], tag="cvar")
            hr1 = s.activation(out=r_rst, in_=r_var, func=AF.Ln, bias=eps_col[0:1, :],
                               wait=[hv2, h_eps] + ([prev["rst"]] if "rst" in prev else []), tag="clnr")
            hr2 = s.activation(out=r_rst, in_=r_rst, func=AF.Exp, scale=-0.5, wait=[hr1], tag="crstd")
            hb6 = t.matmul(PB[6][:, 0:512], onesrow, r_mu, start=True, stop=True,
                           wait=[hmu, hr2, h_onesrow], tag="cbcmu")
            hb7 = t.matmul(PB[7][:, 0:512], onesrow, r_rst, start=True, stop=True,
                           wait=[hr2, h_onesrow], tag="cbcr")
            happ = []
            for c in range(CT):
                sl2 = c % 2
                scr2 = AR[:, 4608 + sl2 * 512:4608 + (sl2 + 1) * 512]
                wa = [hb6] + ([happ[c - 2][0]] if c >= 2 else [])
                ha1 = v.tensor_tensor(out=scr2, in0=ST[:, c * S + t0:c * S + t0 + 512],
                                      in1=PB[6][:, 0:512], op=ALU.subtract, wait=wa, tag="capsub")
                ha2 = v.tensor_tensor(out=ST[:, c * S + t0:c * S + t0 + 512], in0=scr2,
                                      in1=PB[7][:, 0:512], op=ALU.mult, wait=[ha1, hb7, hsq[-1], hsum[-1]],
                                      tag="capmul")
                happ.append((ha1, ha2))
            prev = {"b6": happ[-1][1], "b7": happ[-1][1], "mu": hb6, "var": hv2, "rst": hb7,
                    ("sq", i % 2): hsq[-1]}
            hout = [h[1] for h in happ]
        return hout

    def emit_spill(phase_wait=()):
        hs_ = []
        for c in range(CT):
            hs_.append(sy.dma(xscr[c * 128:(c + 1) * 128, :], ST[:, c * S:(c + 1) * S],
                              wait=list(phase_wait), ch="spill", tag="spill"))
        return [ChanTotal("sync", "spill")]

    def emit_final_combine(phase_wait=()):
        hout = []
        hprev = [[], []]
        for i in range(8):
            sl = i % 2
            xb = AR[:, sl * 4096:(sl + 1) * 4096]
            hld = sy.dma(xb.rearrange("p (c t) -> p c t", c=CT),
                         xscr[:, i * 512:(i + 1) * 512].rearrange("(c p) t -> p c t", p=128),
                         wait=list(phase_wait) + hprev[sl], ch=f"fc{sl}", tag="fcld")
            h = v.scalar_tensor_tensor(out=ST3(0, CT, i * 512, (i + 1) * 512),
                                       in0=ST3(0, CT, i * 512, (i + 1) * 512), scalar=0.5,
                                       in1=xb.rearrange("p (c t) -> p c t", c=CT),
                                       op0=ALU.mult, op1=ALU.add,
                                       wait=[hld] + list(phase_wait), tag="fcomb")
            hprev[sl] = [h]
            hout.append(h)
        return hout[-2:]

    if stage in ("ffn1", "x1", "y1", "x2", "full"):
        for l in (0, 1):
            emit_prestage_attn(l)
            emit_prestage_mlp(l)
        htb = emit_tables()
    hz0 = emit_ln_load_transpose()
    if stage == "z0":
        fin = emit_store_stream(extra_wait=hz0)
        Sd.sync.raw(lambda eng: eng.nop(), wait=fin, tag="final")
        Sd.finalize()
        es.close()
        return nc

    if stage == "devA":
        htb = emit_tables()
        hp = emit_prestage_attn(0)
        ha = emit_ffn_attn(0, phase_wait=hz0 + [ChanTotal("gpsimd", "pa0")] + htb, dbg=True)
        fin = emit_store_stream(extra_wait=ha)
        Sd.sync.raw(lambda eng: eng.nop(), wait=fin, tag="final")
        Sd.finalize()
        es.close()
        return nc

    if stage in ("ffn1", "x1", "y1", "x2", "full"):
        hpb = emit_prestage_big(phase_wait=[]) if stage in ("x2", "full") else []
        hph = hz0 + htb + hpb
        _os = os
        NLAY = int(_os.environ.get("KNL", "2"))
        for l in range(NLAY):
            hph = emit_ffn_attn(l, phase_wait=hph)
            if not _os.environ.get("KSKIPB"):
                hph = emit_mlp(l, phase_wait=hph, dbg=(l == 0 and bool(_os.environ.get("KDBGB"))))
                hph = hph + ([ChanTotal("sync", "dbg")] if l == 0 and _os.environ.get("KDBGB") else [])
        if stage == "x2":
            hph = emit_resid_combine(0.5, phase_wait=hph)
            hph = emit_bigattn(phase_wait=hph + hpb, dbg=bool(_os.environ.get("KDBG")))
            hph = hph + ([ChanTotal("sync", "dbg")] if _os.environ.get("KDBG") else [])
            fin = emit_store_stream(extra_wait=hph)
            Sd.sync.raw(lambda eng: eng.nop(), wait=fin, tag="final")
            Sd.finalize()
            es.close()
            return nc
        if stage in ("x1", "y1"):
            hph = emit_resid_combine(0.5, phase_wait=hph)
            if stage == "y1":
                hph = emit_chan_ln(phase_wait=hph)
            fin = emit_store_stream(extra_wait=hph)
            Sd.sync.raw(lambda eng: eng.nop(), wait=fin, tag="final")
            Sd.finalize()
            es.close()
            return nc
        if stage == "ffn1":
            fin = emit_store_stream(extra_wait=hph)
            Sd.sync.raw(lambda eng: eng.nop(), wait=fin, tag="final")
            Sd.finalize()
            es.close()
            return nc
        if stage == "full":
            hph = emit_resid_combine(0.5, phase_wait=hph)
            hph = emit_bigattn(phase_wait=hph + hpb)
            hsp = emit_spill(phase_wait=hph)
            hph = emit_chan_ln(phase_wait=hph + hsp)
            for l in (0, 1):
                hph = emit_ffn_attn(l, phase_wait=hph)
                hph = emit_mlp(l, phase_wait=hph)
            hph = emit_final_combine(phase_wait=hph + hsp)
            fin = emit_store_stream(extra_wait=hph)
            Sd.sync.raw(lambda eng: eng.nop(), wait=fin, tag="final")
            Sd.finalize()
            es.close()
            return nc

    if stage == "devB":
        hp = emit_prestage_mlp(0)
        hm = emit_mlp(0, phase_wait=hz0 + [ChanTotal("gpsimd", "pm0")])
        fin = emit_store_stream(extra_wait=hm)
        Sd.sync.raw(lambda eng: eng.nop(), wait=fin, tag="final")
        Sd.finalize()
        es.close()
        return nc

    raise NotImplementedError(stage)


def kernel(**inputs):
    stage = inputs.pop("_stage", "full")
    nc = build(stage)
    hsl = np.ascontiguousarray(inputs["L0_attn_hidden_states"])
    names = ["L0_attn_Wq", "L0_attn_Wk", "L0_attn_Wv", "L0_attn_Wo",
             "L0_ffn_wqkv", "L0_ffn_q_scale", "L0_ffn_k_scale", "L0_ffn_wo",
             "L0_ffn_ln_g", "L0_ffn_ln_b", "L0_ffn_w1", "L0_ffn_b1",
             "L0_ffn_w2", "L0_ffn_b2"]
    base = {n: np.ascontiguousarray(inputs[n]) for n in names}
    in_maps = [dict(base, hs=hsl[i]) for i in range(8)]
    res = run_bass_kernel_spmd(nc, in_maps, core_ids=list(range(8)))
    return np.stack([r["out"] for r in res.results], axis=0)


# version 15 (final)
# speedup vs baseline: 1.0209x; 1.0209x over previous
import contextlib
import os
import numpy as np
import concourse.bass as bass
import concourse.mybir as mybir
from concourse.bass_utils import run_bass_kernel_spmd

F32 = mybir.dt.float32
BF16 = mybir.dt.bfloat16
I32 = mybir.dt.int32
AF = mybir.ActivationFunctionType
ALU = mybir.AluOpType
AX = mybir.AxisListType

ENGINES = ("tensor", "vector", "scalar", "gpsimd", "sync")


class ChanTotal:
    __slots__ = ("eng", "ch")

    def __init__(self, eng, ch):
        self.eng, self.ch = eng, ch


class Handle:
    __slots__ = ("eng", "idx")

    def __init__(self, eng, idx):
        self.eng, self.idx = eng, idx


class _Op:
    __slots__ = ("emit", "waits", "sig", "ch", "tag")

    def __init__(self, emit, waits, ch, tag):
        self.emit = emit
        self.waits = list(waits)
        self.sig = None
        self.ch = ch
        self.tag = tag


class _EngineQ:
    def __init__(self, name):
        self.name = name
        self.ops = []

    def op(self, method, *args, wait=(), tag=None, **kwargs):
        def emit(eng):
            return getattr(eng, method)(*args, **kwargs)
        self.ops.append(_Op(emit, wait, None, tag or method))
        return Handle(self.name, len(self.ops) - 1)

    def raw(self, fn, wait=(), tag="raw"):
        self.ops.append(_Op(fn, wait, None, tag))
        return Handle(self.name, len(self.ops) - 1)

    def dma(self, out, in_, wait=(), ch="m", tag="dma", **kw):
        def emit(eng):
            return eng.dma_start(out=out, in_=in_, **kw)
        o = _Op(emit, wait, ch, tag)
        o.sig = True
        self.ops.append(o)
        return Handle(self.name, len(self.ops) - 1)

    def __getattr__(self, method):
        def f(*args, wait=(), tag=None, **kwargs):
            return self.op(method, *args, wait=wait, tag=tag, **kwargs)
        return f


class Sched:
    def __init__(self, nc):
        self.nc = nc
        self.q = {e: _EngineQ(e) for e in ENGINES}
        self.tensor, self.vector, self.scalar, self.gpsimd, self.sync = (
            self.q[e] for e in ENGINES)
        self._stack = contextlib.ExitStack()
        self._sems = {}

    def _sem(self, key):
        if key not in self._sems:
            self._sems[key] = self._stack.enter_context(
                self.nc.semaphore("s_" + "_".join(str(k) for k in key)))
        return self._sems[key]

    def finalize(self):
        for e in ENGINES:
            for o in self.q[e].ops:
                for h in o.waits:
                    if isinstance(h, Handle):
                        self.q[h.eng].ops[h.idx].sig = True
        for e in ENGINES:
            csig, dsig = 0, {}
            for o in self.q[e].ops:
                if o.sig:
                    if o.ch is not None:
                        dsig[o.ch] = dsig.get(o.ch, 0) + 16
                        o.sig = dsig[o.ch]
                    else:
                        csig += 1
                        o.sig = csig
        totals = {}
        for e in ENGINES:
            for o in self.q[e].ops:
                if o.ch is not None and isinstance(o.sig, int):
                    totals[(e, o.ch)] = max(totals.get((e, o.ch), 0), o.sig)

        def rtok(h):
            if isinstance(h, Handle):
                o = self.q[h.eng].ops[h.idx]
                if o.ch is not None:
                    return ((h.eng, "d", o.ch), o.sig)
                return ((h.eng, "c"), o.sig)
            if isinstance(h, ChanTotal):
                return ((h.eng, "d", h.ch), totals.get((h.eng, h.ch), 0))
            return h
        for e in ENGINES:
            for o in self.q[e].ops:
                o.waits = [rtok(h) for h in o.waits]
        with self.nc.Block() as block:
            for e in ENGINES:
                ops = self.q[e].ops
                if not ops:
                    continue

                def body(eng, _ops=ops, _e=e):
                    water = {}
                    for o in _ops:
                        for (sk, sv) in o.waits:
                            if water.get(sk, 0) >= sv:
                                continue
                            water[sk] = sv
                            eng.wait_ge(self._sem(sk), sv)
                        ins = o.emit(eng)
                        if o.sig is not None:
                            if o.ch is not None:
                                ins.then_inc(self._sem((_e, "d", o.ch)), 16)
                            else:
                                if ins is None:
                                    raise RuntimeError(f"op {o.tag} on {_e} returned None")
                                ins.then_inc(self._sem((_e, "c")), 1)
                getattr(block, e)(body)
        self._stack.close()


S = 4096
D = 1024
CT = 8
TT = 32
NH = 8
DH = 64
W = 512
GD = 2730
NQH, NKV = 16, 4
BIGNEG = -1e9


def build(stage="full"):
    nc = bass.Bass("TRN2", target_bir_lowering=False, debug=False)

    hs = nc.dram_tensor("hs", [S, D], F32, kind="ExternalInput").ap()
    wq = nc.dram_tensor("L0_attn_Wq", [1024, 1024], F32, kind="ExternalInput").ap()
    wk = nc.dram_tensor("L0_attn_Wk", [256, 1024], F32, kind="ExternalInput").ap()
    wv = nc.dram_tensor("L0_attn_Wv", [256, 1024], F32, kind="ExternalInput").ap()
    wo_big = nc.dram_tensor("L0_attn_Wo", [1024, 1024], F32, kind="ExternalInput").ap()
    wqkv = nc.dram_tensor("L0_ffn_wqkv", [2, 1024, 1536], F32, kind="ExternalInput").ap()
    qs_in = nc.dram_tensor("L0_ffn_q_scale", [2, 64], F32, kind="ExternalInput").ap()
    ks_in = nc.dram_tensor("L0_ffn_k_scale", [2, 64], F32, kind="ExternalInput").ap()
    wo_ffn = nc.dram_tensor("L0_ffn_wo", [2, 512, 1024], F32, kind="ExternalInput").ap()
    lng = nc.dram_tensor("L0_ffn_ln_g", [2, 1024], F32, kind="ExternalInput").ap()
    lnb = nc.dram_tensor("L0_ffn_ln_b", [2, 1024], F32, kind="ExternalInput").ap()
    w1 = nc.dram_tensor("L0_ffn_w1", [2, 1024, 5460], F32, kind="ExternalInput").ap()
    b1 = nc.dram_tensor("L0_ffn_b1", [2, 5460], F32, kind="ExternalInput").ap()
    w2 = nc.dram_tensor("L0_ffn_w2", [2, 2730, 1024], F32, kind="ExternalInput").ap()
    b2 = nc.dram_tensor("L0_ffn_b2", [2, 1024], F32, kind="ExternalInput").ap()
    out = nc.dram_tensor("out", [S, D], F32, kind="ExternalOutput").ap()
    if os.environ.get("KDBG") or os.environ.get("KDBGB"):
        dbg_q = nc.dram_tensor("dbg_q", [128, 4, 1024], BF16, kind="ExternalOutput").ap()
        dbg_k = nc.dram_tensor("dbg_k", [128, 4, 1536], BF16, kind="ExternalOutput").ap()
        dbg_v = nc.dram_tensor("dbg_v", [128, 12, 528], BF16, kind="ExternalOutput").ap()
        dbg_o = nc.dram_tensor("dbg_o", [128, 8, 512], BF16, kind="ExternalOutput").ap()
        dbg_t = nc.dram_tensor("dbg_t", [4, S, DH], F32, kind="ExternalOutput").ap()
    xscr = nc.dram_tensor("xscr", [CT * 128, S], F32, kind="Internal").ap()
    tabscr = nc.dram_tensor("tabscr", [2, 4, S, DH], F32, kind="Internal").ap()

    es = contextlib.ExitStack()

    def sb(name, shape, dt=F32):
        return es.enter_context(nc.sbuf_tensor(name, shape, dt))

    ST = sb("ST", [128, CT * S])
    idf = sb("idf", [128, 128], F32)
    idb = sb("idb", [128, 128], BF16)
    CONST = sb("CONST", [128, 512], F32)
    ROWS = sb("ROWS", [1, 704], F32)
    lngt = sb("lngt", [128, 16], F32)
    lnbt = sb("lnbt", [128, 16], F32)
    b1at = sb("b1at", [128, 2 * 22], F32)
    b1gt = sb("b1gt", [128, 2 * 22], F32)
    b2t = sb("b2t", [128, 16], F32)
    MLOC = sb("MLOC", [128, 256], F32)
    MBIG = sb("MBIG", [128, 512], F32)
    STASH = sb("STASH", [128, CT * 4], F32)
    BIAS = sb("BIAS", [128, 64], F32)

    ARENA_BYTES = 68608
    AR = sb("AR", [128, ARENA_BYTES // 4], F32)
    TOKST = AR[:, 0:4 * D]

    class Arena:
        def __init__(self, ap):
            self.ap = ap
            self.off = 0

        def take(self, n_elems_f32, align=16):
            o = (self.off + align - 1) // align * align
            self.off = o + n_elems_f32
            assert self.off <= ARENA_BYTES // 4, f"arena overflow {self.off * 4}B"
            return o

        def f32(self, n):
            o = self.take(n)
            return self.ap[:, o:o + n]

        def bf16(self, n):
            o = self.take((n + 1) // 2)
            return self.ap[:, o:o + (n + 1) // 2].bitcast(BF16)

        def reset(self):
            self.off = 0

    PB = [es.enter_context(nc.psum_tensor(f"pb{i}", [128, 512], F32)) for i in range(8)]

    Sd = Sched(nc)
    t, v, s, g, sy = Sd.tensor, Sd.vector, Sd.scalar, Sd.gpsimd, Sd.sync

    _cc = {}

    def ccol(val):
        if val not in _cc:
            i = len(_cc)
            assert i < 16
            ap = CONST[:, i:i + 1]
            h = g.memset(ap, float(val), tag="ccol")
            _cc[val] = (ap, h)
        return _cc[val]

    inits = []

    h_i1 = g.iota(idf[:, :], pattern=[[1, 128]], base=0, channel_multiplier=-1,
                  allow_small_or_imprecise_dtypes=True)
    h_i3 = v.tensor_scalar(out=idf[:, :], in0=idf[:, :], scalar1=0.0, scalar2=None,
                           op0=ALU.is_equal, wait=[h_i1])
    h_i4 = v.tensor_copy(idb[:, :], idf[:, :], wait=[h_i3])
    inits += [h_i4]
    piota = CONST[:, 50:51]
    h_pio = g.iota(piota, pattern=[[1, 1]], base=0, channel_multiplier=1,
                   allow_small_or_imprecise_dtypes=True, tag="piota")
    r_half_i = AR[0:1, 8192 + 192:8192 + 224]
    h_rhi = g.iota(r_half_i, pattern=[[2, 32]], base=0, channel_multiplier=0,
                   allow_small_or_imprecise_dtypes=True, tag="rhalf")
    for _cv in (-0.5, float(np.pi / 2), float(np.log(1.0 / (2 * np.pi))),
                25.6 / 89.6, -8.0, 1.0, 1e-5):
        ccol(_cv)

    h_m1 = g.iota(MLOC[:, 0:128], pattern=[[1, 128]], base=0, channel_multiplier=-1,
                  allow_small_or_imprecise_dtypes=True)
    h_m2 = v.tensor_scalar(out=MLOC[:, 0:128], in0=MLOC[:, 0:128], scalar1=0.0, scalar2=float(BIGNEG),
                           op0=ALU.is_lt, op1=ALU.mult, wait=[h_m1])
    h_m3 = g.iota(MLOC[:, 128:256], pattern=[[1, 128]], base=0, channel_multiplier=-1,
                  allow_small_or_imprecise_dtypes=True, wait=[h_m1])
    h_m4 = v.tensor_scalar(out=MLOC[:, 128:256], in0=MLOC[:, 128:256], scalar1=0.0, scalar2=float(BIGNEG),
                           op0=ALU.is_gt, op1=ALU.mult, wait=[h_m3])
    h_m5 = g.iota(MBIG[:, :], pattern=[[1, 512]], base=0, channel_multiplier=-1,
                  allow_small_or_imprecise_dtypes=True, wait=[h_m3])
    h_m6 = v.tensor_scalar(out=MBIG[:, :], in0=MBIG[:, :], scalar1=0.0, scalar2=float(BIGNEG),
                           op0=ALU.is_lt, op1=ALU.mult, wait=[h_m5])
    inits += [h_m2, h_m4, h_m6]

    for l in (0, 1):
        for c in range(8):
            inits.append(sy.dma(lngt[:, l * 8 + c:l * 8 + c + 1],
                                lng[l, c * 128:(c + 1) * 128].unsqueeze(1), ch="prm"))
            inits.append(sy.dma(lnbt[:, l * 8 + c:l * 8 + c + 1],
                                lnb[l, c * 128:(c + 1) * 128].unsqueeze(1), ch="prm"))
            inits.append(sy.dma(b2t[:, l * 8 + c:l * 8 + c + 1],
                                b2[l, c * 128:(c + 1) * 128].unsqueeze(1), ch="prm"))
        for j in range(22):
            cw = 128 if j < 21 else 42
            inits.append(sy.dma(b1at[0:cw, l * 22 + j:l * 22 + j + 1],
                                b1[l, 128 * j:128 * j + cw].unsqueeze(1), ch="prm"))
            inits.append(sy.dma(b1gt[0:cw, l * 22 + j:l * 22 + j + 1],
                                b1[l, 2730 + 128 * j:2730 + 128 * j + cw].unsqueeze(1), ch="prm"))

    def ST3(c0, c1, t0, t1):
        return ST[:, :].rearrange("p (c t) -> p c t", c=CT)[:, c0:c1, t0:t1]

    def emit_ln_load_transpose():
        slot = [dict(readers=[], evac=[None, None]) for _ in range(4)]
        bank_guard = [dict(evac=[None, None]), dict(evac=[None, None])]
        lastv = []
        for i in range(TT):
            sl = i % 4
            st = slot[sl]
            xt = TOKST[:, sl * D:(sl + 1) * D]
            hld = sy.dma(xt, hs[i * 128:(i + 1) * 128, :], wait=st["readers"], ch=f"in{sl}")
            cb = 64 + sl * 8
            mu = CONST[:, cb:cb + 1]
            ssq = CONST[:, cb + 1:cb + 2]
            var = CONST[:, cb + 2:cb + 3]
            rst = CONST[:, cb + 3:cb + 4]
            nmr = CONST[:, cb + 4:cb + 5]
            sqt = AR[:, 4 * D + (sl % 2) * 1024: 4 * D + (sl % 2 + 1) * 1024]
            pr = st.get("stats", [])
            h1 = v.reduce_sum(out=mu, in_=xt, axis=AX.X, wait=[hld] + pr)
            h2 = s.activation(out=sqt, in_=xt, func=AF.Square, wait=[hld] + pr)
            h3 = v.reduce_sum(out=ssq, in_=sqt, axis=AX.X, wait=[h2])
            h4 = v.tensor_scalar(out=mu, in0=mu, scalar1=1.0 / D, scalar2=None,
                                 op0=ALU.mult, wait=[h1])
            h5 = v.tensor_scalar(out=var, in0=mu, scalar1=mu, scalar2=None,
                                 op0=ALU.mult, wait=[h4])
            h6 = v.tensor_scalar(out=var, in0=ssq, scalar1=1.0 / D, scalar2=var,
                                 op0=ALU.mult, op1=ALU.subtract, wait=[h3, h5])
            epsc, heps = ccol(1e-5)
            h7 = s.activation(out=rst, in_=var, func=AF.Ln, bias=epsc, wait=[h6, heps])
            h8 = s.activation(out=rst, in_=rst, func=AF.Exp, scale=-0.5, wait=[h7])
            h9 = v.tensor_scalar(out=nmr, in0=mu, scalar1=rst, scalar2=-1.0,
                                 op0=ALU.mult, op1=ALU.mult, wait=[h8])
            h10 = s.activation(out=xt, in_=xt, func=AF.Identity, scale=rst, bias=nmr,
                               wait=[h9, h3])
            st["stats"] = [h9, h10]
            pbank = (PB[(sl % 2) * 2], PB[(sl % 2) * 2 + 1])
            bg = bank_guard[sl % 2]
            evs = []
            for c in range(CT):
                pb = pbank[c // 4]
                po = (c % 4) * 128
                w = [h10]
                if bg["evac"][c // 4] is not None:
                    w.append(bg["evac"][c // 4])
                evs.append(t.transpose(pb[:, po:po + 128], xt[:, c * 128:(c + 1) * 128],
                                       idf[:, :], wait=w, tag="trT"))
            hv = []
            for half in range(2):
                hv.append(v.tensor_copy(
                    ST3(half * 4, (half + 1) * 4, i * 128, (i + 1) * 128),
                    pbank[half][:, :].rearrange("p (c t) -> p c t", c=4),
                    wait=evs[half * 4:half * 4 + 4], tag="evT"))
                bg["evac"][half] = hv[half]
            st["readers"] = [h10] + evs
            lastv = hv
        return lastv

    def emit_store_stream(extra_wait=(), combine=None):
        slot = [dict(readers=[], evac=[None, None]), dict(readers=[], evac=[None, None])]
        for i in range(TT):
            sl = i % 2
            st = slot[sl]
            xt = TOKST[:, sl * D:(sl + 1) * D]
            pbank = (PB[4 + sl * 2], PB[5 + sl * 2])
            evs = []
            for c in range(CT):
                pb = pbank[c // 4]
                po = (c % 4) * 128
                w = list(extra_wait) if i < 2 else []
                if st["evac"][c // 4] is not None:
                    w.append(st["evac"][c // 4])
                evs.append(t.transpose(pb[:, po:po + 128],
                                       ST[:, c * S + i * 128: c * S + (i + 1) * 128],
                                       idf[:, :], wait=w, tag="trS"))
            hst = []
            for half in range(2):
                hst.append(v.tensor_copy(
                    xt.rearrange("p (c q) -> p c q", c=CT)[:, half * 4:(half + 1) * 4, :],
                    pbank[half][:, :].rearrange("p (c t) -> p c t", c=4),
                    wait=evs[half * 4:half * 4 + 4] + st["readers"], tag="evS"))
                st["evac"][half] = hst[half]
            if combine is not None:
                hst = combine(i, sl, xt, hst)
            hd = sy.dma(out[i * 128:(i + 1) * 128, :], xt, wait=hst, ch=f"out{sl}")
            st["readers"] = [hd]
        return [ChanTotal("sync", "out0"), ChanTotal("sync", "out1")]

    onesrow = CONST[0:1, 128:256]
    h_onesrow = g.memset(onesrow, 1.0, tag="onesrow")
    PRM = ChanTotal("sync", "prm")

    w1ascr = nc.dram_tensor("w1ascr", [2, 22, 8, 128, 128], BF16, kind="Internal").ap()
    w1gscr = nc.dram_tensor("w1gscr", [2, 22, 8, 128, 128], BF16, kind="Internal").ap()
    w2scr = nc.dram_tensor("w2scr", [2, GD, 1024], BF16, kind="Internal").ap()

    def emit_prestage_mlp(l):
        hs_ = []
        for (scr, c0) in ((w1ascr, 0), (w1gscr, GD)):
            for j in range(22):
                cw = 128 if j < 21 else 42
                srcj = w1[l, :, c0 + 128 * j:c0 + 128 * j + cw].rearrange(
                    "(kt p) c -> kt p c", p=128)
                hs_.append(g.dma(scr[l, j, :, :, 0:cw], srcj, ch=f"pm{l}", tag="presw1"))
        hs_.append(g.dma(w2scr[l], w2[l], ch=f"pm{l}", tag="presw2"))
        return hs_

    def emit_mlp(l, phase_wait=(), dbg=False):
        TB = 512
        NB = S // TB
        NCK = 22
        a_h2lo = AR[:, 0:2048].rearrange("p (c t) -> p c t", c=4)
        a_y2 = AR[:, 2048:4096].bitcast(BF16).rearrange("p (c t) -> p c t", c=8)
        a_hg = AR[:, 4096:9728].bitcast(BF16).rearrange("p (c t) -> p c t", c=22)
        a_sq = [AR[:, 9728 + i * 512: 9728 + (i + 1) * 512] for i in range(2)]
        a_gg = [AR[:, 10752 + i * 512: 10752 + (i + 1) * 512] for i in range(2)]
        a_w1 = [AR[:, 11776 + i * 1024: 11776 + (i + 1) * 1024].bitcast(BF16)
                .rearrange("p (kt c) -> p kt c", kt=8) for i in range(2)]
        a_w2 = [AR[:, 13824 + i * 512: 13824 + (i + 1) * 512].bitcast(BF16) for i in range(2)]
        r_mu = AR[0:1, 14848:15360]
        r_var = AR[0:1, 15360:15872]
        r_rst = AR[0:1, 15872:16384]

        ones_col, h_oc = ccol(1.0)
        eps_col, h_eps = ccol(1e-5)
        h_or = h_onesrow
        PRES = ChanTotal("gpsimd", f"pm{l}")
        PRESG = PRES

        h_stash = v.tensor_copy(
            STASH[:, 0:28].rearrange("p (c b) -> p c b", c=4),
            ST3(4, 8, 0, S)[:, :, 511:S - 1:512],
            wait=list(phase_wait), tag="stashB")

        hw1 = [None, None]
        hw2 = [None, None]
        ph2_res = [None] * 8
        ph1_free = [None] * 4
        last_apply_readers = []
        dbgtok = []
        h_h2lo_prev = []
        for B in range(NB):
            t0 = B * TB
            wz = [h_stash] + h_h2lo_prev
            if B == 0:
                h1 = v.memset(a_h2lo[:, :, 0:1], 0.0, wait=wz, tag="h2lo0")
                h2 = v.tensor_copy(a_h2lo[:, :, 1:TB], ST3(4, 8, 0, TB - 1), wait=wz, tag="h2lo")
                h_h2lo = [h1, h2]
            else:
                h2 = v.tensor_copy(a_h2lo[:, :, 1:TB], ST3(4, 8, t0, t0 + TB - 1), wait=wz, tag="h2lo")
                h1 = v.tensor_copy(a_h2lo[:, :, 0:1],
                                   STASH[:, 0:28].rearrange("p (c b) -> p c b", c=4)[:, :, B - 1:B],
                                   wait=wz, tag="h2lob")
                h_h2lo = [h1, h2]

            def h2tile(c):
                if c < 4:
                    return ST[:, c * S + t0: c * S + t0 + TB]
                return a_h2lo[:, c - 4, :]

            hsum = []
            hsq = []
            sq_rot = [None, None]
            for c in range(CT):
                w = h_h2lo if c >= 4 else list(phase_wait)
                hsum.append(t.matmul(PB[6][0:1, 0:TB], ones_col, h2tile(c),
                                     start=(c == 0), stop=(c == CT - 1),
                                     wait=w + [h_oc] + ([ph2_res[6]] if c == 0 and ph2_res[6] else []),
                                     tag="lnsum"))
                wsq = w + ([sq_rot[c % 2]] if sq_rot[c % 2] else [])
                hq = s.activation(out=a_sq[c % 2], in_=h2tile(c), func=AF.Square, wait=wsq, tag="lnsq")
                hm = t.matmul(PB[7][0:1, 0:TB], ones_col, a_sq[c % 2],
                              start=(c == 0), stop=(c == CT - 1),
                              wait=[hq, h_oc] + ([ph2_res[7]] if c == 0 and ph2_res[7] else []),
                              tag="lnssq")
                sq_rot[c % 2] = hm
                hsq.append(hm)
            hmu = v.tensor_scalar(out=r_mu, in0=PB[6][0:1, 0:TB], scalar1=1.0 / D, scalar2=None,
                                  op0=ALU.mult, wait=[hsum[-1]], tag="mu")
            hv1 = v.tensor_tensor(out=r_var, in0=r_mu, in1=r_mu, op=ALU.mult, wait=[hmu], tag="mu2")
            hv2 = v.scalar_tensor_tensor(out=r_var, in0=PB[7][0:1, 0:TB], scalar=1.0 / D,
                                         in1=r_var, op0=ALU.mult, op1=ALU.subtract,
                                         wait=[hsq[-1], hv1], tag="var")
            hr1 = s.activation(out=r_rst, in_=r_var, func=AF.Ln, bias=eps_col[0:1, :],
                               wait=[hv2, h_eps], tag="lnr")
            hr2 = s.activation(out=r_rst, in_=r_rst, func=AF.Exp, scale=-0.5, wait=[hr1], tag="rstd")
            hbc1 = t.matmul(PB[6][:, 0:TB], onesrow, r_mu, start=True, stop=True,
                            wait=[hmu, hr2, h_or, hv2], tag="bcmu")
            hbc2 = t.matmul(PB[7][:, 0:TB], onesrow, r_rst, start=True, stop=True,
                            wait=[hr2, h_or], tag="bcr")
            happly = []
            ap_rot = [None, None]
            for c in range(CT):
                w0 = (h_h2lo if c >= 4 else []) + [hbc1] + dbgtok + ([ap_rot[c % 2]] if ap_rot[c % 2] else [])
                ht1 = v.tensor_tensor(out=a_sq[c % 2], in0=h2tile(c), in1=PB[6][:, 0:TB],
                                      op=ALU.subtract, wait=w0, tag="apsub")
                ht2 = v.tensor_tensor(out=a_sq[c % 2], in0=a_sq[c % 2], in1=PB[7][:, 0:TB],
                                      op=ALU.mult, wait=[ht1, hbc2], tag="apmul")
                ht3 = s.activation(out=a_y2[:, c, :], in_=a_sq[c % 2], func=AF.Identity,
                                   scale=lngt[:, l * 8 + c:l * 8 + c + 1],
                                   bias=lnbt[:, l * 8 + c:l * 8 + c + 1],
                                   wait=[ht2, PRM], tag="apaff")
                ap_rot[c % 2] = ht3
                happly.append(ht3)

            if dbg and B == 0:
                sy.dma(dbg_o, a_y2, wait=happly, ch="dbg")
                dbgtok.append(ChanTotal("sync", "dbg"))
            hstt_all = []
            gg_rot = [None, None]
            for j in range(NCK):
                cw = 128 if j < 21 else 42
                sl = j % 2
                wd = [hw1[sl]] if hw1[sl] else list(phase_wait)
                hda = sy.dma(a_w1[sl][:, :, 0:cw], w1ascr[l, j, :, :, 0:cw].rearrange("kt p c -> p kt c"),
                             wait=wd + [PRES], ch=f"w1{sl}", tag="w1a")
                hdg = sy.dma(a_w1[sl][:, :, 128:128 + cw], w1gscr[l, j, :, :, 0:cw].rearrange("kt p c -> p kt c"),
                             wait=wd + [PRESG], ch=f"w1g{sl}", tag="w1g")
                hw1[sl] = None
                pa, pg = PB[0 + 2 * (j % 2)], PB[1 + 2 * (j % 2)]
                hmma = []
                hmmg = []
                for kt in range(CT):
                    wm = [hda] + happly + ([ph1_free[0 + 2 * (j % 2)]] if kt == 0 and ph1_free[0 + 2 * (j % 2)] else [])
                    hmma.append(t.matmul(pa[0:cw, 0:TB], a_w1[sl][:, kt, 0:cw], a_y2[:, kt, :],
                                         start=(kt == 0), stop=(kt == CT - 1), wait=wm, tag="w1amm"))
                    wg = [hdg] + happly + ([ph1_free[1 + 2 * (j % 2)]] if kt == 0 and ph1_free[1 + 2 * (j % 2)] else [])
                    hmmg.append(t.matmul(pg[0:cw, 0:TB], a_w1[sl][:, kt, 128:128 + cw], a_y2[:, kt, :],
                                         start=(kt == 0), stop=(kt == CT - 1), wait=wg, tag="w1gmm"))
                wgg = [hmmg[-1]] + ([gg_rot[sl]] if gg_rot[sl] else [])
                hgel = s.activation(out=a_gg[sl].unsqueeze(0)[0][0:cw, :], in_=pg[0:cw, 0:TB],
                                    func=AF.Gelu, bias=b1gt[0:cw, l * 22 + j:l * 22 + j + 1],
                                    wait=wgg, tag="gelu")
                hstt = v.scalar_tensor_tensor(
                    out=a_hg[0:cw, j, :],
                    in0=pa[0:cw, 0:TB],
                    scalar=b1at[0:cw, l * 22 + j:l * 22 + j + 1],
                    in1=a_gg[sl].unsqueeze(0)[0][0:cw, :],
                    op0=ALU.add, op1=ALU.mult, wait=[hgel, hmma[-1], PRM], tag="hg")
                gg_rot[sl] = hstt
                ph1_free[0 + 2 * (j % 2)] = hstt
                ph1_free[1 + 2 * (j % 2)] = hstt
                hw1[sl] = hmmg[-1]
                hstt_all.append(hstt)

            hmm2 = [[None] * 8 for _ in range(NCK)]
            for kt in range(NCK):
                kw = 128 if kt < 21 else 42
                sl = kt % 2
                wd = [hw2[sl]] if hw2[sl] else list(phase_wait)
                hd2 = s.dma(a_w2[sl][0:kw, :], w2scr[l, kt * 128:kt * 128 + kw],
                            wait=wd + [PRESG], ch=f"w2{sl}", tag="w2")
                for m in range(8):
                    wm = [hd2, hstt_all[kt]]
                    if kt == 0:
                        wm += [ph1_free[m]] if m < 4 and ph1_free[m] else []
                        wm += [ph2_res[m]] if ph2_res[m] else []
                        if m in (6, 7):
                            wm += [hbc1, hbc2, happly[-1]]
                    hmm2[kt][m] = t.matmul(PB[m][:, 0:TB],
                                           a_w2[sl][0:kw, m * 128:(m + 1) * 128],
                                           a_hg[0:kw, kt, :],
                                           start=(kt == 0), stop=(kt == NCK - 1), wait=wm, tag="w2mm")
                hw2[sl] = hmm2[kt][7]
            hres = []
            for m in range(8):
                hres.append(v.scalar_tensor_tensor(
                    out=ST[:, m * S + t0: m * S + t0 + TB],
                    in0=PB[m][:, 0:TB], scalar=b2t[:, l * 8 + m:l * 8 + m + 1],
                    in1=ST[:, m * S + t0: m * S + t0 + TB],
                    op0=ALU.add, op1=ALU.add,
                    wait=[hmm2[NCK - 1][m], h_stash], tag="zres"))
                ph2_res[m] = hres[m]
            h_h2lo_prev = hres
        return [x for x in ph2_res if x]

    def emit_tables():
        A0 = 8192
        b_i2p = AR[:, A0:A0 + 32]
        b_lnb = AR[:, A0 + 32:A0 + 64]
        b_qs = AR[:, A0 + 64:A0 + 128]
        b_ks = AR[:, A0 + 128:A0 + 192]
        r_half = AR[0:1, A0 + 192:A0 + 224]
        r_tmp = AR[0:1, A0 + 224:A0 + 288]
        stg = [AR[:, A0 + 512 + i * 1024:A0 + 512 + (i + 1) * 1024] for i in range(2)]
        scr = [AR[:, A0 + 2560 + i * 256:A0 + 2560 + (i + 1) * 256] for i in range(6)]
        c_mh = ccol(-0.5)
        c_p2 = ccol(np.pi / 2)
        c_l2p = ccol(float(np.log(1.0 / (2 * np.pi))))
        c_b = ccol(25.6 / 89.6)
        c_m8 = ccol(-8.0)
        hh = h_rhi
        h1 = s.activation(out=r_tmp[0:1, 0:32], in_=r_half, func=AF.Exp,
                          scale=float(-np.log(10000.0) / 64.0), bias=c_l2p[0][0:1, :],
                          wait=[hh, c_l2p[1]], tag="invf")
        hb1 = t.matmul(PB[6][:, 0:32], onesrow, r_tmp[0:1, 0:32], start=True, stop=True,
                       wait=[h1, h_onesrow], tag="bci2p")
        hb1v = v.tensor_copy(b_i2p, PB[6][:, 0:32], wait=[hb1], tag="evb")
        h2 = s.activation(out=r_tmp[0:1, 32:64], in_=r_half, func=AF.Identity,
                          scale=float(1.0 / 89.6), bias=c_b[0][0:1, :], wait=[hh, c_b[1]], tag="base")
        h3 = s.activation(out=r_tmp[0:1, 32:64], in_=r_tmp[0:1, 32:64], func=AF.Ln,
                          wait=[h2], tag="lnbase")
        hb2 = t.matmul(PB[7][:, 0:32], onesrow, r_tmp[0:1, 32:64], start=True, stop=True,
                       wait=[h3, h_onesrow, hb1], tag="bclnb")
        hb2v = v.tensor_copy(b_lnb, PB[7][:, 0:32], wait=[hb2], tag="evb2")
        htabs = []
        h_rtmp_free = [hb1, hb2]
        hprev = [None, None]
        for l in (0, 1):
            hq = sy.dma(r_tmp[0:1, 0:64], qs_in[l].unsqueeze(0), wait=h_rtmp_free,
                        ch="rows", tag="qsrow")
            hq2 = v.tensor_scalar(out=r_tmp[0:1, 0:64], in0=r_tmp[0:1, 0:64], scalar1=8.0,
                                  scalar2=None, op0=ALU.mult, wait=[hq, hb1v], tag="qs8")
            hqb = t.matmul(PB[6][:, 0:64], onesrow, r_tmp[0:1, 0:64], start=True, stop=True,
                           wait=[hq2, h_onesrow], tag="bcqs")
            hqv = v.tensor_copy(b_qs, PB[6][:, 0:64], wait=[hqb], tag="evqs")
            hk = sy.dma(r_tmp[0:1, 0:64], ks_in[l].unsqueeze(0), wait=[hqb], ch="rows", tag="ksrow")
            hkb = t.matmul(PB[7][:, 0:64], onesrow, r_tmp[0:1, 0:64], start=True, stop=True,
                           wait=[hk, h_onesrow, hb2v], tag="bcks")
            hkv = v.tensor_copy(b_ks, PB[7][:, 0:64], wait=[hkb], tag="evks")
            h_rtmp_free = [hkb]
            for i in range(TT):
                sl = i % 2
                tcol = CONST[:, 32 + sl * 8:33 + sl * 8]
                pcol = CONST[:, 33 + sl * 8:34 + sl * 8]
                u_, w_, aw_ = scr[sl * 3:sl * 3 + 3]
                u, w, aw = u_[:, 0:32], w_[:, 0:32], aw_[:, 0:32]
                iu = u_[:, 32:64].bitcast(I32)
                sc_, isc_, sin_, cos_ = (w_[:, 32:64], w_[:, 64:96],
                                         aw_[:, 32:64], aw_[:, 64:96])
                st4 = stg[sl]
                wv = [hprev[sl]] if hprev[sl] else [hqv, hkv]
                hio = v.tensor_scalar(out=tcol, in0=piota, scalar1=float(128 * i),
                                      scalar2=None, op0=ALU.add, wait=[h_pio] + wv, tag="tcol")
                h4 = v.tensor_scalar(out=u, in0=b_i2p, scalar1=tcol, scalar2=None,
                                     op0=ALU.mult, wait=[hio, hb1v] + wv, tag="u")
                h5 = v.tensor_copy(iu, u, wait=[h4], tag="cvt")
                h6 = v.tensor_copy(w, iu, wait=[h5], tag="back")
                h7 = v.tensor_tensor(out=w, in0=u, in1=w, op=ALU.subtract, wait=[h6], tag="frac")
                h8a = v.tensor_scalar(out=isc_, in0=w, scalar1=0.5, scalar2=None,
                                      op0=ALU.is_gt, wait=[h7], tag="wgt")
                h8 = v.tensor_tensor(out=w, in0=w, in1=isc_, op=ALU.subtract,
                                     wait=[h8a], tag="wrap")
                h9 = s.activation(out=aw, in_=w, func=AF.Abs, wait=[h8], tag="aw")
                h10 = s.activation(out=sin_, in_=w, func=AF.Sin, scale=float(2 * np.pi),
                                   wait=[h9], tag="sin")
                h11 = s.activation(out=cos_, in_=aw, func=AF.Sin, scale=float(-2 * np.pi),
                                   bias=c_p2[0], wait=[h10, c_p2[1]], tag="cos")
                h12 = s.activation(out=pcol, in_=tcol, func=AF.Identity, scale=1.0 / 256.0,
                                   bias=c_m8[0], wait=[hio, c_m8[1]], tag="pcol")
                h13 = v.tensor_scalar(out=u, in0=b_lnb, scalar1=pcol, scalar2=None,
                                      op0=ALU.mult, wait=[h12, h5, hb2v], tag="plnb")
                h14 = s.activation(out=sc_, in_=u, func=AF.Exp, wait=[h13, h11], tag="sc")
                h15 = s.activation(out=isc_, in_=u, func=AF.Exp, scale=-1.0, wait=[h14, h8], tag="isc")
                ha1 = v.tensor_tensor(out=u, in0=cos_, in1=sc_, op=ALU.mult,
                                      wait=[h15, h11], tag="cs")
                ha2 = v.tensor_tensor(out=w, in0=sin_, in1=sc_, op=ALU.mult,
                                      wait=[h14, h10, h8], tag="ss")
                hm = []
                for (dsto, src2, tb, rot) in ((0, u, b_qs, 0), (64, w, b_qs, 32),
                                              (128, None, b_ks, 0), (192, None, b_ks, 32)):
                    pass
                hm.append(v.tensor_tensor(out=st4[:, 0:32], in0=u, in1=b_qs[:, 0:32], op=ALU.mult, wait=[ha1, hqv] + wv, tag="cq0"))
                hm.append(v.tensor_tensor(out=st4[:, 32:64], in0=u, in1=b_qs[:, 32:64], op=ALU.mult, wait=[ha1, hqv], tag="cq1"))
                hm.append(v.scalar_tensor_tensor(out=st4[:, 64:96], in0=w, scalar=-1.0,
                                                 in1=b_qs[:, 32:64], op0=ALU.mult, op1=ALU.mult,
                                                 wait=[ha2, hqv], tag="sq0"))
                hm.append(v.tensor_tensor(out=st4[:, 96:128], in0=w, in1=b_qs[:, 0:32], op=ALU.mult, wait=[ha2, hqv], tag="sq1"))
                ha3 = v.tensor_tensor(out=u, in0=cos_, in1=isc_, op=ALU.mult,
                                      wait=[h15, hm[0], hm[1]], tag="ci")
                ha4 = v.tensor_tensor(out=w, in0=sin_, in1=isc_, op=ALU.mult,
                                      wait=[h15, hm[2], hm[3]], tag="si")
                hm.append(v.tensor_tensor(out=st4[:, 128:160], in0=u, in1=b_ks[:, 0:32], op=ALU.mult, wait=[ha3, hkv], tag="ck0"))
                hm.append(v.tensor_tensor(out=st4[:, 160:192], in0=u, in1=b_ks[:, 32:64], op=ALU.mult, wait=[ha3, hkv], tag="ck1"))
                hm.append(v.scalar_tensor_tensor(out=st4[:, 192:224], in0=w, scalar=-1.0,
                                                 in1=b_ks[:, 32:64], op0=ALU.mult, op1=ALU.mult,
                                                 wait=[ha4, hkv], tag="sk0"))
                hm.append(v.tensor_tensor(out=st4[:, 224:256], in0=w, in1=b_ks[:, 0:32], op=ALU.mult, wait=[ha4, hkv], tag="sk1"))
                hd = g.dma(tabscr[l, :, i * 128:(i + 1) * 128, :].rearrange("f p d -> p f d"),
                            st4[:, 0:256].rearrange("p (f d) -> p f d", f=4),
                            wait=hm, ch=f"tabst{sl}", tag="tabst")
                hprev[sl] = hd
                htabs.append(hd)
        return [ChanTotal("gpsimd", "tabst0"), ChanTotal("gpsimd", "tabst1")]

    wqkvscr = nc.dram_tensor("wqkvscr", [2, 3, 8, 128, 512], BF16, kind="Internal").ap()
    woscr = nc.dram_tensor("woscr", [2, 4, 128, 1024], BF16, kind="Internal").ap()

    def emit_prestage_attn(l):
        hs_ = []
        for part in range(3):
            srcp = wqkv[l, :, part * 512:(part + 1) * 512].rearrange(
                "(kt p) c -> kt p c", p=128)
            hs_.append(g.dma(wqkvscr[l, part], srcp, ch=f"pa{l}", tag="preqkv"))
        hs_.append(g.dma(woscr[l], wo_ffn[l].rearrange("(kt p) c -> kt p c", p=128),
                         ch=f"pa{l}", tag="prewo"))
        return hs_

    def emit_ffn_attn(l, phase_wait=(), dbg=False):
        a_tabs = [AR[:, i * 256:(i + 1) * 256] for i in range(2)]
        a_rsc = [AR[:, 512 + i * 512:512 + (i + 1) * 512] for i in range(2)] + \
                [AR[:, 15968:16480], AR[:, 16640:17152]]
        a_stg = [AR[:, 1536 + i * 256:1536 + (i + 1) * 256].bitcast(BF16) for i in range(2)]
        a_zbc = AR[:, 2048:6144].bitcast(BF16).rearrange("p (c t) -> p c t", c=8)
        a_zbt = AR[:, 6144:6656].bitcast(BF16).rearrange("p (c t) -> p c t", c=8)
        a_slab = [AR[:, 6656 + i * 256:6656 + (i + 1) * 256].bitcast(BF16) for i in range(3)]
        a_PT = AR[:, 0:3200].bitcast(BF16).rearrange("p (s h q) -> p s h q", s=5, h=2)
        a_on = AR[:, 3200:5248].bitcast(BF16).rearrange("p (q c) -> p q c", q=8)
        a_oT = AR[:, 0:2048].bitcast(BF16).rearrange("p (c t) -> p c t", c=4)
        a_woh = [AR[:, 2048:3072].bitcast(BF16).rearrange("p (c t) -> p c t", c=2),
                 AR[:, 5248:6272].bitcast(BF16).rearrange("p (c t) -> p c t", c=2)]
        a_qT = AR[:, 7680:9728].bitcast(BF16).rearrange("p (c t) -> p c t", c=4)
        a_kT = AR[:, 9728:12800].bitcast(BF16).rearrange("p (c t) -> p c t", c=4)
        a_vE = AR[:, 12800:15968].bitcast(BF16).rearrange("p (r t) -> p r t", r=12)
        vE4 = a_vE.rearrange("p r (h w) -> p r h w", h=8)

        PRES = ChanTotal("gpsimd", f"pa{l}")
        TABS = [ChanTotal("gpsimd", "tabst0"), ChanTotal("gpsimd", "tabst1")]
        pw = list(phase_wait)

        hq = sy.dma(ROWS[0:1, 0:64], qs_in[l].unsqueeze(0), wait=pw, ch="rows", tag="qsr")
        hk = sy.dma(ROWS[0:1, 64:128], ks_in[l].unsqueeze(0), wait=[hq], ch="rows", tag="ksr")
        hm1 = v.tensor_reduce(out=ROWS[0:1, 128:129], in_=ROWS[0:1, 0:64], axis=AX.X,
                              op=ALU.max, apply_absolute_value=True, wait=[hk] + pw, tag="mq")
        hm2_ = v.tensor_reduce(out=ROWS[0:1, 129:130], in_=ROWS[0:1, 64:128], axis=AX.X,
                               op=ALU.max, apply_absolute_value=True, wait=[hm1], tag="mk")
        hm3 = v.tensor_tensor(out=ROWS[0:1, 128:129], in0=ROWS[0:1, 128:129],
                              in1=ROWS[0:1, 129:130], op=ALU.mult, wait=[hm2_], tag="mmul")
        hm4_ = v.tensor_scalar(out=ROWS[0:1, 128:129], in0=ROWS[0:1, 128:129], scalar1=-8.0,
                               scalar2=None, op0=ALU.mult, wait=[hm3], tag="m8")
        hm5 = t.matmul(PB[6][:, 0:1], onesrow, ROWS[0:1, 128:129], start=True, stop=True,
                       wait=[hm4_, h_onesrow] + pw, tag="bcm")
        hbias = v.tensor_copy(BIAS[:, l:l + 1], PB[6][:, 0:1], wait=[hm5], tag="evbias")
        h_stash = v.tensor_copy(
            STASH[:, 0:12].rearrange("p (c b) -> p c b", c=4),
            ST3(4, 8, 0, S)[:, :, 1023:S - 1:1024], wait=pw, tag="stashA")
        if dbg:
            hz1 = g.memset(a_vE[:, :, :], 0.0, wait=pw, tag="dbg0")
            hz2 = g.memset(a_kT[:, :, :], 0.0, wait=pw, tag="dbg0")
            pw = pw + [hz1, hz2]
        h_vones = g.memset(vE4[:, :, :, 64:65], 1.0, wait=pw, tag="vones")

        tTE, tV, tS, tG = [hm5] + pw, [h_stash, hbias] + pw, list(pw), [h_vones] + pw
        hslab = [None, None, None]
        htab_rot = [None, None]
        for ch in range(4):
            cs, ce = ch * 8, ch * 8 + 8
            CS, CE = cs * 128, ce * 128
            wz = tV + tTE
            h_zb1 = v.tensor_copy(a_zbc[:, 0:4, :], ST3(0, 4, CS, CE), wait=wz, tag="zbu")
            h_zb2 = v.tensor_copy(a_zbc[:, 4:8, :], ST3(4, 8, CS, CE), wait=wz, tag="zbl")
            if ch == 0:
                hb0 = v.memset(a_zbt[:, 4:8, 0:1], 0.0, wait=wz, tag="zbt0")
            else:
                hb0 = v.tensor_copy(a_zbt[:, 4:8, 0:1],
                                    STASH[:, 0:12].rearrange("p (c b) -> p c b", c=4)[:, :, ch - 1:ch],
                                    wait=wz, tag="zbt0")
            hb1 = v.tensor_copy(a_zbt[:, 4:8, 1:128], ST3(4, 8, CS, CS + 127), wait=wz, tag="zbt1")
            hb2 = v.tensor_copy(a_zbt[:, 0:4, :], ST3(0, 4, CS, CS + 128), wait=wz, tag="zbtu")
            h_zb = [h_zb1, h_zb2, hb0, hb1, hb2]

            def statq(T, c):
                if c < 4:
                    return a_zbc[:, c, T * 128:(T + 1) * 128]
                if T == 0:
                    return a_zbt[:, c, 0:128]
                return a_zbc[:, c, T * 128 - 1:(T + 1) * 128 - 1]

            rotdone = [[None] * 8, [None] * 8, [None] * 8]
            tab_reader = [[], []]
            htr_ev = [[None] * 8, [None] * 8]
            hv_ev = [None] * 8
            tr_bank_guard = [None] * 8
            hrsc_guard = [tTE[0]] * 4
            hstg_guard = [tTE[0], tTE[0]]
            lastG = [tTE[0], tTE[0]]
            for part in range(3):
                hmm_last = [None] * 8
                for kt in range(CT):
                    sl = (part * CT + kt) % 3
                    wd = ([hslab[sl]] if hslab[sl] else pw) + [PRES]
                    hdsl = s.dma(a_slab[sl], wqkvscr[l, part, kt], wait=wd,
                                 ch=f"qkv{sl}", tag="qkvslab")
                    for T in range(8):
                        wm = [hdsl] + h_zb
                        if kt == 0:
                            if part == 0:
                                wm += tTE
                            else:
                                if rotdone[part - 1][T]:
                                    wm.append(rotdone[part - 1][T])
                                if tr_bank_guard[T]:
                                    wm.append(tr_bank_guard[T])
                        hmm_last[T] = t.matmul(
                            PB[T][:, 0:512], statq(T, kt), a_slab[sl],
                            start=(kt == 0), stop=(kt == CT - 1), wait=wm, tag=f"qkv{part}")
                    hslab[sl] = hmm_last[7]
                chst = {}

                def chainA(T, part=part, hmm_last=hmm_last):
                    gt = cs + T
                    sl = T % 2
                    wtb = tab_reader[sl] if tab_reader[sl] else (TABS + tTE)
                    htab = sy.dma(a_tabs[sl].rearrange("p (f d) -> p f d", f=4),
                                  tabscr[l, :, gt * 128:(gt + 1) * 128, :].rearrange("f p d -> p f d"),
                                  wait=wtb, ch=f"tab{sl}", tag="tabtile")
                    htab_rot[sl] = htab
                    tab_reader[sl] = []
                    ps3 = PB[T][:, 0:512].rearrange("p (h d) -> p h d", h=8)
                    nrm = CONST[:, 256 + T * 8:264 + T * 8]
                    rp = 2 * (T % 2)
                    wn = [hmm_last[T]] + ([hrsc_guard[rp]] if hrsc_guard[rp] else [])
                    hsq = s.activation(out=a_rsc[rp], in_=PB[T][:, 0:512], func=AF.Square,
                                       wait=wn, tag="nsq")
                    hrd = v.tensor_reduce(out=nrm, in_=a_rsc[rp].rearrange("p (h d) -> p h d", h=8),
                                          axis=AX.X, op=ALU.add, wait=[hsq], tag="nred")
                    hc1 = v.tensor_scalar(out=nrm, in0=nrm, scalar1=1e-18, scalar2=None,
                                          op0=ALU.max, wait=[hrd], tag="nclip")
                    hl1 = s.activation(out=nrm, in_=nrm, func=AF.Ln, wait=[hc1], tag="nln")
                    hl2 = s.activation(out=nrm, in_=nrm, func=AF.Exp, scale=-0.5, wait=[hl1], tag="nexp")
                    s0 = a_rsc[rp].rearrange("p (h d) -> p h d", h=8)
                    hr0 = v.tensor_tensor(out=s0, in0=ps3,
                                          in1=nrm.unsqueeze(2).broadcast_to([128, 8, 64]),
                                          op=ALU.mult, wait=[hl2], tag="rqn")
                    rotdone[part][T] = hr0
                    chst[T] = (sl, rp, htab, hr0)

                def chainB(T, part=part):
                    gt = cs + T
                    sl, rp, htab, hr0 = chst[T]
                    tb = a_tabs[sl].rearrange("p (f d) -> p f d", f=4)
                    ct_ = tb[:, 2 * part + 0:2 * part + 1, :].broadcast_to([128, 8, 64])
                    st_ = tb[:, 2 * part + 1:2 * part + 2, :].broadcast_to([128, 8, 64])
                    s0 = a_rsc[rp].rearrange("p (h d) -> p h d", h=8)
                    s1 = a_rsc[rp + 1].rearrange("p (h d) -> p h d", h=8)
                    stgv = a_stg[sl].rearrange("p (h d) -> p h d", h=8)
                    g1 = [hrsc_guard[rp + 1]] if hrsc_guard[rp + 1] else []
                    hp0 = v.tensor_tensor(out=stgv, in0=s0, in1=ct_, op=ALU.mult,
                                          wait=[hr0, htab] +
                                          ([hstg_guard[sl]] if hstg_guard[sl] else []), tag="rc")
                    hp1a = g.tensor_tensor(out=s1[:, :, 0:32], in0=s0[:, :, 32:64],
                                           in1=st_[:, :, 0:32], op=ALU.mult,
                                           wait=[hr0, htab] + g1, tag="rsa")
                    hp1b = g.tensor_tensor(out=s1[:, :, 32:64], in0=s0[:, :, 0:32],
                                           in1=st_[:, :, 32:64], op=ALU.mult,
                                           wait=[hr0, htab] + g1, tag="rsb")
                    tab_reader[sl] = tab_reader[sl] + [hp0, hp1a, hp1b]
                    lastG[:] = [hp1a, hp1b]
                    hadd = v.tensor_tensor(out=stgv, in0=stgv, in1=s1, op=ALU.add,
                                           wait=[hp0, hp1a, hp1b], tag="radd")
                    hrsc_guard[rp] = hadd
                    hrsc_guard[rp + 1] = hadd
                    pvt = PB[T].bitcast(BF16)
                    htr = []
                    for c4 in range(4):
                        wtr = [hadd]
                        if c4 == 0 and tr_bank_guard[T]:
                            wtr.append(tr_bank_guard[T])
                        htr.append(t.transpose(pvt[:, c4 * 128:(c4 + 1) * 128],
                                               a_stg[sl][:, c4 * 128:(c4 + 1) * 128],
                                               idb[:, :], wait=wtr, tag="trqk"))
                    if part == 0:
                        dview = a_qT[:, :, T * 128:(T + 1) * 128]
                    else:
                        dview = a_kT[:, :, (gt % 12) * 128:((gt % 12) + 1) * 128]
                    hev = v.tensor_copy(dview, pvt[:, 0:512].rearrange("p (c t) -> p c t", c=4),
                                        wait=htr + tTE, tag="evqk")
                    htr_ev[part][T] = hev
                    hstg_guard[sl] = hev
                    tr_bank_guard[T] = hev

                for T in range(8):
                    if part == 2:
                        gt = cs + T
                        slot = gt % 12
                        wv = [hmm_last[T], h_vones] + tTE
                        hv_ev[T] = v.tensor_copy(
                            vE4[:, slot, :, 0:64],
                            PB[T][:, 0:512].rearrange("p (h d) -> p h d", h=8),
                            wait=wv, tag="vev")
                        rotdone[2][T] = hv_ev[T]
                        continue
                    chainA(T)
                    if T >= 1:
                        chainB(T - 1)
                if part != 2:
                    chainB(7)
            hA1 = htr_ev[0] + htr_ev[1] + hv_ev
            dbg_tok = []
            if dbg and ch == 0:
                sy.dma(dbg_q, a_qT, wait=hA1, ch="dbg")
                sy.dma(dbg_k, a_kT, wait=hA1, ch="dbg")
                sy.dma(dbg_v, a_vE, wait=hA1 + [h_vones], ch="dbg")
                sy.dma(dbg_t, tabscr[l], ch="dbg", wait=TABS)
                dbg_tok = [ChanTotal("sync", "dbg")]
            scb = {(0, 0): (PB[2], PB[4], 0), (0, 1): (PB[3], PB[6], 0),
                   (1, 0): (PB[5], PB[4], 128), (1, 1): (PB[7], PB[6], 128)}
            jlist = list(range(max(cs - 4, 0), ce))
            h_on_done = []
            hlastPV = None
            a1done = [hA1[-1], hv_ev[-1]]
            allexp = []
            pv_guard = [None, None]
            for pr in range(4):
                exp_guard = {}
                sc_guard = {}
                exp_h = {}
                spans = {}
                for ji, J in enumerate(jlist):
                    slot_ = ji % 2
                    span_lo = max(128 * J, CS)
                    span_hi = min(128 * J + 640, CE)
                    N = span_hi - span_lo
                    N1 = min(N, 512)
                    N2 = N - N1
                    jslot = J % 12
                    spans[J] = (span_lo, N1, N2)
                    hsc = {}
                    for hh in range(2):
                        b512, b128, o128 = scb[(slot_, hh)]
                        kst = a_kT[hh * 64:(hh + 1) * 64, pr, jslot * 128:(jslot + 1) * 128]
                        qrh = a_qT[hh * 64:(hh + 1) * 64, pr, span_lo - CS:span_hi - CS]
                        wsc = a1done + ([exp_guard[(slot_, hh)]] if (slot_, hh) in exp_guard else [])
                        hsc[(hh, 0)] = t.matmul(b512[:, 0:N1], kst, qrh[:, 0:N1],
                                                start=True, stop=True, wait=wsc, tag="sc512")
                        if N2 > 0:
                            hsc[(hh, 1)] = t.matmul(b128[:, o128:o128 + N2], kst, qrh[:, N1:N],
                                                    start=True, stop=True, wait=wsc, tag="sc128")
                    hmk = {}
                    allsc = list(hsc.values())
                    for hh in range(2):
                        b512, b128, o128 = scb[(slot_, hh)]
                        ww = list(allsc)
                        hml = []
                        if J >= cs:
                            hml.append(v.tensor_tensor(out=b512[:, 0:128], in0=b512[:, 0:128],
                                                       in1=MLOC[:, 0:128], op=ALU.add,
                                                       wait=ww + [h_m2], tag="mkf"))
                        if J + 4 <= ce - 1:
                            off = 128 * (J + 4) - span_lo
                            tgt = b512[:, off:off + 128] if off < 512 else b128[:, o128:o128 + 128]
                            hml.append(v.tensor_tensor(out=tgt, in0=tgt, in1=MLOC[:, 128:256],
                                                       op=ALU.add, wait=ww + [h_m4], tag="mkl"))
                        hmk[hh] = hml if hml else ww
                    hex_ = {}
                    ptslot = J % 5
                    for hh in range(2):
                        b512, b128, o128 = scb[(slot_, hh)]
                        wex = hmk[hh] + [hsc[(hh, 0)], hbias]
                        if (hh, 1) in hsc:
                            wex.append(hsc[(hh, 1)])
                        if (slot_, hh) in sc_guard:
                            wex.append(sc_guard[(slot_, hh)])
                        if hlastPV is not None:
                            wex.append(hlastPV)
                        hex_[(hh, 0)] = s.activation(out=a_PT[:, ptslot, hh, 0:N1], in_=b512[:, 0:N1],
                                                     func=AF.Exp, bias=BIAS[:, l:l + 1],
                                                     wait=wex, tag="lexp")
                        if N2 > 0:
                            hex_[(hh, 1)] = s.activation(out=a_PT[:, ptslot, hh, N1:N],
                                                         in_=b128[:, o128:o128 + N2],
                                                         func=AF.Exp, bias=BIAS[:, l:l + 1],
                                                         wait=wex, tag="lexp2")
                        exp_guard[(slot_, hh)] = hex_[(hh, 1)] if (hh, 1) in hex_ else hex_[(hh, 0)]
                        allexp.append(exp_guard[(slot_, hh)])
                    exp_h[J] = hex_
                    if J < cs:
                        sc_guard[(slot_, 0)] = exp_guard[(slot_, 0)]
                        sc_guard[(slot_, 1)] = exp_guard[(slot_, 1)]
                        continue
                    Q = J
                    for hh in range(2):
                        head = 2 * pr + hh
                        js = [jj for jj in range(max(Q - 4, 0), Q + 1)]
                        hpv = None
                        for idx2, J2 in enumerate(js):
                            splo2, N1b, N2b = spans[J2]
                            qoff = 128 * Q - splo2
                            wpv = []
                            he = exp_h[J2]
                            if qoff + 128 <= N1b:
                                wpv.append(he[(hh, 0)])
                            else:
                                wpv.append(he[(hh, 1)])
                                if qoff < N1b:
                                    wpv.append(he[(hh, 0)])
                            first = (idx2 == 0)
                            if first and pv_guard[hh]:
                                wpv.append(pv_guard[hh])
                            if first:
                                wpv += a1done
                            hpv = t.matmul(PB[hh][:, 0:65], a_PT[:, J2 % 5, hh, qoff:qoff + 128],
                                           vE4[:, J2 % 12, head, 0:65],
                                           start=first, stop=(idx2 == len(js) - 1),
                                           wait=wpv, tag="pv")
                        hlastPV = hpv
                        sc_guard[(slot_, hh)] = hpv
                        rc = CONST[:, 48 + hh:49 + hh]
                        hrc = v.reciprocal(out=rc, in_=PB[hh][:, 64:65], wait=[hpv], tag="rcp")
                        hno = v.tensor_scalar(out=a_on[:, Q - cs, head * 64:(head + 1) * 64],
                                              in0=PB[hh][:, 0:64], scalar1=rc, scalar2=None,
                                              op0=ALU.mult, wait=[hrc], tag="onorm")
                        pv_guard[hh] = hno
                        h_on_done.append(hno)
            if dbg and ch == 0:
                sy.dma(dbg_o, a_on, wait=[h_on_done[-1]], ch="dbg")
            hwod1 = sy.dma(a_woh[0], woscr[l, 0:2].rearrange("c p t -> p c t"),
                           wait=[h for h in hslab if h] + [PRES, hlastPV, h_on_done[-1], allexp[-1]] + tTE,
                           ch="wo", tag="wodma")
            hwod = sy.dma(a_woh[1], woscr[l, 2:4].rearrange("c p t -> p c t"),
                          wait=[hwod1], ch="wo", tag="wodma2")
            hev3 = []
            allexp = list(exp_guard.values())
            for Q in range(8):
                pvt = PB[2].bitcast(BF16)
                htrq = []
                for c4 in range(4):
                    wtr = [h_on_done[-1]] + ([hev3[-1]] if hev3 else allexp + [hlastPV])
                    htrq.append(t.transpose(pvt[:, c4 * 128:(c4 + 1) * 128],
                                            a_on[:, Q, c4 * 128:(c4 + 1) * 128], idb[:, :],
                                            wait=wtr, tag="tro"))
                hev3.append(v.tensor_copy(a_oT[:, :, Q * 128:(Q + 1) * 128],
                                          pvt[:, 0:512].rearrange("p (c t) -> p c t", c=4),
                                          wait=htrq, tag="evo"))
            hres = [None] * 8
            hmmw = None
            for half in range(2):
                for m in range(8):
                    for kc in range(4):
                        wmw = [hwod, hev3[-1]]
                        if kc == 0 and half == 0:
                            wmw += [h_on_done[-1], hlastPV, allexp[-1]]
                        if kc == 0 and half == 1:
                            wmw.append(hres[m])
                        hmmw = t.matmul(PB[m][:, 0:512], a_woh[kc // 2][:, kc % 2, m * 128:(m + 1) * 128],
                                        a_oT[:, kc, half * 512:half * 512 + 512],
                                        start=(kc == 0), stop=(kc == 3), wait=wmw, tag="womm")
                    hres[m] = v.tensor_tensor(
                        out=ST[:, m * S + CS + half * 512: m * S + CS + half * 512 + 512],
                        in0=ST[:, m * S + CS + half * 512: m * S + CS + half * 512 + 512],
                        in1=PB[m][:, 0:512], op=ALU.add, wait=[hmmw, h_stash], tag="zatt")
            tTE = [hmmw]
            tV = list(hres) + dbg_tok
            tS = [allexp[-1]]
            tG = list(lastG)
        return tV + tTE



    wqTscr = nc.dram_tensor("wqTscr", [8, 128, 1024], BF16, kind="Internal").ap()
    wkTscr = nc.dram_tensor("wkTscr", [8, 128, 256], BF16, kind="Internal").ap()
    wvTscr = nc.dram_tensor("wvTscr", [8, 128, 256], BF16, kind="Internal").ap()
    woTscr = nc.dram_tensor("woTscr", [8, 128, 1024], BF16, kind="Internal").ap()

    def emit_prestage_big(phase_wait=()):
        stg_in = [AR[:, 6144 + i * 128:6144 + (i + 1) * 128].bitcast(BF16) for i in range(4)]
        stg_out = [AR[:, 6656 + i * 512:6656 + (i + 1) * 512].bitcast(BF16) for i in range(2)]
        pwl = list(phase_wait)
        dst64_q = lambda b: 128 * (b % 8) + 64 * (b // 8)
        dst64_k = lambda b: {0: 0, 2: 64, 1: 128, 3: 192}[b]
        dst64_n = lambda b: 64 * b
        jobs = [(wq, wqTscr, 8, dst64_q), (wk, wkTscr, 2, dst64_k),
                (wv, wvTscr, 2, dst64_n), (wo_big, woTscr, 8, dst64_n)]
        hld_g = [None] * 4
        htr_g = [None, None]
        hst_prev = []
        it = 0
        tc = 0
        for (wsrc, wdst, nrt, dmap) in jobs:
            for ktg in range(4):
                hev_all = [[], []]
                for r in range(nrt):
                    sl = it % 4
                    wl = ([hld_g[sl]] if hld_g[sl] else pwl)
                    hld = g.dma(stg_in[sl], wsrc[r * 128:(r + 1) * 128, ktg * 256:(ktg + 1) * 256],
                                wait=wl, ch=f"bw{sl}", tag="bw")
                    it += 1
                    last_tr = None
                    for kk in range(2):
                        bk = tc % 2
                        tc += 1
                        htr = t.transpose(PB[4 + bk].bitcast(BF16)[:, 0:128],
                                          stg_in[sl][:, kk * 128:(kk + 1) * 128], idb[:, :],
                                          wait=[hld] + ([htr_g[bk]] if htr_g[bk] else []), tag="bwtr")
                        last_tr = htr
                        for hf in range(2):
                            b = 2 * r + hf
                            hev_all[kk].append(v.tensor_copy(
                                stg_out[kk][:, dmap(b):dmap(b) + 64],
                                PB[4 + bk].bitcast(BF16)[:, hf * 64:(hf + 1) * 64],
                                wait=[htr] + hst_prev, tag="bwev"))
                        htr_g[bk] = hev_all[kk][-1]
                    hld_g[sl] = last_tr
                hst_prev = [sy.dma(wdst[ktg * 2 + kk], stg_out[kk][:, 0:nrt * 128],
                                   wait=hev_all[kk], ch="bigs", tag="bws") for kk in range(2)][-1:]
        return [ChanTotal("sync", "bigs")]

    def emit_bigattn(phase_wait=(), dbg=False):
        kTA = AR[:, 0:2048].bitcast(BF16)
        kTB = AR[:, 2048:4096].bitcast(BF16)
        a_vE = AR[:, 4096:8256].bitcast(BF16).rearrange("p (r w) -> p r w", r=32)
        a_y1 = AR[:, 8256:9280].bitcast(BF16).rearrange("p (c t) -> p c t", c=8)
        a_qT = AR[:, 9280:11328].bitcast(BF16).rearrange("p (m t) -> p m t", m=8)
        a_PT = AR[:, 11328:13376].bitcast(BF16).rearrange("p (s h q) -> p s h q", s=4, h=2)
        a_oT = AR[:, 13376:15424].bitcast(BF16).rearrange("p (c t) -> p c t", c=8)
        a_ws = [AR[:, 15424 + i * 512:15424 + (i + 1) * 512].bitcast(BF16) for i in range(2)]
        a_wkv = AR[:, 9280:11328].bitcast(BF16)
        a_sq = AR[:, 11328:11840]
        a_rows = AR[0:1, 11840:12352]
        a_nrows = AR[0:1, 8256:8768]
        a_obn = AR[:, 8768:9024].bitcast(BF16)
        a_ob = AR[:, 12352:12864].bitcast(BF16)
        BIGS = ChanTotal("sync", "bigs")
        pw = list(phase_wait)
        eps_col, h_eps = ccol(1e-5)
        ones_col, h_oc = ccol(1.0)
        l8_col, h_l8 = ccol(float(np.log(0.125)))
        h_it = g.iota(CONST[:, 100:101], pattern=[[1, 1]], base=0, channel_multiplier=1,
                      allow_small_or_imprecise_dtypes=True, wait=pw, tag="iotp")
        h_t1 = v.tensor_scalar(out=CONST[:, 101:102], in0=CONST[:, 100:101], scalar1=64.0,
                               scalar2=None, op0=ALU.is_lt, wait=[h_it], tag="otop")
        h_t2 = v.tensor_scalar(out=CONST[:, 102:103], in0=CONST[:, 100:101], scalar1=64.0,
                               scalar2=None, op0=ALU.is_ge, wait=[h_it], tag="obot")
        ones_top, ones_bot = CONST[:, 101:102], CONST[:, 102:103]
        h_vone = g.memset(a_vE.rearrange("p r (k w) -> p r k w", k=4)[:, :, :, 64:65], 1.0,
                          wait=pw, tag="bvones")

        asq_g = [[]]

        def chan_ln_chunk(t0, TBc, dst, wl, pg):
            b0, b1 = pg
            hsum = []
            hsq = []
            for c in range(CT):
                w = list(wl) if c == 0 else []
                hsum.append(t.matmul(b0[0:1, 0:TBc], ones_col, ST[:, c * S + t0:c * S + t0 + TBc],
                                     start=(c == 0), stop=(c == CT - 1), wait=w + [h_oc], tag="bln1"))
                hq = s.activation(out=a_sq[:, 0:TBc], in_=ST[:, c * S + t0:c * S + t0 + TBc],
                                  func=AF.Square,
                                  wait=w + ([hsq[-1]] if hsq else asq_g[0]), tag="bln2")
                hsq.append(t.matmul(b1[0:1, 0:TBc], ones_col, a_sq[:, 0:TBc],
                                    start=(c == 0), stop=(c == CT - 1), wait=[hq, h_oc] + w, tag="bln3"))
            r_mu = a_rows[0:1, 0:256]
            r_var = a_rows[0:1, 256:512]
            hmu = v.tensor_scalar(out=r_mu[0:1, 0:TBc], in0=b0[0:1, 0:TBc], scalar1=1.0 / D,
                                  scalar2=None, op0=ALU.mult, wait=[hsum[-1]], tag="bln4")
            hv1 = v.tensor_tensor(out=r_var[0:1, 0:TBc], in0=r_mu[0:1, 0:TBc], in1=r_mu[0:1, 0:TBc],
                                  op=ALU.mult, wait=[hmu], tag="bln5")
            hv2 = v.scalar_tensor_tensor(out=r_var[0:1, 0:TBc], in0=b1[0:1, 0:TBc], scalar=1.0 / D,
                                         in1=r_var[0:1, 0:TBc], op0=ALU.mult, op1=ALU.subtract,
                                         wait=[hsq[-1], hv1], tag="bln6")
            hr1 = s.activation(out=r_var[0:1, 0:TBc], in_=r_var[0:1, 0:TBc], func=AF.Ln,
                               bias=eps_col[0:1, :], wait=[hv2, h_eps], tag="bln7")
            hr2 = s.activation(out=r_var[0:1, 0:TBc], in_=r_var[0:1, 0:TBc], func=AF.Exp,
                               scale=-0.5, wait=[hr1], tag="bln8")
            hb0 = t.matmul(b0[:, 0:TBc], onesrow, r_mu[0:1, 0:TBc], start=True, stop=True,
                           wait=[hmu, hr2, h_onesrow], tag="bln9")
            hb1 = t.matmul(b1[:, 0:TBc], onesrow, r_var[0:1, 0:TBc], start=True, stop=True,
                           wait=[hr2, h_onesrow], tag="bln10")
            happ = []
            for c in range(CT):
                ha1 = v.tensor_tensor(out=a_sq[:, 0:TBc], in0=ST[:, c * S + t0:c * S + t0 + TBc],
                                      in1=b0[:, 0:TBc], op=ALU.subtract,
                                      wait=[hb0, hsq[-1]] + ([happ[-1]] if happ else []), tag="bln11")
                ha2 = v.tensor_tensor(out=dst[:, c, :], in0=a_sq[:, 0:TBc], in1=b1[:, 0:TBc],
                                      op=ALU.mult, wait=[ha1, hb1, hsq[-1]], tag="bln12")
                happ.append(ha2)
            asq_g[0] = [happ[-1]]
            return happ

        hkv = []
        hwk = sy.dma(a_wkv[:, 0:2048].rearrange("p (kt c) -> p kt c", kt=8),
                     wkTscr.rearrange("kt p c -> p kt c"), wait=pw + [BIGS], ch="bkv", tag="wkld")
        hwv = sy.dma(a_wkv[:, 2048:4096].rearrange("p (kt c) -> p kt c", kt=8),
                     wvTscr.rearrange("kt p c -> p kt c"), wait=[hwk], ch="bkv", tag="wvld")
        wkv3 = a_wkv.rearrange("p (x kt c) -> p x kt c", x=2, kt=8)
        hkmax = [None, None]
        hln_g = []
        hev_prev = []
        for cc in range(16):
            t0 = cc * 256
            hln = chan_ln_chunk(t0, 256, a_y1, pw + hln_g, (PB[6], PB[7]))
            hln_g = [hln[-1]]
            hk2 = {}
            for (xi, dstk) in ((0, kTA), (1, kTB)):
                hmmk = None
                for kt in range(CT):
                    wm = [hwk, hln[-1]]
                    if kt == 0:
                        wm += hev_prev
                    hmmk = t.matmul(PB[xi][:, 0:256], wkv3[:, 0, kt, 0 + 128 * xi:128 + 128 * xi],
                                    a_y1[:, kt, :], start=(kt == 0), stop=(kt == CT - 1),
                                    wait=wm, tag="kproj")
                hk2[xi] = hmmk
            hv2_ = {}
            for tt in range(2):
                hmmv = None
                for kt in range(CT):
                    wm = [hwv, hln[-1]]
                    if kt == 0:
                        wm += hev_prev
                    hmmv = t.matmul(PB[2 + tt][:, 0:256], a_y1[:, kt, tt * 128:(tt + 1) * 128],
                                    wkv3[:, 1, kt, :], start=(kt == 0), stop=(kt == CT - 1),
                                    wait=wm, tag="vproj")
                hv2_[tt] = hmmv
            hev = []
            hkn_prev = []
            for (xi, dstk) in ((0, kTA), (1, kTB)):
                hq = s.activation(out=a_sq[:, 0:256], in_=PB[xi][:, 0:256], func=AF.Square,
                                  wait=[hk2[xi], hln[-1]] + hkn_prev + asq_g[0], tag="ksq")
                wkn = [hq] + ([hkmax[0]] if hkmax[0] else [])
                hn1 = t.matmul(PB[4][0:1, xi * 256:xi * 256 + 256], ones_top, a_sq[:, 0:256],
                               start=True, stop=True, wait=wkn + [h_t1], tag="kn1")
                hn2 = t.matmul(PB[5][0:1, xi * 256:xi * 256 + 256], ones_bot, a_sq[:, 0:256],
                               start=True, stop=True, wait=wkn + [h_t2], tag="kn2")
                for (pbn, hn, kv) in ((PB[4], hn1, 0 + xi), (PB[5], hn2, 2 + xi)):
                    hr = v.tensor_reduce(out=a_rows[0:1, 320 + kv:321 + kv],
                                         in_=pbn[0:1, xi * 256:xi * 256 + 256],
                                         axis=AX.X, op=ALU.max, wait=[hn] + ([hkmax[0]] if hkmax[0] else []),
                                         tag="kred")
                    if cc == 0:
                        hm_ = v.tensor_copy(ROWS[0:1, 296 + kv:297 + kv], a_rows[0:1, 320 + kv:321 + kv],
                                            wait=[hr], tag="kmax0")
                    else:
                        hm_ = v.tensor_tensor(out=ROWS[0:1, 296 + kv:297 + kv],
                                              in0=ROWS[0:1, 296 + kv:297 + kv],
                                              in1=a_rows[0:1, 320 + kv:321 + kv], op=ALU.max,
                                              wait=[hr, hkmax[1]] if hkmax[1] else [hr], tag="kmax")
                    hkmax = [hm_, hm_]
                hkn_prev = [hn1, hn2]
                asq_g[0] = [hn1, hn2]
                hev.append(v.tensor_copy(dstk[:, t0:t0 + 256], PB[xi][:, 0:256],
                                         wait=[hk2[xi], hq], tag="kev"))
            for tt in range(2):
                gt = cc * 2 + tt
                hev.append(v.tensor_copy(
                    a_vE.rearrange("p r (k w) -> p r k w", k=4)[:, gt, :, 0:64],
                    PB[2 + tt][:, 0:256].rearrange("p (k d) -> p k d", k=4),
                    wait=[hv2_[tt], h_vone], tag="vev"))
            hev_prev = hev
            hln_g = [hln[-1]] + hev
        if dbg:
            sy.dma(dbg_k[:, 0:2, :].rearrange("p a t -> p (a t)")[:, 0:3072],
                   AR[:, 0:3072].bitcast(BF16)[:, 0:3072], wait=hev_prev + [hkmax[0]], ch="dbg")
            sy.dma(dbg_v[:, :, 0:260], a_vE[:, 0:12, :], wait=hev_prev, ch="dbg")
        tTE = hev_prev[:1]
        tV = hev_prev
        hwslab = [None, None]
        prev_wo = []
        prev_norm = []
        BIGW = [ChanTotal("sync", "bigs")]
        hexp_all = []
        dbg_gtok = []
        for gq in range(8):
            G0 = gq * 512
            hlnq_g = [tV[-1], hkmax[0]] + prev_wo + prev_norm
            hqmm = [None] * 8
            hqev = []
            for hf in range(2):
                hln = chan_ln_chunk(G0 + hf * 256, 256, a_y1, hlnq_g, (PB[6], PB[7]))
                hlnq_g = [hln[-1]]
                for kt in range(CT):
                    sl = kt % 2
                    wd = [hwslab[sl]] if hwslab[sl] else (pw + BIGW)
                    hdw = s.dma(a_ws[sl], wqTscr[kt], wait=wd, ch=f"bws{sl}", tag="wqld")
                    for M in range(8):
                        wm = [hdw, hln[-1]]
                        if kt == 0:
                            wm += [hqev[M]] if hf == 1 else prev_wo
                            if hf == 0:
                                wm += prev_norm
                        hqmm[M] = t.matmul(PB[M][:, hf * 256:hf * 256 + 256],
                                           a_ws[sl][:, M * 128:(M + 1) * 128], a_y1[:, kt, :],
                                           start=(kt == 0), stop=(kt == CT - 1), wait=wm,
                                           tag="qproj")
                    hwslab[sl] = hqmm[7]
                for M in range(8):
                    hqev.append(v.tensor_copy(a_qT[:, M, hf * 256:hf * 256 + 256],
                                              PB[M][:, hf * 256:hf * 256 + 256],
                                              wait=[hqmm[M]] + prev_norm, tag="qev"))
                hlnq_g = hlnq_g + hqev[-2:]
            hnorm = []
            asql = [a_sq, AR[:, 11840:12352]]
            qbg = [list(asq_g[0]), []]
            hqs = {}

            def qnA(M):
                b = M % 2
                hqs[M] = s.activation(out=asql[b][:, 0:512], in_=a_qT[:, M, :], func=AF.Square,
                                      wait=[hqev[M], hqev[8 + M]] + qbg[b], tag="qsq")

            def qnB(M):
                b = M % 2
                hn1 = t.matmul(PB[0][0:1, 0:512], ones_top, asql[b][:, 0:512], start=True,
                               stop=True, wait=[hqs[M], h_t1, hqev[0], hqev[8]] + ([hnorm[-2]] if hnorm else []), tag="qn1")
                hn2 = t.matmul(PB[1][0:1, 0:512], ones_bot, asql[b][:, 0:512], start=True,
                               stop=True, wait=[hqs[M], h_t2, hqev[1], hqev[9]] + ([hnorm[-1]] if hnorm else []), tag="qn2")
                hr1 = v.tensor_reduce(out=ROWS[0:1, 300 + M:301 + M], in_=PB[0][0:1, 0:512],
                                      axis=AX.X, op=ALU.max, wait=[hn1], tag="qred1")
                hr2 = v.tensor_reduce(out=ROWS[0:1, 308 + M:309 + M], in_=PB[1][0:1, 0:512],
                                      axis=AX.X, op=ALU.max, wait=[hn2], tag="qred2")
                hnorm.extend([hr1, hr2])
                qbg[b] = [hn1, hn2]

            for M in range(8):
                qnA(M)
                if M >= 1:
                    qnB(M - 1)
            qnB(7)
            asq_g[0] = qbg[0]
            hb1_ = v.tensor_copy(ROWS[0:1, 316:332].rearrange("p (a b) -> p a b", a=4),
                                 ROWS[0:1, 296:300].unsqueeze(2).broadcast_to([1, 4, 4]),
                                 wait=[hkmax[0]] + hnorm[-2:], tag="kkrow")
            hb2_ = v.tensor_tensor(out=ROWS[0:1, 300:316], in0=ROWS[0:1, 300:316],
                                   in1=ROWS[0:1, 316:332], op=ALU.mult, wait=[hb1_] + hnorm, tag="qk")
            hb3_ = s.activation(out=ROWS[0:1, 300:316], in_=ROWS[0:1, 300:316], func=AF.Ln,
                                wait=[hb2_], tag="lnqk")
            hb4_ = s.activation(out=ROWS[0:1, 300:316], in_=ROWS[0:1, 300:316], func=AF.Exp,
                                scale=0.5, bias=l8_col[0:1, :], wait=[hb3_, h_l8], tag="mrow")
            hb5_ = t.matmul(PB[2][:, 0:16], onesrow, ROWS[0:1, 300:316], start=True, stop=True,
                            wait=[hb4_, h_onesrow] + prev_wo, tag="bcm16")
            hbias = v.tensor_scalar(out=BIAS[:, 8:24], in0=PB[2][:, 0:16], scalar1=-1.0,
                                    scalar2=None, op0=ALU.mult, wait=[hb5_] + hexp_all[-1:], tag="nbias")
            if dbg and gq == 0:
                sy.dma(dbg_q.rearrange("p a b -> p (a b)"), a_qT.rearrange("p m t -> p (m t)"),
                       wait=hqev + [hbias], ch="dbg")
                dbg_gtok = [ChanTotal("sync", "dbg")]
            allexp = []
            hno = []
            norm_q = []
            for M in range(8):
                heads = (M, M + 8)
                pvb = 6 if M % 2 == 0 else 0
                kts = kTA if M < 4 else kTB
                njt = 4 * gq + 4
                hpv = [None, None]
                hexp_slot = {}
                scg = {}
                for bt0 in range(0, njt, 4):
                    jts = list(range(bt0, min(bt0 + 4, njt)))
                    hexp_b = {}
                    for jt in jts:
                        r = jt - 4 * gq
                        qlo = max(0, r) * 128 if r >= 0 else 0
                        N = 512 - (qlo)
                        sb0 = PB[2 + 2 * (jt % 2)]
                        sb1 = PB[3 + 2 * (jt % 2)]
                        for hh in range(2):
                            bnk = (sb0, sb1)[hh]
                            wsc = [hqev[M], hbias] + (hno[-1:] if hno else [])
                            if (jt % 2, hh) in scg:
                                wsc.append(scg[(jt % 2, hh)])
                            kst = kts[hh * 64:(hh + 1) * 64, jt * 128:(jt + 1) * 128]
                            hs_ = t.matmul(bnk[:, 0:N], kst,
                                           a_qT[hh * 64:(hh + 1) * 64, M, qlo:512],
                                           start=True, stop=True, wait=wsc + tTE, tag="bsc")
                            if r >= 0:
                                hs_ = v.tensor_tensor(out=bnk[:, 0:128], in0=bnk[:, 0:128],
                                                      in1=MLOC[:, 0:128], op=ALU.add,
                                                      wait=[hs_, h_m2], tag="bmask")
                            wex = [hs_] + ([hexp_slot.get((jt % 4, hh))] if hexp_slot.get((jt % 4, hh)) else [])
                            if hpv[hh]:
                                wex.append(hpv[hh])
                            he = s.activation(out=a_PT[:, jt % 4, hh, qlo:512], in_=bnk[:, 0:N],
                                              func=AF.Exp, scale=0.125,
                                              bias=BIAS[:, 8 + heads[hh]:9 + heads[hh]],
                                              wait=wex, tag="bexp")
                            hexp_b[(jt, hh)] = he
                            hexp_slot[(jt % 4, hh)] = he
                            scg[(jt % 2, hh)] = he
                            allexp.append(he)
                            hexp_all.append(he)
                    for jt in jts:
                        r = jt - 4 * gq
                        qlo = max(0, r) * 128 if r >= 0 else 0
                        N = 512 - qlo
                        kv = M // 4 if False else None
                        for hh in range(2):
                            head = heads[hh]
                            kvh = head // 4
                            wpv = [hexp_b[(jt, hh)], h_vone]
                            if jt == 0:
                                wpv += [hno[-1]] if hno else prev_norm + tTE
                            hpv[hh] = t.matmul(PB[pvb + hh][0:65, qlo:512],
                                               a_vE[:, jt, kvh * 65:kvh * 65 + 65],
                                               a_PT[:, jt % 4, hh, qlo:512],
                                               start=(jt == 0), stop=(jt == njt - 1),
                                               wait=wpv, tag="bpv")
                norm_q.append((M, heads, pvb, list(hpv)))
                while len(norm_q) > (1 if M < 7 else 0):
                    Mn, headsn, pvbn, hpvn = norm_q.pop(0)
                    for hh in range(2):
                        head = headsn[hh]
                        hrc = v.reciprocal(out=a_nrows[0:1, 0:512], in_=PB[pvbn + hh][64:65, 0:512],
                                           wait=[hpvn[hh]] + ([hno[-1]] if hno else []), tag="brcp")
                        hoev = v.tensor_copy(a_obn[0:64, 0:512], PB[pvbn + hh][0:64, 0:512],
                                             wait=[hpvn[hh], hrc], tag="boev")
                        hbc = t.matmul(PB[pvbn + hh][:, 0:512], onesrow, a_nrows[0:1, 0:512],
                                       start=True, stop=True, wait=[hrc, hoev], tag="brbc")
                        hmul = v.tensor_tensor(
                            out=a_oT[64 * (head % 2):64 * (head % 2) + 64, head // 2, :],
                            in0=a_obn[0:64, 0:512], in1=PB[pvbn + hh][64:128, 0:512], op=ALU.mult,
                            wait=[hbc, hoev], tag="bnorm")
                        hno.append(hmul)
            if dbg and gq == 0:
                sy.dma(dbg_o, a_oT, wait=[hno[-1]], ch="dbg")
            hwo = [None] * 8
            hres = []
            for kt in range(CT):
                sl = kt % 2
                hdw = s.dma(a_ws[sl], woTscr[kt], wait=[hwslab[sl]], ch=f"bws{sl}", tag="wold")
                for m in range(8):
                    wm = [hdw, hno[-1]]
                    if kt == 0:
                        wm += [allexp[-1], hnorm[-1], hno[-1]]
                    hwo[m] = t.matmul(PB[m][:, 0:512], a_ws[sl][:, m * 128:(m + 1) * 128],
                                      a_oT[:, kt, :], start=(kt == 0), stop=(kt == CT - 1),
                                      wait=wm, tag="bwomm")
                hwslab[sl] = hwo[7]
            for m in range(8):
                hres.append(v.tensor_tensor(out=ST[:, m * S + G0:m * S + G0 + 512],
                                            in0=ST[:, m * S + G0:m * S + G0 + 512],
                                            in1=PB[m][:, 0:512], op=ALU.add,
                                            wait=[hwo[m]], tag="bres"))
            prev_wo = [hwo[7]]
            prev_norm = list(hres) + dbg_gtok
            tTE = [hwo[7]]
            tV = list(hres)
        return tV + tTE

    def emit_resid_combine(scale, phase_wait=()):
        slot = [dict(evac=[None, None], rd=[]), dict(evac=[None, None], rd=[])]
        hout = []
        for i in range(TT):
            sl = i % 2
            stt = slot[sl]
            xt = AR[:, sl * D:(sl + 1) * D]
            hld = sy.dma(xt, hs[i * 128:(i + 1) * 128, :],
                         wait=stt["rd"] + (list(phase_wait) if i < 2 else []), ch=f"rc{sl}", tag="rcld")
            pbank = (PB[sl * 2], PB[sl * 2 + 1])
            evs = []
            for c in range(CT):
                pb = pbank[c // 4]
                po = (c % 4) * 128
                w = [hld] + (list(phase_wait) if i < 2 else [])
                if stt["evac"][c // 4] is not None:
                    w.append(stt["evac"][c // 4])
                evs.append(t.transpose(pb[:, po:po + 128], xt[:, c * 128:(c + 1) * 128],
                                       idf[:, :], wait=w, tag="trR"))
            for half in range(2):
                hv = v.scalar_tensor_tensor(
                    out=ST3(half * 4, (half + 1) * 4, i * 128, (i + 1) * 128),
                    in0=ST3(half * 4, (half + 1) * 4, i * 128, (i + 1) * 128),
                    scalar=float(scale),
                    in1=pbank[half][:, :].rearrange("p (c t) -> p c t", c=4),
                    op0=ALU.mult, op1=ALU.add,
                    wait=evs[half * 4:half * 4 + 4] + (list(phase_wait) if i < 2 else []),
                    tag="rcomb")
                stt["evac"][half] = hv
                hout.append(hv)
            stt["rd"] = evs
        return hout[-4:]

    def emit_chan_ln(phase_wait=()):
        hout = []
        prev = {}
        eps_col, h_eps = ccol(1.0e-5)
        ones_col, h_oc = ccol(1.0)
        for i in range(8):
            t0 = i * 512
            r_mu = ROWS[0:1, 130:642] if False else AR[0:1, 2048 + 0:2048 + 512]
            r_mu = AR[0:1, 2048:2560]
            r_var = AR[0:1, 2560:3072]
            r_rst = AR[0:1, 3072:3584]
            sq = AR[:, 3584 + (i % 2) * 512: 3584 + (i % 2 + 1) * 512]
            hsum = []
            hsq = []
            sqg = prev.get(("sq", i % 2))
            for c in range(CT):
                w = list(phase_wait) if i < 2 else []
                if c == 0 and ("b6" in prev):
                    w.append(prev["b6"])
                hsum.append(t.matmul(PB[6][0:1, 0:512], ones_col, ST[:, c * S + t0:c * S + t0 + 512],
                                     start=(c == 0), stop=(c == CT - 1),
                                     wait=w + [h_oc], tag="clnsum"))
                wq = w + ([sqg] if sqg and c < 2 else [])
                hq = s.activation(out=sq, in_=ST[:, c * S + t0:c * S + t0 + 512], func=AF.Square,
                                  wait=wq + ([hsq[-1]] if hsq else []), tag="clnsq")
                w7 = [hq] + ([prev["b7"]] if c == 0 and "b7" in prev else [])
                hsq.append(t.matmul(PB[7][0:1, 0:512], ones_col, sq,
                                    start=(c == 0), stop=(c == CT - 1), wait=w7 + [h_oc], tag="clnssq"))
            hmu = v.tensor_scalar(out=r_mu, in0=PB[6][0:1, 0:512], scalar1=1.0 / D, scalar2=None,
                                  op0=ALU.mult, wait=[hsum[-1]] + ([prev["mu"]] if "mu" in prev else []),
                                  tag="cmu")
            hv1 = v.tensor_tensor(out=r_var, in0=r_mu, in1=r_mu, op=ALU.mult,
                                  wait=[hmu] + ([prev["var"]] if "var" in prev else []), tag="cmu2")
            hv2 = v.scalar_tensor_tensor(out=r_var, in0=PB[7][0:1, 0:512], scalar=1.0 / D,
                                         in1=r_var, op0=ALU.mult, op1=ALU.subtract,
                                         wait=[hsq[-1], hv1], tag="cvar")
            hr1 = s.activation(out=r_rst, in_=r_var, func=AF.Ln, bias=eps_col[0:1, :],
                               wait=[hv2, h_eps] + ([prev["rst"]] if "rst" in prev else []), tag="clnr")
            hr2 = s.activation(out=r_rst, in_=r_rst, func=AF.Exp, scale=-0.5, wait=[hr1], tag="crstd")
            hb6 = t.matmul(PB[6][:, 0:512], onesrow, r_mu, start=True, stop=True,
                           wait=[hmu, hr2, h_onesrow], tag="cbcmu")
            hb7 = t.matmul(PB[7][:, 0:512], onesrow, r_rst, start=True, stop=True,
                           wait=[hr2, h_onesrow], tag="cbcr")
            happ = []
            for c in range(CT):
                sl2 = c % 2
                scr2 = AR[:, 4608 + sl2 * 512:4608 + (sl2 + 1) * 512]
                wa = [hb6] + ([happ[c - 2][0]] if c >= 2 else [])
                ha1 = v.tensor_tensor(out=scr2, in0=ST[:, c * S + t0:c * S + t0 + 512],
                                      in1=PB[6][:, 0:512], op=ALU.subtract, wait=wa, tag="capsub")
                ha2 = v.tensor_tensor(out=ST[:, c * S + t0:c * S + t0 + 512], in0=scr2,
                                      in1=PB[7][:, 0:512], op=ALU.mult, wait=[ha1, hb7, hsq[-1], hsum[-1]],
                                      tag="capmul")
                happ.append((ha1, ha2))
            prev = {"b6": happ[-1][1], "b7": happ[-1][1], "mu": hb6, "var": hv2, "rst": hb7,
                    ("sq", i % 2): hsq[-1]}
            hout = [h[1] for h in happ]
        return hout

    def emit_spill(phase_wait=()):
        hs_ = []
        for c in range(CT):
            hs_.append(sy.dma(xscr[c * 128:(c + 1) * 128, :], ST[:, c * S:(c + 1) * S],
                              wait=list(phase_wait), ch="spill", tag="spill"))
        return [ChanTotal("sync", "spill")]

    def emit_final_combine(phase_wait=()):
        hout = []
        hprev = [[], []]
        for i in range(8):
            sl = i % 2
            xb = AR[:, sl * 4096:(sl + 1) * 4096]
            hld = sy.dma(xb.rearrange("p (c t) -> p c t", c=CT),
                         xscr[:, i * 512:(i + 1) * 512].rearrange("(c p) t -> p c t", p=128),
                         wait=list(phase_wait) + hprev[sl], ch=f"fc{sl}", tag="fcld")
            h = v.scalar_tensor_tensor(out=ST3(0, CT, i * 512, (i + 1) * 512),
                                       in0=ST3(0, CT, i * 512, (i + 1) * 512), scalar=0.5,
                                       in1=xb.rearrange("p (c t) -> p c t", c=CT),
                                       op0=ALU.mult, op1=ALU.add,
                                       wait=[hld] + list(phase_wait), tag="fcomb")
            hprev[sl] = [h]
            hout.append(h)
        return hout[-2:]

    if stage in ("ffn1", "x1", "y1", "x2", "full"):
        for l in (0, 1):
            emit_prestage_attn(l)
            emit_prestage_mlp(l)
        htb = emit_tables()
    hz0 = emit_ln_load_transpose()
    if stage == "z0":
        fin = emit_store_stream(extra_wait=hz0)
        Sd.sync.raw(lambda eng: eng.nop(), wait=fin, tag="final")
        Sd.finalize()
        es.close()
        return nc

    if stage == "devA":
        htb = emit_tables()
        hp = emit_prestage_attn(0)
        ha = emit_ffn_attn(0, phase_wait=hz0 + [ChanTotal("gpsimd", "pa0")] + htb, dbg=True)
        fin = emit_store_stream(extra_wait=ha)
        Sd.sync.raw(lambda eng: eng.nop(), wait=fin, tag="final")
        Sd.finalize()
        es.close()
        return nc

    if stage in ("ffn1", "x1", "y1", "x2", "full"):
        hpb = emit_prestage_big(phase_wait=[]) if stage in ("x2", "full") else []
        hph = hz0 + htb + hpb
        _os = os
        NLAY = int(_os.environ.get("KNL", "2"))
        for l in range(NLAY):
            hph = emit_ffn_attn(l, phase_wait=hph)
            if not _os.environ.get("KSKIPB"):
                hph = emit_mlp(l, phase_wait=hph, dbg=(l == 0 and bool(_os.environ.get("KDBGB"))))
                hph = hph + ([ChanTotal("sync", "dbg")] if l == 0 and _os.environ.get("KDBGB") else [])
        if stage == "x2":
            hph = emit_resid_combine(0.5, phase_wait=hph)
            hph = emit_bigattn(phase_wait=hph + hpb, dbg=bool(_os.environ.get("KDBG")))
            hph = hph + ([ChanTotal("sync", "dbg")] if _os.environ.get("KDBG") else [])
            fin = emit_store_stream(extra_wait=hph)
            Sd.sync.raw(lambda eng: eng.nop(), wait=fin, tag="final")
            Sd.finalize()
            es.close()
            return nc
        if stage in ("x1", "y1"):
            hph = emit_resid_combine(0.5, phase_wait=hph)
            if stage == "y1":
                hph = emit_chan_ln(phase_wait=hph)
            fin = emit_store_stream(extra_wait=hph)
            Sd.sync.raw(lambda eng: eng.nop(), wait=fin, tag="final")
            Sd.finalize()
            es.close()
            return nc
        if stage == "ffn1":
            fin = emit_store_stream(extra_wait=hph)
            Sd.sync.raw(lambda eng: eng.nop(), wait=fin, tag="final")
            Sd.finalize()
            es.close()
            return nc
        if stage == "full":
            hph = emit_resid_combine(0.5, phase_wait=hph)
            hph = emit_bigattn(phase_wait=hph + hpb)
            hsp = emit_spill(phase_wait=hph)
            hph = emit_chan_ln(phase_wait=hph + hsp)
            for l in (0, 1):
                hph = emit_ffn_attn(l, phase_wait=hph)
                hph = emit_mlp(l, phase_wait=hph)
            hph = emit_final_combine(phase_wait=hph + hsp)
            fin = emit_store_stream(extra_wait=hph)
            Sd.sync.raw(lambda eng: eng.nop(), wait=fin, tag="final")
            Sd.finalize()
            es.close()
            return nc

    if stage == "devB":
        hp = emit_prestage_mlp(0)
        hm = emit_mlp(0, phase_wait=hz0 + [ChanTotal("gpsimd", "pm0")])
        fin = emit_store_stream(extra_wait=hm)
        Sd.sync.raw(lambda eng: eng.nop(), wait=fin, tag="final")
        Sd.finalize()
        es.close()
        return nc

    raise NotImplementedError(stage)


def kernel(**inputs):
    stage = inputs.pop("_stage", "full")
    nc = build(stage)
    hsl = np.ascontiguousarray(inputs["L0_attn_hidden_states"])
    names = ["L0_attn_Wq", "L0_attn_Wk", "L0_attn_Wv", "L0_attn_Wo",
             "L0_ffn_wqkv", "L0_ffn_q_scale", "L0_ffn_k_scale", "L0_ffn_wo",
             "L0_ffn_ln_g", "L0_ffn_ln_b", "L0_ffn_w1", "L0_ffn_b1",
             "L0_ffn_w2", "L0_ffn_b2"]
    base = {n: np.ascontiguousarray(inputs[n]) for n in names}
    in_maps = [dict(base, hs=hsl[i]) for i in range(8)]
    res = run_bass_kernel_spmd(nc, in_maps, core_ids=list(range(8)))
    return np.stack([r["out"] for r in res.results], axis=0)


# version 16
# speedup vs baseline: 1.1620x; 1.1382x over previous
import contextlib
import os
import numpy as np
import concourse.bass as bass
import concourse.mybir as mybir
from concourse.bass_utils import run_bass_kernel_spmd

F32 = mybir.dt.float32
BF16 = mybir.dt.bfloat16
I32 = mybir.dt.int32
AF = mybir.ActivationFunctionType
ALU = mybir.AluOpType
AX = mybir.AxisListType

ENGINES = ("tensor", "vector", "scalar", "gpsimd", "sync")


class ChanTotal:
    __slots__ = ("eng", "ch")

    def __init__(self, eng, ch):
        self.eng, self.ch = eng, ch


class Handle:
    __slots__ = ("eng", "idx")

    def __init__(self, eng, idx):
        self.eng, self.idx = eng, idx


class _Op:
    __slots__ = ("emit", "waits", "sig", "ch", "tag")

    def __init__(self, emit, waits, ch, tag):
        self.emit = emit
        self.waits = list(waits)
        self.sig = None
        self.ch = ch
        self.tag = tag


class _EngineQ:
    def __init__(self, name):
        self.name = name
        self.ops = []

    def op(self, method, *args, wait=(), tag=None, **kwargs):
        def emit(eng):
            return getattr(eng, method)(*args, **kwargs)
        self.ops.append(_Op(emit, wait, None, tag or method))
        return Handle(self.name, len(self.ops) - 1)

    def raw(self, fn, wait=(), tag="raw"):
        self.ops.append(_Op(fn, wait, None, tag))
        return Handle(self.name, len(self.ops) - 1)

    def dma(self, out, in_, wait=(), ch="m", tag="dma", **kw):
        def emit(eng):
            return eng.dma_start(out=out, in_=in_, **kw)
        o = _Op(emit, wait, ch, tag)
        o.sig = True
        self.ops.append(o)
        return Handle(self.name, len(self.ops) - 1)

    def __getattr__(self, method):
        def f(*args, wait=(), tag=None, **kwargs):
            return self.op(method, *args, wait=wait, tag=tag, **kwargs)
        return f


class Sched:
    def __init__(self, nc):
        self.nc = nc
        self.q = {e: _EngineQ(e) for e in ENGINES}
        self.tensor, self.vector, self.scalar, self.gpsimd, self.sync = (
            self.q[e] for e in ENGINES)
        self._stack = contextlib.ExitStack()
        self._sems = {}

    def _sem(self, key):
        if key not in self._sems:
            self._sems[key] = self._stack.enter_context(
                self.nc.semaphore("s_" + "_".join(str(k) for k in key)))
        return self._sems[key]

    def finalize(self):
        for e in ENGINES:
            for o in self.q[e].ops:
                for h in o.waits:
                    if isinstance(h, Handle):
                        self.q[h.eng].ops[h.idx].sig = True
        for e in ENGINES:
            csig, dsig = 0, {}
            for o in self.q[e].ops:
                if o.sig:
                    if o.ch is not None:
                        dsig[o.ch] = dsig.get(o.ch, 0) + 16
                        o.sig = dsig[o.ch]
                    else:
                        csig += 1
                        o.sig = csig
        totals = {}
        for e in ENGINES:
            for o in self.q[e].ops:
                if o.ch is not None and isinstance(o.sig, int):
                    totals[(e, o.ch)] = max(totals.get((e, o.ch), 0), o.sig)

        def rtok(h):
            if isinstance(h, Handle):
                o = self.q[h.eng].ops[h.idx]
                if o.ch is not None:
                    return ((h.eng, "d", o.ch), o.sig)
                return ((h.eng, "c"), o.sig)
            if isinstance(h, ChanTotal):
                return ((h.eng, "d", h.ch), totals.get((h.eng, h.ch), 0))
            return h
        for e in ENGINES:
            for o in self.q[e].ops:
                o.waits = [rtok(h) for h in o.waits]
        with self.nc.Block() as block:
            for e in ENGINES:
                ops = self.q[e].ops
                if not ops:
                    continue

                def body(eng, _ops=ops, _e=e):
                    water = {}
                    for o in _ops:
                        for (sk, sv) in o.waits:
                            if water.get(sk, 0) >= sv:
                                continue
                            water[sk] = sv
                            eng.wait_ge(self._sem(sk), sv)
                        ins = o.emit(eng)
                        if o.sig is not None:
                            if o.ch is not None:
                                ins.then_inc(self._sem((_e, "d", o.ch)), 16)
                            else:
                                if ins is None:
                                    raise RuntimeError(f"op {o.tag} on {_e} returned None")
                                ins.then_inc(self._sem((_e, "c")), 1)
                getattr(block, e)(body)
        self._stack.close()


S = 4096
D = 1024
CT = 8
TT = 32
NH = 8
DH = 64
W = 512
GD = 2730
NQH, NKV = 16, 4
BIGNEG = -1e9


def build(stage="full"):
    nc = bass.Bass("TRN2", target_bir_lowering=False, debug=False)

    hs = nc.dram_tensor("hs", [S, D], F32, kind="ExternalInput").ap()
    wq = nc.dram_tensor("L0_attn_Wq", [1024, 1024], F32, kind="ExternalInput").ap()
    wk = nc.dram_tensor("L0_attn_Wk", [256, 1024], F32, kind="ExternalInput").ap()
    wv = nc.dram_tensor("L0_attn_Wv", [256, 1024], F32, kind="ExternalInput").ap()
    wo_big = nc.dram_tensor("L0_attn_Wo", [1024, 1024], F32, kind="ExternalInput").ap()
    wqkv = nc.dram_tensor("L0_ffn_wqkv", [2, 1024, 1536], F32, kind="ExternalInput").ap()
    qs_in = nc.dram_tensor("L0_ffn_q_scale", [2, 64], F32, kind="ExternalInput").ap()
    ks_in = nc.dram_tensor("L0_ffn_k_scale", [2, 64], F32, kind="ExternalInput").ap()
    wo_ffn = nc.dram_tensor("L0_ffn_wo", [2, 512, 1024], F32, kind="ExternalInput").ap()
    lng = nc.dram_tensor("L0_ffn_ln_g", [2, 1024], F32, kind="ExternalInput").ap()
    lnb = nc.dram_tensor("L0_ffn_ln_b", [2, 1024], F32, kind="ExternalInput").ap()
    w1 = nc.dram_tensor("L0_ffn_w1", [2, 1024, 5460], F32, kind="ExternalInput").ap()
    b1 = nc.dram_tensor("L0_ffn_b1", [2, 5460], F32, kind="ExternalInput").ap()
    w2 = nc.dram_tensor("L0_ffn_w2", [2, 2730, 1024], F32, kind="ExternalInput").ap()
    b2 = nc.dram_tensor("L0_ffn_b2", [2, 1024], F32, kind="ExternalInput").ap()
    out = nc.dram_tensor("out", [S, D], F32, kind="ExternalOutput").ap()
    if os.environ.get("KDBG") or os.environ.get("KDBGB"):
        dbg_q = nc.dram_tensor("dbg_q", [128, 4, 1024], BF16, kind="ExternalOutput").ap()
        dbg_k = nc.dram_tensor("dbg_k", [128, 4, 1536], BF16, kind="ExternalOutput").ap()
        dbg_v = nc.dram_tensor("dbg_v", [128, 12, 528], BF16, kind="ExternalOutput").ap()
        dbg_o = nc.dram_tensor("dbg_o", [128, 8, 512], BF16, kind="ExternalOutput").ap()
        dbg_t = nc.dram_tensor("dbg_t", [4, S, DH], F32, kind="ExternalOutput").ap()
    xscr = nc.dram_tensor("xscr", [CT * 128, S], F32, kind="Internal").ap()
    tabscr = nc.dram_tensor("tabscr", [2, 4, S, DH], F32, kind="Internal").ap()

    es = contextlib.ExitStack()

    def sb(name, shape, dt=F32):
        return es.enter_context(nc.sbuf_tensor(name, shape, dt))

    ST = sb("ST", [128, CT * S])
    idf = sb("idf", [128, 128], F32)
    idb = sb("idb", [128, 128], BF16)
    CONST = sb("CONST", [128, 512], F32)
    ROWS = sb("ROWS", [1, 704], F32)
    lngt = sb("lngt", [128, 16], F32)
    lnbt = sb("lnbt", [128, 16], F32)
    b1at = sb("b1at", [128, 2 * 22], F32)
    b1gt = sb("b1gt", [128, 2 * 22], F32)
    b2t = sb("b2t", [128, 16], F32)
    MLOC = sb("MLOC", [128, 256], F32)
    MBIG = sb("MBIG", [128, 512], F32)
    STASH = sb("STASH", [128, CT * 4], F32)
    BIAS = sb("BIAS", [128, 64], F32)

    ARENA_BYTES = 68608
    AR = sb("AR", [128, ARENA_BYTES // 4], F32)
    TOKST = AR[:, 0:4 * D]

    class Arena:
        def __init__(self, ap):
            self.ap = ap
            self.off = 0

        def take(self, n_elems_f32, align=16):
            o = (self.off + align - 1) // align * align
            self.off = o + n_elems_f32
            assert self.off <= ARENA_BYTES // 4, f"arena overflow {self.off * 4}B"
            return o

        def f32(self, n):
            o = self.take(n)
            return self.ap[:, o:o + n]

        def bf16(self, n):
            o = self.take((n + 1) // 2)
            return self.ap[:, o:o + (n + 1) // 2].bitcast(BF16)

        def reset(self):
            self.off = 0

    PB = [es.enter_context(nc.psum_tensor(f"pb{i}", [128, 512], F32)) for i in range(8)]

    Sd = Sched(nc)
    t, v, s, g, sy = Sd.tensor, Sd.vector, Sd.scalar, Sd.gpsimd, Sd.sync

    _cc = {}

    def ccol(val):
        if val not in _cc:
            i = len(_cc)
            assert i < 16
            ap = CONST[:, i:i + 1]
            h = g.memset(ap, float(val), tag="ccol")
            _cc[val] = (ap, h)
        return _cc[val]

    inits = []

    h_i1 = g.iota(idf[:, :], pattern=[[1, 128]], base=0, channel_multiplier=-1,
                  allow_small_or_imprecise_dtypes=True)
    h_i3 = v.tensor_scalar(out=idf[:, :], in0=idf[:, :], scalar1=0.0, scalar2=None,
                           op0=ALU.is_equal, wait=[h_i1])
    h_i4 = v.tensor_copy(idb[:, :], idf[:, :], wait=[h_i3])
    inits += [h_i4]
    piota = CONST[:, 50:51]
    h_pio = g.iota(piota, pattern=[[1, 1]], base=0, channel_multiplier=1,
                   allow_small_or_imprecise_dtypes=True, tag="piota")
    r_half_i = AR[0:1, 8192 + 192:8192 + 224]
    h_rhi = g.iota(r_half_i, pattern=[[2, 32]], base=0, channel_multiplier=0,
                   allow_small_or_imprecise_dtypes=True, tag="rhalf")
    for _cv in (-0.5, float(np.pi / 2), float(np.log(1.0 / (2 * np.pi))),
                25.6 / 89.6, -8.0, 1.0, 1e-5):
        ccol(_cv)

    h_m1 = g.iota(MLOC[:, 0:128], pattern=[[1, 128]], base=0, channel_multiplier=-1,
                  allow_small_or_imprecise_dtypes=True)
    h_m2 = v.tensor_scalar(out=MLOC[:, 0:128], in0=MLOC[:, 0:128], scalar1=0.0, scalar2=float(BIGNEG),
                           op0=ALU.is_lt, op1=ALU.mult, wait=[h_m1])
    h_m3 = g.iota(MLOC[:, 128:256], pattern=[[1, 128]], base=0, channel_multiplier=-1,
                  allow_small_or_imprecise_dtypes=True, wait=[h_m1])
    h_m4 = v.tensor_scalar(out=MLOC[:, 128:256], in0=MLOC[:, 128:256], scalar1=0.0, scalar2=float(BIGNEG),
                           op0=ALU.is_gt, op1=ALU.mult, wait=[h_m3])
    h_m5 = g.iota(MBIG[:, :], pattern=[[1, 512]], base=0, channel_multiplier=-1,
                  allow_small_or_imprecise_dtypes=True, wait=[h_m3])
    h_m6 = v.tensor_scalar(out=MBIG[:, :], in0=MBIG[:, :], scalar1=0.0, scalar2=float(BIGNEG),
                           op0=ALU.is_lt, op1=ALU.mult, wait=[h_m5])
    inits += [h_m2, h_m4, h_m6]

    for l in (0, 1):
        for c in range(8):
            inits.append(sy.dma(lngt[:, l * 8 + c:l * 8 + c + 1],
                                lng[l, c * 128:(c + 1) * 128].unsqueeze(1), ch="prm"))
            inits.append(sy.dma(lnbt[:, l * 8 + c:l * 8 + c + 1],
                                lnb[l, c * 128:(c + 1) * 128].unsqueeze(1), ch="prm"))
            inits.append(sy.dma(b2t[:, l * 8 + c:l * 8 + c + 1],
                                b2[l, c * 128:(c + 1) * 128].unsqueeze(1), ch="prm"))
        for j in range(22):
            cw = 128 if j < 21 else 42
            inits.append(sy.dma(b1at[0:cw, l * 22 + j:l * 22 + j + 1],
                                b1[l, 128 * j:128 * j + cw].unsqueeze(1), ch="prm"))
            inits.append(sy.dma(b1gt[0:cw, l * 22 + j:l * 22 + j + 1],
                                b1[l, 2730 + 128 * j:2730 + 128 * j + cw].unsqueeze(1), ch="prm"))

    def ST3(c0, c1, t0, t1):
        return ST[:, :].rearrange("p (c t) -> p c t", c=CT)[:, c0:c1, t0:t1]

    def emit_ln_load_transpose():
        slot = [dict(readers=[], evac=[None, None]) for _ in range(4)]
        bank_guard = [dict(evac=[None, None]), dict(evac=[None, None])]
        lastv = []
        for i in range(TT):
            sl = i % 4
            st = slot[sl]
            xt = TOKST[:, sl * D:(sl + 1) * D]
            hld = sy.dma(xt, hs[i * 128:(i + 1) * 128, :], wait=st["readers"], ch=f"in{sl}")
            cb = 64 + sl * 8
            mu = CONST[:, cb:cb + 1]
            ssq = CONST[:, cb + 1:cb + 2]
            var = CONST[:, cb + 2:cb + 3]
            rst = CONST[:, cb + 3:cb + 4]
            nmr = CONST[:, cb + 4:cb + 5]
            sqt = AR[:, 4 * D + (sl % 2) * 1024: 4 * D + (sl % 2 + 1) * 1024]
            pr = st.get("stats", [])
            h1 = v.reduce_sum(out=mu, in_=xt, axis=AX.X, wait=[hld] + pr)
            h2 = s.activation(out=sqt, in_=xt, func=AF.Square, wait=[hld] + pr)
            h3 = v.reduce_sum(out=ssq, in_=sqt, axis=AX.X, wait=[h2])
            h4 = v.tensor_scalar(out=mu, in0=mu, scalar1=1.0 / D, scalar2=None,
                                 op0=ALU.mult, wait=[h1])
            h5 = v.tensor_scalar(out=var, in0=mu, scalar1=mu, scalar2=None,
                                 op0=ALU.mult, wait=[h4])
            h6 = v.tensor_scalar(out=var, in0=ssq, scalar1=1.0 / D, scalar2=var,
                                 op0=ALU.mult, op1=ALU.subtract, wait=[h3, h5])
            epsc, heps = ccol(1e-5)
            h7 = s.activation(out=rst, in_=var, func=AF.Ln, bias=epsc, wait=[h6, heps])
            h8 = s.activation(out=rst, in_=rst, func=AF.Exp, scale=-0.5, wait=[h7])
            h9 = v.tensor_scalar(out=nmr, in0=mu, scalar1=rst, scalar2=-1.0,
                                 op0=ALU.mult, op1=ALU.mult, wait=[h8])
            h10 = s.activation(out=xt, in_=xt, func=AF.Identity, scale=rst, bias=nmr,
                               wait=[h9, h3])
            st["stats"] = [h9, h10]
            pbank = (PB[(sl % 2) * 2], PB[(sl % 2) * 2 + 1])
            bg = bank_guard[sl % 2]
            evs = []
            for c in range(CT):
                pb = pbank[c // 4]
                po = (c % 4) * 128
                w = [h10]
                if bg["evac"][c // 4] is not None:
                    w.append(bg["evac"][c // 4])
                evs.append(t.transpose(pb[:, po:po + 128], xt[:, c * 128:(c + 1) * 128],
                                       idf[:, :], wait=w, tag="trT"))
            hv = []
            for half in range(2):
                hv.append(v.tensor_copy(
                    ST3(half * 4, (half + 1) * 4, i * 128, (i + 1) * 128),
                    pbank[half][:, :].rearrange("p (c t) -> p c t", c=4),
                    wait=evs[half * 4:half * 4 + 4], tag="evT"))
                bg["evac"][half] = hv[half]
            st["readers"] = [h10] + evs
            lastv = hv
        return lastv

    def emit_store_stream(extra_wait=(), combine=None):
        slot = [dict(readers=[], evac=[None, None]), dict(readers=[], evac=[None, None])]
        for i in range(TT):
            sl = i % 2
            st = slot[sl]
            xt = TOKST[:, sl * D:(sl + 1) * D]
            pbank = (PB[4 + sl * 2], PB[5 + sl * 2])
            evs = []
            for c in range(CT):
                pb = pbank[c // 4]
                po = (c % 4) * 128
                w = list(extra_wait) if i < 2 else []
                if st["evac"][c // 4] is not None:
                    w.append(st["evac"][c // 4])
                evs.append(t.transpose(pb[:, po:po + 128],
                                       ST[:, c * S + i * 128: c * S + (i + 1) * 128],
                                       idf[:, :], wait=w, tag="trS"))
            hst = []
            for half in range(2):
                hst.append(v.tensor_copy(
                    xt.rearrange("p (c q) -> p c q", c=CT)[:, half * 4:(half + 1) * 4, :],
                    pbank[half][:, :].rearrange("p (c t) -> p c t", c=4),
                    wait=evs[half * 4:half * 4 + 4] + st["readers"], tag="evS"))
                st["evac"][half] = hst[half]
            if combine is not None:
                hst = combine(i, sl, xt, hst)
            hd = sy.dma(out[i * 128:(i + 1) * 128, :], xt, wait=hst, ch=f"out{sl}")
            st["readers"] = [hd]
        return [ChanTotal("sync", "out0"), ChanTotal("sync", "out1")]

    onesrow = CONST[0:1, 128:256]
    h_onesrow = g.memset(onesrow, 1.0, tag="onesrow")
    PRM = ChanTotal("sync", "prm")

    w1ascr = nc.dram_tensor("w1ascr", [2, 22, 8, 128, 128], BF16, kind="Internal").ap()
    w1gscr = nc.dram_tensor("w1gscr", [2, 22, 8, 128, 128], BF16, kind="Internal").ap()
    w2scr = nc.dram_tensor("w2scr", [2, GD, 1024], BF16, kind="Internal").ap()

    def emit_prestage_mlp(l):
        hs_ = []
        for (scr, c0) in ((w1ascr, 0), (w1gscr, GD)):
            for j in range(22):
                cw = 128 if j < 21 else 42
                srcj = w1[l, :, c0 + 128 * j:c0 + 128 * j + cw].rearrange(
                    "(kt p) c -> kt p c", p=128)
                hs_.append(g.dma(scr[l, j, :, :, 0:cw], srcj, ch=f"pm{l}", tag="presw1"))
        hs_.append(g.dma(w2scr[l], w2[l], ch=f"pm{l}", tag="presw2"))
        return hs_

    def emit_mlp(l, phase_wait=(), dbg=False):
        TB = 512
        NB = S // TB
        NCK = 22
        a_h2lo = AR[:, 0:2048].rearrange("p (c t) -> p c t", c=4)
        a_y2 = AR[:, 2048:4096].bitcast(BF16).rearrange("p (c t) -> p c t", c=8)
        a_hg = AR[:, 4096:9728].bitcast(BF16).rearrange("p (c t) -> p c t", c=22)
        a_sq = [AR[:, 9728 + i * 512: 9728 + (i + 1) * 512] for i in range(2)]
        a_gg = [AR[:, 10752 + i * 512: 10752 + (i + 1) * 512] for i in range(2)]
        a_w1 = [AR[:, 11776 + i * 1024: 11776 + (i + 1) * 1024].bitcast(BF16)
                .rearrange("p (kt c) -> p kt c", kt=8) for i in range(2)]
        a_w2 = [AR[:, 13824 + i * 512: 13824 + (i + 1) * 512].bitcast(BF16) for i in range(2)]
        r_mu = AR[0:1, 14848:15360]
        r_var = AR[0:1, 15360:15872]
        r_rst = AR[0:1, 15872:16384]

        ones_col, h_oc = ccol(1.0)
        eps_col, h_eps = ccol(1e-5)
        h_or = h_onesrow
        PRES = ChanTotal("gpsimd", f"pm{l}")
        PRESG = PRES

        h_stash = v.tensor_copy(
            STASH[:, 0:28].rearrange("p (c b) -> p c b", c=4),
            ST3(4, 8, 0, S)[:, :, 511:S - 1:512],
            wait=list(phase_wait), tag="stashB")

        hw1 = [None, None]
        hw2 = [None, None]
        ph2_res = [None] * 8
        ph1_free = [None] * 4
        last_apply_readers = []
        dbgtok = []
        h_h2lo_prev = []
        for B in range(NB):
            t0 = B * TB
            wz = [h_stash] + h_h2lo_prev
            if B == 0:
                h1 = v.memset(a_h2lo[:, :, 0:1], 0.0, wait=wz, tag="h2lo0")
                h2 = v.tensor_copy(a_h2lo[:, :, 1:TB], ST3(4, 8, 0, TB - 1), wait=wz, tag="h2lo")
                h_h2lo = [h1, h2]
            else:
                h2 = v.tensor_copy(a_h2lo[:, :, 1:TB], ST3(4, 8, t0, t0 + TB - 1), wait=wz, tag="h2lo")
                h1 = v.tensor_copy(a_h2lo[:, :, 0:1],
                                   STASH[:, 0:28].rearrange("p (c b) -> p c b", c=4)[:, :, B - 1:B],
                                   wait=wz, tag="h2lob")
                h_h2lo = [h1, h2]

            def h2tile(c):
                if c < 4:
                    return ST[:, c * S + t0: c * S + t0 + TB]
                return a_h2lo[:, c - 4, :]

            hsum = []
            hsq = []
            sq_rot = [None, None]
            for c in range(CT):
                w = h_h2lo if c >= 4 else list(phase_wait)
                hsum.append(t.matmul(PB[6][0:1, 0:TB], ones_col, h2tile(c),
                                     start=(c == 0), stop=(c == CT - 1),
                                     wait=w + [h_oc] + ([ph2_res[6]] if c == 0 and ph2_res[6] else []),
                                     tag="lnsum"))
                wsq = w + ([sq_rot[c % 2]] if sq_rot[c % 2] else [])
                hq = s.activation(out=a_sq[c % 2], in_=h2tile(c), func=AF.Square, wait=wsq, tag="lnsq")
                hm = t.matmul(PB[7][0:1, 0:TB], ones_col, a_sq[c % 2],
                              start=(c == 0), stop=(c == CT - 1),
                              wait=[hq, h_oc] + ([ph2_res[7]] if c == 0 and ph2_res[7] else []),
                              tag="lnssq")
                sq_rot[c % 2] = hm
                hsq.append(hm)
            hmu = v.tensor_scalar(out=r_mu, in0=PB[6][0:1, 0:TB], scalar1=1.0 / D, scalar2=None,
                                  op0=ALU.mult, wait=[hsum[-1]], tag="mu")
            hv1 = v.tensor_tensor(out=r_var, in0=r_mu, in1=r_mu, op=ALU.mult, wait=[hmu], tag="mu2")
            hv2 = v.scalar_tensor_tensor(out=r_var, in0=PB[7][0:1, 0:TB], scalar=1.0 / D,
                                         in1=r_var, op0=ALU.mult, op1=ALU.subtract,
                                         wait=[hsq[-1], hv1], tag="var")
            hr1 = s.activation(out=r_rst, in_=r_var, func=AF.Ln, bias=eps_col[0:1, :],
                               wait=[hv2, h_eps], tag="lnr")
            hr2 = s.activation(out=r_rst, in_=r_rst, func=AF.Exp, scale=-0.5, wait=[hr1], tag="rstd")
            hbc1 = t.matmul(PB[6][:, 0:TB], onesrow, r_mu, start=True, stop=True,
                            wait=[hmu, hr2, h_or, hv2], tag="bcmu")
            hbc2 = t.matmul(PB[7][:, 0:TB], onesrow, r_rst, start=True, stop=True,
                            wait=[hr2, h_or], tag="bcr")
            happly = []
            ap_rot = [None, None]
            for c in range(CT):
                w0 = (h_h2lo if c >= 4 else []) + [hbc1] + dbgtok + ([ap_rot[c % 2]] if ap_rot[c % 2] else [])
                ht1 = v.tensor_tensor(out=a_sq[c % 2], in0=h2tile(c), in1=PB[6][:, 0:TB],
                                      op=ALU.subtract, wait=w0, tag="apsub")
                ht2 = v.tensor_tensor(out=a_sq[c % 2], in0=a_sq[c % 2], in1=PB[7][:, 0:TB],
                                      op=ALU.mult, wait=[ht1, hbc2], tag="apmul")
                ht3 = s.activation(out=a_y2[:, c, :], in_=a_sq[c % 2], func=AF.Identity,
                                   scale=lngt[:, l * 8 + c:l * 8 + c + 1],
                                   bias=lnbt[:, l * 8 + c:l * 8 + c + 1],
                                   wait=[ht2, PRM], tag="apaff")
                ap_rot[c % 2] = ht3
                happly.append(ht3)

            if dbg and B == 0:
                sy.dma(dbg_o, a_y2, wait=happly, ch="dbg")
                dbgtok.append(ChanTotal("sync", "dbg"))
            hstt_all = []
            gg_rot = [None, None]
            for j in range(NCK):
                cw = 128 if j < 21 else 42
                sl = j % 2
                wd = [hw1[sl]] if hw1[sl] else list(phase_wait)
                hda = sy.dma(a_w1[sl][:, :, 0:cw], w1ascr[l, j, :, :, 0:cw].rearrange("kt p c -> p kt c"),
                             wait=wd + [PRES], ch=f"w1{sl}", tag="w1a")
                hdg = sy.dma(a_w1[sl][:, :, 128:128 + cw], w1gscr[l, j, :, :, 0:cw].rearrange("kt p c -> p kt c"),
                             wait=wd + [PRESG], ch=f"w1g{sl}", tag="w1g")
                hw1[sl] = None
                pa, pg = PB[0 + 2 * (j % 2)], PB[1 + 2 * (j % 2)]
                hmma = []
                hmmg = []
                for kt in range(CT):
                    wm = [hda] + happly + ([ph1_free[0 + 2 * (j % 2)]] if kt == 0 and ph1_free[0 + 2 * (j % 2)] else [])
                    hmma.append(t.matmul(pa[0:cw, 0:TB], a_w1[sl][:, kt, 0:cw], a_y2[:, kt, :],
                                         start=(kt == 0), stop=(kt == CT - 1), wait=wm, tag="w1amm"))
                    wg = [hdg] + happly + ([ph1_free[1 + 2 * (j % 2)]] if kt == 0 and ph1_free[1 + 2 * (j % 2)] else [])
                    hmmg.append(t.matmul(pg[0:cw, 0:TB], a_w1[sl][:, kt, 128:128 + cw], a_y2[:, kt, :],
                                         start=(kt == 0), stop=(kt == CT - 1), wait=wg, tag="w1gmm"))
                wgg = [hmmg[-1]] + ([gg_rot[sl]] if gg_rot[sl] else [])
                hgel = s.activation(out=a_gg[sl].unsqueeze(0)[0][0:cw, :], in_=pg[0:cw, 0:TB],
                                    func=AF.Gelu, bias=b1gt[0:cw, l * 22 + j:l * 22 + j + 1],
                                    wait=wgg, tag="gelu")
                hstt = v.scalar_tensor_tensor(
                    out=a_hg[0:cw, j, :],
                    in0=pa[0:cw, 0:TB],
                    scalar=b1at[0:cw, l * 22 + j:l * 22 + j + 1],
                    in1=a_gg[sl].unsqueeze(0)[0][0:cw, :],
                    op0=ALU.add, op1=ALU.mult, wait=[hgel, hmma[-1], PRM], tag="hg")
                gg_rot[sl] = hstt
                ph1_free[0 + 2 * (j % 2)] = hstt
                ph1_free[1 + 2 * (j % 2)] = hstt
                hw1[sl] = hmmg[-1]
                hstt_all.append(hstt)

            hmm2 = [[None] * 8 for _ in range(NCK)]
            for kt in range(NCK):
                kw = 128 if kt < 21 else 42
                sl = kt % 2
                wd = [hw2[sl]] if hw2[sl] else list(phase_wait)
                hd2 = s.dma(a_w2[sl][0:kw, :], w2scr[l, kt * 128:kt * 128 + kw],
                            wait=wd + [PRESG], ch=f"w2{sl}", tag="w2")
                for m in range(8):
                    wm = [hd2, hstt_all[kt]]
                    if kt == 0:
                        wm += [ph1_free[m]] if m < 4 and ph1_free[m] else []
                        wm += [ph2_res[m]] if ph2_res[m] else []
                        if m in (6, 7):
                            wm += [hbc1, hbc2, happly[-1]]
                    hmm2[kt][m] = t.matmul(PB[m][:, 0:TB],
                                           a_w2[sl][0:kw, m * 128:(m + 1) * 128],
                                           a_hg[0:kw, kt, :],
                                           start=(kt == 0), stop=(kt == NCK - 1), wait=wm, tag="w2mm")
                hw2[sl] = hmm2[kt][7]
            hres = []
            for m in range(8):
                hres.append(v.scalar_tensor_tensor(
                    out=ST[:, m * S + t0: m * S + t0 + TB],
                    in0=PB[m][:, 0:TB], scalar=b2t[:, l * 8 + m:l * 8 + m + 1],
                    in1=ST[:, m * S + t0: m * S + t0 + TB],
                    op0=ALU.add, op1=ALU.add,
                    wait=[hmm2[NCK - 1][m], h_stash], tag="zres"))
                ph2_res[m] = hres[m]
            h_h2lo_prev = hres
        return [x for x in ph2_res if x]

    def emit_tables():
        A0 = 8192
        b_i2p = AR[:, A0:A0 + 32]
        b_lnb = AR[:, A0 + 32:A0 + 64]
        b_qs = AR[:, A0 + 64:A0 + 128]
        b_ks = AR[:, A0 + 128:A0 + 192]
        r_half = AR[0:1, A0 + 192:A0 + 224]
        r_tmp = AR[0:1, A0 + 224:A0 + 288]
        stg = [AR[:, A0 + 512 + i * 1024:A0 + 512 + (i + 1) * 1024] for i in range(2)]
        scr = [AR[:, A0 + 2560 + i * 256:A0 + 2560 + (i + 1) * 256] for i in range(6)]
        c_mh = ccol(-0.5)
        c_p2 = ccol(np.pi / 2)
        c_l2p = ccol(float(np.log(1.0 / (2 * np.pi))))
        c_b = ccol(25.6 / 89.6)
        c_m8 = ccol(-8.0)
        hh = h_rhi
        h1 = s.activation(out=r_tmp[0:1, 0:32], in_=r_half, func=AF.Exp,
                          scale=float(-np.log(10000.0) / 64.0), bias=c_l2p[0][0:1, :],
                          wait=[hh, c_l2p[1]], tag="invf")
        hb1 = t.matmul(PB[6][:, 0:32], onesrow, r_tmp[0:1, 0:32], start=True, stop=True,
                       wait=[h1, h_onesrow], tag="bci2p")
        hb1v = v.tensor_copy(b_i2p, PB[6][:, 0:32], wait=[hb1], tag="evb")
        h2 = s.activation(out=r_tmp[0:1, 32:64], in_=r_half, func=AF.Identity,
                          scale=float(1.0 / 89.6), bias=c_b[0][0:1, :], wait=[hh, c_b[1]], tag="base")
        h3 = s.activation(out=r_tmp[0:1, 32:64], in_=r_tmp[0:1, 32:64], func=AF.Ln,
                          wait=[h2], tag="lnbase")
        hb2 = t.matmul(PB[7][:, 0:32], onesrow, r_tmp[0:1, 32:64], start=True, stop=True,
                       wait=[h3, h_onesrow, hb1], tag="bclnb")
        hb2v = v.tensor_copy(b_lnb, PB[7][:, 0:32], wait=[hb2], tag="evb2")
        htabs = []
        h_rtmp_free = [hb1, hb2]
        hprev = [None, None]
        for l in (0, 1):
            hq = sy.dma(r_tmp[0:1, 0:64], qs_in[l].unsqueeze(0), wait=h_rtmp_free,
                        ch="rows", tag="qsrow")
            hq2 = v.tensor_scalar(out=r_tmp[0:1, 0:64], in0=r_tmp[0:1, 0:64], scalar1=8.0,
                                  scalar2=None, op0=ALU.mult, wait=[hq, hb1v], tag="qs8")
            hqb = t.matmul(PB[6][:, 0:64], onesrow, r_tmp[0:1, 0:64], start=True, stop=True,
                           wait=[hq2, h_onesrow], tag="bcqs")
            hqv = v.tensor_copy(b_qs, PB[6][:, 0:64], wait=[hqb], tag="evqs")
            hk = sy.dma(r_tmp[0:1, 0:64], ks_in[l].unsqueeze(0), wait=[hqb], ch="rows", tag="ksrow")
            hkb = t.matmul(PB[7][:, 0:64], onesrow, r_tmp[0:1, 0:64], start=True, stop=True,
                           wait=[hk, h_onesrow, hb2v], tag="bcks")
            hkv = v.tensor_copy(b_ks, PB[7][:, 0:64], wait=[hkb], tag="evks")
            h_rtmp_free = [hkb]
            for i in range(TT):
                sl = i % 2
                tcol = CONST[:, 32 + sl * 8:33 + sl * 8]
                pcol = CONST[:, 33 + sl * 8:34 + sl * 8]
                u_, w_, aw_ = scr[sl * 3:sl * 3 + 3]
                u, w, aw = u_[:, 0:32], w_[:, 0:32], aw_[:, 0:32]
                iu = u_[:, 32:64].bitcast(I32)
                sc_, isc_, sin_, cos_ = (w_[:, 32:64], w_[:, 64:96],
                                         aw_[:, 32:64], aw_[:, 64:96])
                st4 = stg[sl]
                wv = [hprev[sl]] if hprev[sl] else [hqv, hkv]
                hio = v.tensor_scalar(out=tcol, in0=piota, scalar1=float(128 * i),
                                      scalar2=None, op0=ALU.add, wait=[h_pio] + wv, tag="tcol")
                h4 = v.tensor_scalar(out=u, in0=b_i2p, scalar1=tcol, scalar2=None,
                                     op0=ALU.mult, wait=[hio, hb1v] + wv, tag="u")
                h5 = v.tensor_copy(iu, u, wait=[h4], tag="cvt")
                h6 = v.tensor_copy(w, iu, wait=[h5], tag="back")
                h7 = v.tensor_tensor(out=w, in0=u, in1=w, op=ALU.subtract, wait=[h6], tag="frac")
                h8a = v.tensor_scalar(out=isc_, in0=w, scalar1=0.5, scalar2=None,
                                      op0=ALU.is_gt, wait=[h7], tag="wgt")
                h8 = v.tensor_tensor(out=w, in0=w, in1=isc_, op=ALU.subtract,
                                     wait=[h8a], tag="wrap")
                h9 = s.activation(out=aw, in_=w, func=AF.Abs, wait=[h8], tag="aw")
                h10 = s.activation(out=sin_, in_=w, func=AF.Sin, scale=float(2 * np.pi),
                                   wait=[h9], tag="sin")
                h11 = s.activation(out=cos_, in_=aw, func=AF.Sin, scale=float(-2 * np.pi),
                                   bias=c_p2[0], wait=[h10, c_p2[1]], tag="cos")
                h12 = s.activation(out=pcol, in_=tcol, func=AF.Identity, scale=1.0 / 256.0,
                                   bias=c_m8[0], wait=[hio, c_m8[1]], tag="pcol")
                h13 = v.tensor_scalar(out=u, in0=b_lnb, scalar1=pcol, scalar2=None,
                                      op0=ALU.mult, wait=[h12, h5, hb2v], tag="plnb")
                h14 = s.activation(out=sc_, in_=u, func=AF.Exp, wait=[h13, h11], tag="sc")
                h15 = s.activation(out=isc_, in_=u, func=AF.Exp, scale=-1.0, wait=[h14, h8], tag="isc")
                ha1 = v.tensor_tensor(out=u, in0=cos_, in1=sc_, op=ALU.mult,
                                      wait=[h15, h11], tag="cs")
                ha2 = v.tensor_tensor(out=w, in0=sin_, in1=sc_, op=ALU.mult,
                                      wait=[h14, h10, h8], tag="ss")
                hm = []
                for (dsto, src2, tb, rot) in ((0, u, b_qs, 0), (64, w, b_qs, 32),
                                              (128, None, b_ks, 0), (192, None, b_ks, 32)):
                    pass
                hm.append(v.tensor_tensor(out=st4[:, 0:32], in0=u, in1=b_qs[:, 0:32], op=ALU.mult, wait=[ha1, hqv] + wv, tag="cq0"))
                hm.append(v.tensor_tensor(out=st4[:, 32:64], in0=u, in1=b_qs[:, 32:64], op=ALU.mult, wait=[ha1, hqv], tag="cq1"))
                hm.append(v.scalar_tensor_tensor(out=st4[:, 64:96], in0=w, scalar=-1.0,
                                                 in1=b_qs[:, 32:64], op0=ALU.mult, op1=ALU.mult,
                                                 wait=[ha2, hqv], tag="sq0"))
                hm.append(v.tensor_tensor(out=st4[:, 96:128], in0=w, in1=b_qs[:, 0:32], op=ALU.mult, wait=[ha2, hqv], tag="sq1"))
                ha3 = v.tensor_tensor(out=u, in0=cos_, in1=isc_, op=ALU.mult,
                                      wait=[h15, hm[0], hm[1]], tag="ci")
                ha4 = v.tensor_tensor(out=w, in0=sin_, in1=isc_, op=ALU.mult,
                                      wait=[h15, hm[2], hm[3]], tag="si")
                hm.append(v.tensor_tensor(out=st4[:, 128:160], in0=u, in1=b_ks[:, 0:32], op=ALU.mult, wait=[ha3, hkv], tag="ck0"))
                hm.append(v.tensor_tensor(out=st4[:, 160:192], in0=u, in1=b_ks[:, 32:64], op=ALU.mult, wait=[ha3, hkv], tag="ck1"))
                hm.append(v.scalar_tensor_tensor(out=st4[:, 192:224], in0=w, scalar=-1.0,
                                                 in1=b_ks[:, 32:64], op0=ALU.mult, op1=ALU.mult,
                                                 wait=[ha4, hkv], tag="sk0"))
                hm.append(v.tensor_tensor(out=st4[:, 224:256], in0=w, in1=b_ks[:, 0:32], op=ALU.mult, wait=[ha4, hkv], tag="sk1"))
                hd = g.dma(tabscr[l, :, i * 128:(i + 1) * 128, :].rearrange("f p d -> p f d"),
                            st4[:, 0:256].rearrange("p (f d) -> p f d", f=4),
                            wait=hm, ch=f"tabst{sl}", tag="tabst")
                hprev[sl] = hd
                htabs.append(hd)
        return [ChanTotal("gpsimd", "tabst0"), ChanTotal("gpsimd", "tabst1")]

    wqkvscr = nc.dram_tensor("wqkvscr", [2, 3, 8, 128, 512], BF16, kind="Internal").ap()
    woscr = nc.dram_tensor("woscr", [2, 4, 128, 1024], BF16, kind="Internal").ap()

    def emit_prestage_attn(l):
        hs_ = []
        for part in range(3):
            srcp = wqkv[l, :, part * 512:(part + 1) * 512].rearrange(
                "(kt p) c -> kt p c", p=128)
            hs_.append(g.dma(wqkvscr[l, part], srcp, ch=f"pa{l}", tag="preqkv"))
        hs_.append(g.dma(woscr[l], wo_ffn[l].rearrange("(kt p) c -> kt p c", p=128),
                         ch=f"pa{l}", tag="prewo"))
        return hs_

    def emit_ffn_attn(l, phase_wait=(), dbg=False):
        a_tabs = [AR[:, i * 256:(i + 1) * 256] for i in range(2)]
        a_rsc = [AR[:, 512 + i * 512:512 + (i + 1) * 512] for i in range(2)] + \
                [AR[:, 15968:16480], AR[:, 16640:17152]]
        a_stg = [AR[:, 1536 + i * 256:1536 + (i + 1) * 256].bitcast(BF16) for i in range(2)]
        a_zbc = AR[:, 2048:6144].bitcast(BF16).rearrange("p (c t) -> p c t", c=8)
        a_zbt = AR[:, 6144:6656].bitcast(BF16).rearrange("p (c t) -> p c t", c=8)
        a_slab = [AR[:, 6656 + i * 256:6656 + (i + 1) * 256].bitcast(BF16) for i in range(3)]
        a_PT = AR[:, 0:3200].bitcast(BF16).rearrange("p (s h q) -> p s h q", s=5, h=2)
        a_on = AR[:, 3200:5248].bitcast(BF16).rearrange("p (q c) -> p q c", q=8)
        a_oT = AR[:, 0:2048].bitcast(BF16).rearrange("p (c t) -> p c t", c=4)
        a_woh = [AR[:, 2048:3072].bitcast(BF16).rearrange("p (c t) -> p c t", c=2),
                 AR[:, 5248:6272].bitcast(BF16).rearrange("p (c t) -> p c t", c=2)]
        a_qT = AR[:, 7680:9728].bitcast(BF16).rearrange("p (c t) -> p c t", c=4)
        a_kT = AR[:, 9728:12800].bitcast(BF16).rearrange("p (c t) -> p c t", c=4)
        a_vE = AR[:, 12800:15968].bitcast(BF16).rearrange("p (r t) -> p r t", r=12)
        vE4 = a_vE.rearrange("p r (h w) -> p r h w", h=8)

        PRES = ChanTotal("gpsimd", f"pa{l}")
        TABS = [ChanTotal("gpsimd", "tabst0"), ChanTotal("gpsimd", "tabst1")]
        pw = list(phase_wait)

        hq = sy.dma(ROWS[0:1, 0:64], qs_in[l].unsqueeze(0), wait=pw, ch="rows", tag="qsr")
        hk = sy.dma(ROWS[0:1, 64:128], ks_in[l].unsqueeze(0), wait=[hq], ch="rows", tag="ksr")
        hm1 = v.tensor_reduce(out=ROWS[0:1, 128:129], in_=ROWS[0:1, 0:64], axis=AX.X,
                              op=ALU.max, apply_absolute_value=True, wait=[hk] + pw, tag="mq")
        hm2_ = v.tensor_reduce(out=ROWS[0:1, 129:130], in_=ROWS[0:1, 64:128], axis=AX.X,
                               op=ALU.max, apply_absolute_value=True, wait=[hm1], tag="mk")
        hm3 = v.tensor_tensor(out=ROWS[0:1, 128:129], in0=ROWS[0:1, 128:129],
                              in1=ROWS[0:1, 129:130], op=ALU.mult, wait=[hm2_], tag="mmul")
        hm4_ = v.tensor_scalar(out=ROWS[0:1, 128:129], in0=ROWS[0:1, 128:129], scalar1=-8.0,
                               scalar2=None, op0=ALU.mult, wait=[hm3], tag="m8")
        hm5 = t.matmul(PB[6][:, 0:1], onesrow, ROWS[0:1, 128:129], start=True, stop=True,
                       wait=[hm4_, h_onesrow] + pw, tag="bcm")
        hbias = v.tensor_copy(BIAS[:, l:l + 1], PB[6][:, 0:1], wait=[hm5], tag="evbias")
        h_stash = v.tensor_copy(
            STASH[:, 0:12].rearrange("p (c b) -> p c b", c=4),
            ST3(4, 8, 0, S)[:, :, 1023:S - 1:1024], wait=pw, tag="stashA")
        if dbg:
            hz1 = g.memset(a_vE[:, :, :], 0.0, wait=pw, tag="dbg0")
            hz2 = g.memset(a_kT[:, :, :], 0.0, wait=pw, tag="dbg0")
            pw = pw + [hz1, hz2]
        h_vones = g.memset(vE4[:, :, :, 64:65], 1.0, wait=pw, tag="vones")

        tTE, tV, tS, tG = [hm5] + pw, [h_stash, hbias] + pw, list(pw), [h_vones] + pw
        hslab = [None, None, None]
        htab_rot = [None, None]
        for ch in range(4):
            cs, ce = ch * 8, ch * 8 + 8
            CS, CE = cs * 128, ce * 128
            wz = tV + tTE
            h_zb1 = v.tensor_copy(a_zbc[:, 0:4, :], ST3(0, 4, CS, CE), wait=wz, tag="zbu")
            h_zb2 = v.tensor_copy(a_zbc[:, 4:8, :], ST3(4, 8, CS, CE), wait=wz, tag="zbl")
            if ch == 0:
                hb0 = v.memset(a_zbt[:, 4:8, 0:1], 0.0, wait=wz, tag="zbt0")
            else:
                hb0 = v.tensor_copy(a_zbt[:, 4:8, 0:1],
                                    STASH[:, 0:12].rearrange("p (c b) -> p c b", c=4)[:, :, ch - 1:ch],
                                    wait=wz, tag="zbt0")
            hb1 = v.tensor_copy(a_zbt[:, 4:8, 1:128], ST3(4, 8, CS, CS + 127), wait=wz, tag="zbt1")
            hb2 = v.tensor_copy(a_zbt[:, 0:4, :], ST3(0, 4, CS, CS + 128), wait=wz, tag="zbtu")
            h_zb = [h_zb1, h_zb2, hb0, hb1, hb2]

            def statq(T, c):
                if c < 4:
                    return a_zbc[:, c, T * 128:(T + 1) * 128]
                if T == 0:
                    return a_zbt[:, c, 0:128]
                return a_zbc[:, c, T * 128 - 1:(T + 1) * 128 - 1]

            rotdone = [[None] * 8, [None] * 8, [None] * 8]
            tab_reader = [[], []]
            htr_ev = [[None] * 8, [None] * 8]
            hv_ev = [None] * 8
            tr_bank_guard = [None] * 8
            hrsc_guard = [tTE[0]] * 4
            hstg_guard = [tTE[0], tTE[0]]
            lastG = [tTE[0], tTE[0]]
            for part in range(3):
                hmm_last = [None] * 8
                for kt in range(CT):
                    sl = (part * CT + kt) % 3
                    wd = ([hslab[sl]] if hslab[sl] else pw) + [PRES]
                    hdsl = s.dma(a_slab[sl], wqkvscr[l, part, kt], wait=wd,
                                 ch=f"qkv{sl}", tag="qkvslab")
                    for T in range(8):
                        wm = [hdsl] + h_zb
                        if kt == 0:
                            if part == 0:
                                wm += tTE
                            else:
                                if rotdone[part - 1][T]:
                                    wm.append(rotdone[part - 1][T])
                                if tr_bank_guard[T]:
                                    wm.append(tr_bank_guard[T])
                        hmm_last[T] = t.matmul(
                            PB[T][:, 0:512], statq(T, kt), a_slab[sl],
                            start=(kt == 0), stop=(kt == CT - 1), wait=wm, tag=f"qkv{part}")
                    hslab[sl] = hmm_last[7]
                chst = {}

                def chainA(T, part=part, hmm_last=hmm_last):
                    gt = cs + T
                    sl = T % 2
                    wtb = tab_reader[sl] if tab_reader[sl] else (TABS + tTE)
                    htab = sy.dma(a_tabs[sl].rearrange("p (f d) -> p f d", f=4),
                                  tabscr[l, :, gt * 128:(gt + 1) * 128, :].rearrange("f p d -> p f d"),
                                  wait=wtb, ch=f"tab{sl}", tag="tabtile")
                    htab_rot[sl] = htab
                    tab_reader[sl] = []
                    ps3 = PB[T][:, 0:512].rearrange("p (h d) -> p h d", h=8)
                    nrm = CONST[:, 256 + T * 8:264 + T * 8]
                    rp = 2 * (T % 2)
                    wn = [hmm_last[T]] + ([hrsc_guard[rp]] if hrsc_guard[rp] else [])
                    hsq = s.activation(out=a_rsc[rp], in_=PB[T][:, 0:512], func=AF.Square,
                                       wait=wn, tag="nsq")
                    hrd = v.tensor_reduce(out=nrm, in_=a_rsc[rp].rearrange("p (h d) -> p h d", h=8),
                                          axis=AX.X, op=ALU.add, wait=[hsq], tag="nred")
                    hc1 = v.tensor_scalar(out=nrm, in0=nrm, scalar1=1e-18, scalar2=None,
                                          op0=ALU.max, wait=[hrd], tag="nclip")
                    hl1 = s.activation(out=nrm, in_=nrm, func=AF.Ln, wait=[hc1], tag="nln")
                    hl2 = s.activation(out=nrm, in_=nrm, func=AF.Exp, scale=-0.5, wait=[hl1], tag="nexp")
                    s0 = a_rsc[rp].rearrange("p (h d) -> p h d", h=8)
                    hr0 = v.tensor_tensor(out=s0, in0=ps3,
                                          in1=nrm.unsqueeze(2).broadcast_to([128, 8, 64]),
                                          op=ALU.mult, wait=[hl2], tag="rqn")
                    rotdone[part][T] = hr0
                    chst[T] = (sl, rp, htab, hr0)

                def chainB(T, part=part):
                    gt = cs + T
                    sl, rp, htab, hr0 = chst[T]
                    tb = a_tabs[sl].rearrange("p (f d) -> p f d", f=4)
                    ct_ = tb[:, 2 * part + 0:2 * part + 1, :].broadcast_to([128, 8, 64])
                    st_ = tb[:, 2 * part + 1:2 * part + 2, :].broadcast_to([128, 8, 64])
                    s0 = a_rsc[rp].rearrange("p (h d) -> p h d", h=8)
                    s1 = a_rsc[rp + 1].rearrange("p (h d) -> p h d", h=8)
                    stgv = a_stg[sl].rearrange("p (h d) -> p h d", h=8)
                    g1 = [hrsc_guard[rp + 1]] if hrsc_guard[rp + 1] else []
                    hp0 = v.tensor_tensor(out=stgv, in0=s0, in1=ct_, op=ALU.mult,
                                          wait=[hr0, htab] +
                                          ([hstg_guard[sl]] if hstg_guard[sl] else []), tag="rc")
                    hp1a = g.tensor_tensor(out=s1[:, :, 0:32], in0=s0[:, :, 32:64],
                                           in1=st_[:, :, 0:32], op=ALU.mult,
                                           wait=[hr0, htab] + g1, tag="rsa")
                    hp1b = g.tensor_tensor(out=s1[:, :, 32:64], in0=s0[:, :, 0:32],
                                           in1=st_[:, :, 32:64], op=ALU.mult,
                                           wait=[hr0, htab] + g1, tag="rsb")
                    tab_reader[sl] = tab_reader[sl] + [hp0, hp1a, hp1b]
                    lastG[:] = [hp1a, hp1b]
                    hadd = v.tensor_tensor(out=stgv, in0=stgv, in1=s1, op=ALU.add,
                                           wait=[hp0, hp1a, hp1b], tag="radd")
                    hrsc_guard[rp] = hadd
                    hrsc_guard[rp + 1] = hadd
                    pvt = PB[T].bitcast(BF16)
                    htr = []
                    for c4 in range(4):
                        wtr = [hadd]
                        if c4 == 0 and tr_bank_guard[T]:
                            wtr.append(tr_bank_guard[T])
                        htr.append(t.transpose(pvt[:, c4 * 128:(c4 + 1) * 128],
                                               a_stg[sl][:, c4 * 128:(c4 + 1) * 128],
                                               idb[:, :], wait=wtr, tag="trqk"))
                    if part == 0:
                        dview = a_qT[:, :, T * 128:(T + 1) * 128]
                    else:
                        dview = a_kT[:, :, (gt % 12) * 128:((gt % 12) + 1) * 128]
                    hev = s.activation(out=dview, in_=pvt[:, 0:512].rearrange("p (c t) -> p c t", c=4),
                                       func=AF.Identity, wait=htr + tTE, tag="evqk")
                    htr_ev[part][T] = hev
                    hstg_guard[sl] = hev
                    tr_bank_guard[T] = hev

                for T in range(8):
                    if part == 2:
                        gt = cs + T
                        slot = gt % 12
                        wv = [hmm_last[T], h_vones] + tTE
                        hv_ev[T] = v.tensor_copy(
                            vE4[:, slot, :, 0:64],
                            PB[T][:, 0:512].rearrange("p (h d) -> p h d", h=8),
                            wait=wv, tag="vev")
                        rotdone[2][T] = hv_ev[T]
                        continue
                    chainA(T)
                    if T >= 1:
                        chainB(T - 1)
                if part != 2:
                    chainB(7)
            hA1 = htr_ev[0] + htr_ev[1] + hv_ev
            dbg_tok = []
            if dbg and ch == 0:
                sy.dma(dbg_q, a_qT, wait=hA1, ch="dbg")
                sy.dma(dbg_k, a_kT, wait=hA1, ch="dbg")
                sy.dma(dbg_v, a_vE, wait=hA1 + [h_vones], ch="dbg")
                sy.dma(dbg_t, tabscr[l], ch="dbg", wait=TABS)
                dbg_tok = [ChanTotal("sync", "dbg")]
            scb = {(0, 0): (PB[2], PB[4], 0), (0, 1): (PB[3], PB[6], 0),
                   (1, 0): (PB[5], PB[4], 128), (1, 1): (PB[7], PB[6], 128)}
            jlist = list(range(max(cs - 4, 0), ce))
            h_on_done = []
            hlastPV = None
            a1done = [hA1[-1], hv_ev[-1]]
            allexp = []
            pv_guard = [None, None]
            for pr in range(4):
                exp_guard = {}
                sc_guard = {}
                exp_h = {}
                spans = {}
                for ji, J in enumerate(jlist):
                    slot_ = ji % 2
                    span_lo = max(128 * J, CS)
                    span_hi = min(128 * J + 640, CE)
                    N = span_hi - span_lo
                    N1 = min(N, 512)
                    N2 = N - N1
                    jslot = J % 12
                    spans[J] = (span_lo, N1, N2)
                    hsc = {}
                    for hh in range(2):
                        b512, b128, o128 = scb[(slot_, hh)]
                        kst = a_kT[hh * 64:(hh + 1) * 64, pr, jslot * 128:(jslot + 1) * 128]
                        qrh = a_qT[hh * 64:(hh + 1) * 64, pr, span_lo - CS:span_hi - CS]
                        wsc = a1done + ([exp_guard[(slot_, hh)]] if (slot_, hh) in exp_guard else [])
                        hsc[(hh, 0)] = t.matmul(b512[:, 0:N1], kst, qrh[:, 0:N1],
                                                start=True, stop=True, wait=wsc, tag="sc512")
                        if N2 > 0:
                            hsc[(hh, 1)] = t.matmul(b128[:, o128:o128 + N2], kst, qrh[:, N1:N],
                                                    start=True, stop=True, wait=wsc, tag="sc128")
                    hmk = {}
                    allsc = list(hsc.values())
                    for hh in range(2):
                        b512, b128, o128 = scb[(slot_, hh)]
                        ww = list(allsc)
                        hml = []
                        if J >= cs:
                            hml.append(v.tensor_tensor(out=b512[:, 0:128], in0=b512[:, 0:128],
                                                       in1=MLOC[:, 0:128], op=ALU.add,
                                                       wait=ww + [h_m2], tag="mkf"))
                        if J + 4 <= ce - 1:
                            off = 128 * (J + 4) - span_lo
                            tgt = b512[:, off:off + 128] if off < 512 else b128[:, o128:o128 + 128]
                            hml.append(v.tensor_tensor(out=tgt, in0=tgt, in1=MLOC[:, 128:256],
                                                       op=ALU.add, wait=ww + [h_m4], tag="mkl"))
                        hmk[hh] = hml if hml else ww
                    hex_ = {}
                    ptslot = J % 5
                    for hh in range(2):
                        b512, b128, o128 = scb[(slot_, hh)]
                        wex = hmk[hh] + [hsc[(hh, 0)], hbias]
                        if (hh, 1) in hsc:
                            wex.append(hsc[(hh, 1)])
                        if (slot_, hh) in sc_guard:
                            wex.append(sc_guard[(slot_, hh)])
                        if hlastPV is not None:
                            wex.append(hlastPV)
                        hex_[(hh, 0)] = s.activation(out=a_PT[:, ptslot, hh, 0:N1], in_=b512[:, 0:N1],
                                                     func=AF.Exp, bias=BIAS[:, l:l + 1],
                                                     wait=wex, tag="lexp")
                        if N2 > 0:
                            hex_[(hh, 1)] = s.activation(out=a_PT[:, ptslot, hh, N1:N],
                                                         in_=b128[:, o128:o128 + N2],
                                                         func=AF.Exp, bias=BIAS[:, l:l + 1],
                                                         wait=wex, tag="lexp2")
                        exp_guard[(slot_, hh)] = hex_[(hh, 1)] if (hh, 1) in hex_ else hex_[(hh, 0)]
                        allexp.append(exp_guard[(slot_, hh)])
                    exp_h[J] = hex_
                    if J < cs:
                        sc_guard[(slot_, 0)] = exp_guard[(slot_, 0)]
                        sc_guard[(slot_, 1)] = exp_guard[(slot_, 1)]
                        continue
                    Q = J
                    for hh in range(2):
                        head = 2 * pr + hh
                        js = [jj for jj in range(max(Q - 4, 0), Q + 1)]
                        hpv = None
                        for idx2, J2 in enumerate(js):
                            splo2, N1b, N2b = spans[J2]
                            qoff = 128 * Q - splo2
                            wpv = []
                            he = exp_h[J2]
                            if qoff + 128 <= N1b:
                                wpv.append(he[(hh, 0)])
                            else:
                                wpv.append(he[(hh, 1)])
                                if qoff < N1b:
                                    wpv.append(he[(hh, 0)])
                            first = (idx2 == 0)
                            if first and pv_guard[hh]:
                                wpv.append(pv_guard[hh])
                            if first:
                                wpv += a1done
                            hpv = t.matmul(PB[hh][:, 0:65], a_PT[:, J2 % 5, hh, qoff:qoff + 128],
                                           vE4[:, J2 % 12, head, 0:65],
                                           start=first, stop=(idx2 == len(js) - 1),
                                           wait=wpv, tag="pv")
                        hlastPV = hpv
                        sc_guard[(slot_, hh)] = hpv
                        rc = CONST[:, 48 + hh:49 + hh]
                        hrc = v.reciprocal(out=rc, in_=PB[hh][:, 64:65], wait=[hpv], tag="rcp")
                        hno = v.tensor_scalar(out=a_on[:, Q - cs, head * 64:(head + 1) * 64],
                                              in0=PB[hh][:, 0:64], scalar1=rc, scalar2=None,
                                              op0=ALU.mult, wait=[hrc], tag="onorm")
                        pv_guard[hh] = hno
                        h_on_done.append(hno)
            if dbg and ch == 0:
                sy.dma(dbg_o, a_on, wait=[h_on_done[-1]], ch="dbg")
            hwod1 = sy.dma(a_woh[0], woscr[l, 0:2].rearrange("c p t -> p c t"),
                           wait=[h for h in hslab if h] + [PRES, hlastPV, h_on_done[-1], allexp[-1]] + tTE,
                           ch="wo", tag="wodma")
            hwod = sy.dma(a_woh[1], woscr[l, 2:4].rearrange("c p t -> p c t"),
                          wait=[hwod1], ch="wo", tag="wodma2")
            hev3 = []
            allexp = list(exp_guard.values())
            for Q in range(8):
                pvt = PB[2].bitcast(BF16)
                htrq = []
                for c4 in range(4):
                    wtr = [h_on_done[-1]] + ([hev3[-1]] if hev3 else allexp + [hlastPV])
                    htrq.append(t.transpose(pvt[:, c4 * 128:(c4 + 1) * 128],
                                            a_on[:, Q, c4 * 128:(c4 + 1) * 128], idb[:, :],
                                            wait=wtr, tag="tro"))
                hev3.append(v.tensor_copy(a_oT[:, :, Q * 128:(Q + 1) * 128],
                                          pvt[:, 0:512].rearrange("p (c t) -> p c t", c=4),
                                          wait=htrq, tag="evo"))
            hres = [None] * 8
            hmmw = None
            for half in range(2):
                for m in range(8):
                    for kc in range(4):
                        wmw = [hwod, hev3[-1]]
                        if kc == 0 and half == 0:
                            wmw += [h_on_done[-1], hlastPV, allexp[-1]]
                        if kc == 0 and half == 1:
                            wmw.append(hres[m])
                        hmmw = t.matmul(PB[m][:, 0:512], a_woh[kc // 2][:, kc % 2, m * 128:(m + 1) * 128],
                                        a_oT[:, kc, half * 512:half * 512 + 512],
                                        start=(kc == 0), stop=(kc == 3), wait=wmw, tag="womm")
                    hres[m] = v.tensor_tensor(
                        out=ST[:, m * S + CS + half * 512: m * S + CS + half * 512 + 512],
                        in0=ST[:, m * S + CS + half * 512: m * S + CS + half * 512 + 512],
                        in1=PB[m][:, 0:512], op=ALU.add, wait=[hmmw, h_stash], tag="zatt")
            tTE = [hmmw]
            tV = list(hres) + dbg_tok
            tS = [allexp[-1]]
            tG = list(lastG)
        return tV + tTE



    wqTscr = nc.dram_tensor("wqTscr", [8, 128, 1024], BF16, kind="Internal").ap()
    wkTscr = nc.dram_tensor("wkTscr", [8, 128, 256], BF16, kind="Internal").ap()
    wvTscr = nc.dram_tensor("wvTscr", [8, 128, 256], BF16, kind="Internal").ap()
    woTscr = nc.dram_tensor("woTscr", [8, 128, 1024], BF16, kind="Internal").ap()

    def emit_prestage_big(phase_wait=()):
        stg_in = [AR[:, 6144 + i * 128:6144 + (i + 1) * 128].bitcast(BF16) for i in range(4)]
        stg_out = [AR[:, 6656 + i * 512:6656 + (i + 1) * 512].bitcast(BF16) for i in range(2)]
        pwl = list(phase_wait)
        dst64_q = lambda b: 128 * (b % 8) + 64 * (b // 8)
        dst64_k = lambda b: {0: 0, 2: 64, 1: 128, 3: 192}[b]
        dst64_n = lambda b: 64 * b
        jobs = [(wq, wqTscr, 8, dst64_q), (wk, wkTscr, 2, dst64_k),
                (wv, wvTscr, 2, dst64_n), (wo_big, woTscr, 8, dst64_n)]
        hld_g = [None] * 4
        htr_g = [None, None]
        hst_prev = []
        it = 0
        tc = 0
        for (wsrc, wdst, nrt, dmap) in jobs:
            for ktg in range(4):
                hev_all = [[], []]
                for r in range(nrt):
                    sl = it % 4
                    wl = ([hld_g[sl]] if hld_g[sl] else pwl)
                    hld = g.dma(stg_in[sl], wsrc[r * 128:(r + 1) * 128, ktg * 256:(ktg + 1) * 256],
                                wait=wl, ch=f"bw{sl}", tag="bw")
                    it += 1
                    last_tr = None
                    for kk in range(2):
                        bk = tc % 2
                        tc += 1
                        htr = t.transpose(PB[4 + bk].bitcast(BF16)[:, 0:128],
                                          stg_in[sl][:, kk * 128:(kk + 1) * 128], idb[:, :],
                                          wait=[hld] + ([htr_g[bk]] if htr_g[bk] else []), tag="bwtr")
                        last_tr = htr
                        for hf in range(2):
                            b = 2 * r + hf
                            hev_all[kk].append(v.tensor_copy(
                                stg_out[kk][:, dmap(b):dmap(b) + 64],
                                PB[4 + bk].bitcast(BF16)[:, hf * 64:(hf + 1) * 64],
                                wait=[htr] + hst_prev, tag="bwev"))
                        htr_g[bk] = hev_all[kk][-1]
                    hld_g[sl] = last_tr
                hst_prev = [sy.dma(wdst[ktg * 2 + kk], stg_out[kk][:, 0:nrt * 128],
                                   wait=hev_all[kk], ch="bigs", tag="bws") for kk in range(2)][-1:]
        return [ChanTotal("sync", "bigs")]

    def emit_bigattn(phase_wait=(), dbg=False):
        kTA = AR[:, 0:2048].bitcast(BF16)
        kTB = AR[:, 2048:4096].bitcast(BF16)
        a_vE = AR[:, 4096:8256].bitcast(BF16).rearrange("p (r w) -> p r w", r=32)
        a_y1 = AR[:, 8256:9280].bitcast(BF16).rearrange("p (c t) -> p c t", c=8)
        a_qT = AR[:, 9280:11328].bitcast(BF16).rearrange("p (m t) -> p m t", m=8)
        a_PT = AR[:, 11328:13376].bitcast(BF16).rearrange("p (s h q) -> p s h q", s=4, h=2)
        a_oT = AR[:, 13376:15424].bitcast(BF16).rearrange("p (c t) -> p c t", c=8)
        a_ws = [AR[:, 15424 + i * 512:15424 + (i + 1) * 512].bitcast(BF16) for i in range(2)]
        a_wkv = AR[:, 9280:11328].bitcast(BF16)
        a_sq = AR[:, 11328:11840]
        a_rows = AR[0:1, 11840:12352]
        a_nrows = AR[0:1, 8256:8768]
        a_obn = AR[:, 8768:9024].bitcast(BF16)
        a_ob = AR[:, 12352:12864].bitcast(BF16)
        BIGS = ChanTotal("sync", "bigs")
        pw = list(phase_wait)
        eps_col, h_eps = ccol(1e-5)
        ones_col, h_oc = ccol(1.0)
        l8_col, h_l8 = ccol(float(np.log(0.125)))
        h_it = g.iota(CONST[:, 100:101], pattern=[[1, 1]], base=0, channel_multiplier=1,
                      allow_small_or_imprecise_dtypes=True, wait=pw, tag="iotp")
        h_t1 = v.tensor_scalar(out=CONST[:, 101:102], in0=CONST[:, 100:101], scalar1=64.0,
                               scalar2=None, op0=ALU.is_lt, wait=[h_it], tag="otop")
        h_t2 = v.tensor_scalar(out=CONST[:, 102:103], in0=CONST[:, 100:101], scalar1=64.0,
                               scalar2=None, op0=ALU.is_ge, wait=[h_it], tag="obot")
        ones_top, ones_bot = CONST[:, 101:102], CONST[:, 102:103]
        h_vone = g.memset(a_vE.rearrange("p r (k w) -> p r k w", k=4)[:, :, :, 64:65], 1.0,
                          wait=pw, tag="bvones")

        asq_g = [[]]

        def chan_ln_chunk(t0, TBc, dst, wl, pg):
            b0, b1 = pg
            hsum = []
            hsq = []
            for c in range(CT):
                w = list(wl) if c == 0 else []
                hsum.append(t.matmul(b0[0:1, 0:TBc], ones_col, ST[:, c * S + t0:c * S + t0 + TBc],
                                     start=(c == 0), stop=(c == CT - 1), wait=w + [h_oc], tag="bln1"))
                hq = s.activation(out=a_sq[:, 0:TBc], in_=ST[:, c * S + t0:c * S + t0 + TBc],
                                  func=AF.Square,
                                  wait=w + ([hsq[-1]] if hsq else asq_g[0]), tag="bln2")
                hsq.append(t.matmul(b1[0:1, 0:TBc], ones_col, a_sq[:, 0:TBc],
                                    start=(c == 0), stop=(c == CT - 1), wait=[hq, h_oc] + w, tag="bln3"))
            r_mu = a_rows[0:1, 0:256]
            r_var = a_rows[0:1, 256:512]
            hmu = v.tensor_scalar(out=r_mu[0:1, 0:TBc], in0=b0[0:1, 0:TBc], scalar1=1.0 / D,
                                  scalar2=None, op0=ALU.mult, wait=[hsum[-1]], tag="bln4")
            hv1 = v.tensor_tensor(out=r_var[0:1, 0:TBc], in0=r_mu[0:1, 0:TBc], in1=r_mu[0:1, 0:TBc],
                                  op=ALU.mult, wait=[hmu], tag="bln5")
            hv2 = v.scalar_tensor_tensor(out=r_var[0:1, 0:TBc], in0=b1[0:1, 0:TBc], scalar=1.0 / D,
                                         in1=r_var[0:1, 0:TBc], op0=ALU.mult, op1=ALU.subtract,
                                         wait=[hsq[-1], hv1], tag="bln6")
            hr1 = s.activation(out=r_var[0:1, 0:TBc], in_=r_var[0:1, 0:TBc], func=AF.Ln,
                               bias=eps_col[0:1, :], wait=[hv2, h_eps], tag="bln7")
            hr2 = s.activation(out=r_var[0:1, 0:TBc], in_=r_var[0:1, 0:TBc], func=AF.Exp,
                               scale=-0.5, wait=[hr1], tag="bln8")
            hb0 = t.matmul(b0[:, 0:TBc], onesrow, r_mu[0:1, 0:TBc], start=True, stop=True,
                           wait=[hmu, hr2, h_onesrow], tag="bln9")
            hb1 = t.matmul(b1[:, 0:TBc], onesrow, r_var[0:1, 0:TBc], start=True, stop=True,
                           wait=[hr2, h_onesrow], tag="bln10")
            happ = []
            for c in range(CT):
                ha1 = v.tensor_tensor(out=a_sq[:, 0:TBc], in0=ST[:, c * S + t0:c * S + t0 + TBc],
                                      in1=b0[:, 0:TBc], op=ALU.subtract,
                                      wait=[hb0, hsq[-1]] + ([happ[-1]] if happ else []), tag="bln11")
                ha2 = v.tensor_tensor(out=dst[:, c, :], in0=a_sq[:, 0:TBc], in1=b1[:, 0:TBc],
                                      op=ALU.mult, wait=[ha1, hb1, hsq[-1]], tag="bln12")
                happ.append(ha2)
            asq_g[0] = [happ[-1]]
            return happ

        hkv = []
        hwk = sy.dma(a_wkv[:, 0:2048].rearrange("p (kt c) -> p kt c", kt=8),
                     wkTscr.rearrange("kt p c -> p kt c"), wait=pw + [BIGS], ch="bkv", tag="wkld")
        hwv = sy.dma(a_wkv[:, 2048:4096].rearrange("p (kt c) -> p kt c", kt=8),
                     wvTscr.rearrange("kt p c -> p kt c"), wait=[hwk], ch="bkv", tag="wvld")
        wkv3 = a_wkv.rearrange("p (x kt c) -> p x kt c", x=2, kt=8)
        hkmax = [None, None]
        hln_g = []
        hev_prev = []
        for cc in range(16):
            t0 = cc * 256
            hln = chan_ln_chunk(t0, 256, a_y1, pw + hln_g, (PB[6], PB[7]))
            hln_g = [hln[-1]]
            hk2 = {}
            for (xi, dstk) in ((0, kTA), (1, kTB)):
                hmmk = None
                for kt in range(CT):
                    wm = [hwk, hln[-1]]
                    if kt == 0:
                        wm += hev_prev
                    hmmk = t.matmul(PB[xi][:, 0:256], wkv3[:, 0, kt, 0 + 128 * xi:128 + 128 * xi],
                                    a_y1[:, kt, :], start=(kt == 0), stop=(kt == CT - 1),
                                    wait=wm, tag="kproj")
                hk2[xi] = hmmk
            hv2_ = {}
            for tt in range(2):
                hmmv = None
                for kt in range(CT):
                    wm = [hwv, hln[-1]]
                    if kt == 0:
                        wm += hev_prev
                    hmmv = t.matmul(PB[2 + tt][:, 0:256], a_y1[:, kt, tt * 128:(tt + 1) * 128],
                                    wkv3[:, 1, kt, :], start=(kt == 0), stop=(kt == CT - 1),
                                    wait=wm, tag="vproj")
                hv2_[tt] = hmmv
            hev = []
            hkn_prev = []
            for (xi, dstk) in ((0, kTA), (1, kTB)):
                hq = s.activation(out=a_sq[:, 0:256], in_=PB[xi][:, 0:256], func=AF.Square,
                                  wait=[hk2[xi], hln[-1]] + hkn_prev + asq_g[0], tag="ksq")
                wkn = [hq] + ([hkmax[0]] if hkmax[0] else [])
                hn1 = t.matmul(PB[4][0:1, xi * 256:xi * 256 + 256], ones_top, a_sq[:, 0:256],
                               start=True, stop=True, wait=wkn + [h_t1], tag="kn1")
                hn2 = t.matmul(PB[5][0:1, xi * 256:xi * 256 + 256], ones_bot, a_sq[:, 0:256],
                               start=True, stop=True, wait=wkn + [h_t2], tag="kn2")
                for (pbn, hn, kv) in ((PB[4], hn1, 0 + xi), (PB[5], hn2, 2 + xi)):
                    hr = v.tensor_reduce(out=a_rows[0:1, 320 + kv:321 + kv],
                                         in_=pbn[0:1, xi * 256:xi * 256 + 256],
                                         axis=AX.X, op=ALU.max, wait=[hn] + ([hkmax[0]] if hkmax[0] else []),
                                         tag="kred")
                    if cc == 0:
                        hm_ = v.tensor_copy(ROWS[0:1, 296 + kv:297 + kv], a_rows[0:1, 320 + kv:321 + kv],
                                            wait=[hr], tag="kmax0")
                    else:
                        hm_ = v.tensor_tensor(out=ROWS[0:1, 296 + kv:297 + kv],
                                              in0=ROWS[0:1, 296 + kv:297 + kv],
                                              in1=a_rows[0:1, 320 + kv:321 + kv], op=ALU.max,
                                              wait=[hr, hkmax[1]] if hkmax[1] else [hr], tag="kmax")
                    hkmax = [hm_, hm_]
                hkn_prev = [hn1, hn2]
                asq_g[0] = [hn1, hn2]
                hev.append(v.tensor_copy(dstk[:, t0:t0 + 256], PB[xi][:, 0:256],
                                         wait=[hk2[xi], hq], tag="kev"))
            for tt in range(2):
                gt = cc * 2 + tt
                hev.append(v.tensor_copy(
                    a_vE.rearrange("p r (k w) -> p r k w", k=4)[:, gt, :, 0:64],
                    PB[2 + tt][:, 0:256].rearrange("p (k d) -> p k d", k=4),
                    wait=[hv2_[tt], h_vone], tag="vev"))
            hev_prev = hev
            hln_g = [hln[-1]] + hev
        if dbg:
            sy.dma(dbg_k[:, 0:2, :].rearrange("p a t -> p (a t)")[:, 0:3072],
                   AR[:, 0:3072].bitcast(BF16)[:, 0:3072], wait=hev_prev + [hkmax[0]], ch="dbg")
            sy.dma(dbg_v[:, :, 0:260], a_vE[:, 0:12, :], wait=hev_prev, ch="dbg")
        tTE = hev_prev[:1]
        tV = hev_prev
        hwslab = [None, None]
        prev_wo = []
        prev_norm = []
        BIGW = [ChanTotal("sync", "bigs")]
        hexp_all = []
        dbg_gtok = []
        for gq in range(8):
            G0 = gq * 512
            hlnq_g = [tV[-1], hkmax[0]] + prev_wo + prev_norm
            hqmm = [None] * 8
            hqev = []
            for hf in range(2):
                hln = chan_ln_chunk(G0 + hf * 256, 256, a_y1, hlnq_g, (PB[6], PB[7]))
                hlnq_g = [hln[-1]]
                for kt in range(CT):
                    sl = kt % 2
                    wd = [hwslab[sl]] if hwslab[sl] else (pw + BIGW)
                    hdw = s.dma(a_ws[sl], wqTscr[kt], wait=wd, ch=f"bws{sl}", tag="wqld")
                    for M in range(8):
                        wm = [hdw, hln[-1]]
                        if kt == 0:
                            wm += [hqev[M]] if hf == 1 else prev_wo
                            if hf == 0:
                                wm += prev_norm
                        hqmm[M] = t.matmul(PB[M][:, hf * 256:hf * 256 + 256],
                                           a_ws[sl][:, M * 128:(M + 1) * 128], a_y1[:, kt, :],
                                           start=(kt == 0), stop=(kt == CT - 1), wait=wm,
                                           tag="qproj")
                    hwslab[sl] = hqmm[7]
                for M in range(8):
                    hqev.append(v.tensor_copy(a_qT[:, M, hf * 256:hf * 256 + 256],
                                              PB[M][:, hf * 256:hf * 256 + 256],
                                              wait=[hqmm[M]] + prev_norm, tag="qev"))
                hlnq_g = hlnq_g + hqev[-2:]
            hnorm = []
            asql = [a_sq, AR[:, 11840:12352]]
            qbg = [list(asq_g[0]), []]
            hqs = {}

            def qnA(M):
                b = M % 2
                hqs[M] = s.activation(out=asql[b][:, 0:512], in_=a_qT[:, M, :], func=AF.Square,
                                      wait=[hqev[M], hqev[8 + M]] + qbg[b], tag="qsq")

            def qnB(M):
                b = M % 2
                hn1 = t.matmul(PB[0][0:1, 0:512], ones_top, asql[b][:, 0:512], start=True,
                               stop=True, wait=[hqs[M], h_t1, hqev[0], hqev[8]] + ([hnorm[-2]] if hnorm else []), tag="qn1")
                hn2 = t.matmul(PB[1][0:1, 0:512], ones_bot, asql[b][:, 0:512], start=True,
                               stop=True, wait=[hqs[M], h_t2, hqev[1], hqev[9]] + ([hnorm[-1]] if hnorm else []), tag="qn2")
                hr1 = v.tensor_reduce(out=ROWS[0:1, 300 + M:301 + M], in_=PB[0][0:1, 0:512],
                                      axis=AX.X, op=ALU.max, wait=[hn1], tag="qred1")
                hr2 = v.tensor_reduce(out=ROWS[0:1, 308 + M:309 + M], in_=PB[1][0:1, 0:512],
                                      axis=AX.X, op=ALU.max, wait=[hn2], tag="qred2")
                hnorm.extend([hr1, hr2])
                qbg[b] = [hn1, hn2]

            for M in range(8):
                qnA(M)
                if M >= 1:
                    qnB(M - 1)
            qnB(7)
            asq_g[0] = qbg[0]
            hb1_ = v.tensor_copy(ROWS[0:1, 316:332].rearrange("p (a b) -> p a b", a=4),
                                 ROWS[0:1, 296:300].unsqueeze(2).broadcast_to([1, 4, 4]),
                                 wait=[hkmax[0]] + hnorm[-2:], tag="kkrow")
            hb2_ = v.tensor_tensor(out=ROWS[0:1, 300:316], in0=ROWS[0:1, 300:316],
                                   in1=ROWS[0:1, 316:332], op=ALU.mult, wait=[hb1_] + hnorm, tag="qk")
            hb3_ = s.activation(out=ROWS[0:1, 300:316], in_=ROWS[0:1, 300:316], func=AF.Ln,
                                wait=[hb2_], tag="lnqk")
            hb4_ = s.activation(out=ROWS[0:1, 300:316], in_=ROWS[0:1, 300:316], func=AF.Exp,
                                scale=0.5, bias=l8_col[0:1, :], wait=[hb3_, h_l8], tag="mrow")
            hb5_ = t.matmul(PB[2][:, 0:16], onesrow, ROWS[0:1, 300:316], start=True, stop=True,
                            wait=[hb4_, h_onesrow] + prev_wo, tag="bcm16")
            hbias = v.tensor_scalar(out=BIAS[:, 8:24], in0=PB[2][:, 0:16], scalar1=-1.0,
                                    scalar2=None, op0=ALU.mult, wait=[hb5_] + hexp_all[-1:], tag="nbias")
            if dbg and gq == 0:
                sy.dma(dbg_q.rearrange("p a b -> p (a b)"), a_qT.rearrange("p m t -> p (m t)"),
                       wait=hqev + [hbias], ch="dbg")
                dbg_gtok = [ChanTotal("sync", "dbg")]
            allexp = []
            hno = []
            norm_q = []
            for M in range(8):
                heads = (M, M + 8)
                pvb = 6 if M % 2 == 0 else 0
                kts = kTA if M < 4 else kTB
                njt = 4 * gq + 4
                hpv = [None, None]
                hexp_slot = {}
                scg = {}
                for bt0 in range(0, njt, 4):
                    jts = list(range(bt0, min(bt0 + 4, njt)))
                    hexp_b = {}
                    for jt in jts:
                        r = jt - 4 * gq
                        qlo = max(0, r) * 128 if r >= 0 else 0
                        N = 512 - (qlo)
                        sb0 = PB[2 + 2 * (jt % 2)]
                        sb1 = PB[3 + 2 * (jt % 2)]
                        for hh in range(2):
                            bnk = (sb0, sb1)[hh]
                            wsc = [hqev[M], hbias] + (hno[-1:] if hno else [])
                            if (jt % 2, hh) in scg:
                                wsc.append(scg[(jt % 2, hh)])
                            kst = kts[hh * 64:(hh + 1) * 64, jt * 128:(jt + 1) * 128]
                            hs_ = t.matmul(bnk[:, 0:N], kst,
                                           a_qT[hh * 64:(hh + 1) * 64, M, qlo:512],
                                           start=True, stop=True, wait=wsc + tTE, tag="bsc")
                            if r >= 0:
                                hs_ = v.tensor_tensor(out=bnk[:, 0:128], in0=bnk[:, 0:128],
                                                      in1=MLOC[:, 0:128], op=ALU.add,
                                                      wait=[hs_, h_m2], tag="bmask")
                            wex = [hs_] + ([hexp_slot.get((jt % 4, hh))] if hexp_slot.get((jt % 4, hh)) else [])
                            if hpv[hh]:
                                wex.append(hpv[hh])
                            he = s.activation(out=a_PT[:, jt % 4, hh, qlo:512], in_=bnk[:, 0:N],
                                              func=AF.Exp, scale=0.125,
                                              bias=BIAS[:, 8 + heads[hh]:9 + heads[hh]],
                                              wait=wex, tag="bexp")
                            hexp_b[(jt, hh)] = he
                            hexp_slot[(jt % 4, hh)] = he
                            scg[(jt % 2, hh)] = he
                            allexp.append(he)
                            hexp_all.append(he)
                    for jt in jts:
                        r = jt - 4 * gq
                        qlo = max(0, r) * 128 if r >= 0 else 0
                        N = 512 - qlo
                        kv = M // 4 if False else None
                        for hh in range(2):
                            head = heads[hh]
                            kvh = head // 4
                            wpv = [hexp_b[(jt, hh)], h_vone]
                            if jt == 0:
                                wpv += [hno[-1]] if hno else prev_norm + tTE
                            hpv[hh] = t.matmul(PB[pvb + hh][0:65, qlo:512],
                                               a_vE[:, jt, kvh * 65:kvh * 65 + 65],
                                               a_PT[:, jt % 4, hh, qlo:512],
                                               start=(jt == 0), stop=(jt == njt - 1),
                                               wait=wpv, tag="bpv")
                norm_q.append((M, heads, pvb, list(hpv)))
                while len(norm_q) > (1 if M < 7 else 0):
                    Mn, headsn, pvbn, hpvn = norm_q.pop(0)
                    for hh in range(2):
                        head = headsn[hh]
                        hrc = v.reciprocal(out=a_nrows[0:1, 0:512], in_=PB[pvbn + hh][64:65, 0:512],
                                           wait=[hpvn[hh]] + ([hno[-1]] if hno else []), tag="brcp")
                        hoev = v.tensor_copy(a_obn[0:64, 0:512], PB[pvbn + hh][0:64, 0:512],
                                             wait=[hpvn[hh], hrc], tag="boev")
                        hbc = t.matmul(PB[pvbn + hh][:, 0:512], onesrow, a_nrows[0:1, 0:512],
                                       start=True, stop=True, wait=[hrc, hoev], tag="brbc")
                        hmul = v.tensor_tensor(
                            out=a_oT[64 * (head % 2):64 * (head % 2) + 64, head // 2, :],
                            in0=a_obn[0:64, 0:512], in1=PB[pvbn + hh][64:128, 0:512], op=ALU.mult,
                            wait=[hbc, hoev], tag="bnorm")
                        hno.append(hmul)
            if dbg and gq == 0:
                sy.dma(dbg_o, a_oT, wait=[hno[-1]], ch="dbg")
            hwo = [None] * 8
            hres = []
            for kt in range(CT):
                sl = kt % 2
                hdw = s.dma(a_ws[sl], woTscr[kt], wait=[hwslab[sl]], ch=f"bws{sl}", tag="wold")
                for m in range(8):
                    wm = [hdw, hno[-1]]
                    if kt == 0:
                        wm += [allexp[-1], hnorm[-1], hno[-1]]
                    hwo[m] = t.matmul(PB[m][:, 0:512], a_ws[sl][:, m * 128:(m + 1) * 128],
                                      a_oT[:, kt, :], start=(kt == 0), stop=(kt == CT - 1),
                                      wait=wm, tag="bwomm")
                hwslab[sl] = hwo[7]
            for m in range(8):
                hres.append(v.tensor_tensor(out=ST[:, m * S + G0:m * S + G0 + 512],
                                            in0=ST[:, m * S + G0:m * S + G0 + 512],
                                            in1=PB[m][:, 0:512], op=ALU.add,
                                            wait=[hwo[m]], tag="bres"))
            prev_wo = [hwo[7]]
            prev_norm = list(hres) + dbg_gtok
            tTE = [hwo[7]]
            tV = list(hres)
        return tV + tTE

    def emit_resid_combine(scale, phase_wait=()):
        slot = [dict(evac=[None, None], rd=[]), dict(evac=[None, None], rd=[])]
        hout = []
        for i in range(TT):
            sl = i % 2
            stt = slot[sl]
            xt = AR[:, sl * D:(sl + 1) * D]
            hld = sy.dma(xt, hs[i * 128:(i + 1) * 128, :],
                         wait=stt["rd"] + (list(phase_wait) if i < 2 else []), ch=f"rc{sl}", tag="rcld")
            pbank = (PB[sl * 2], PB[sl * 2 + 1])
            evs = []
            for c in range(CT):
                pb = pbank[c // 4]
                po = (c % 4) * 128
                w = [hld] + (list(phase_wait) if i < 2 else [])
                if stt["evac"][c // 4] is not None:
                    w.append(stt["evac"][c // 4])
                evs.append(t.transpose(pb[:, po:po + 128], xt[:, c * 128:(c + 1) * 128],
                                       idf[:, :], wait=w, tag="trR"))
            for half in range(2):
                hv = v.scalar_tensor_tensor(
                    out=ST3(half * 4, (half + 1) * 4, i * 128, (i + 1) * 128),
                    in0=ST3(half * 4, (half + 1) * 4, i * 128, (i + 1) * 128),
                    scalar=float(scale),
                    in1=pbank[half][:, :].rearrange("p (c t) -> p c t", c=4),
                    op0=ALU.mult, op1=ALU.add,
                    wait=evs[half * 4:half * 4 + 4] + (list(phase_wait) if i < 2 else []),
                    tag="rcomb")
                stt["evac"][half] = hv
                hout.append(hv)
            stt["rd"] = evs
        return hout[-4:]

    def emit_chan_ln(phase_wait=()):
        hout = []
        prev = {}
        eps_col, h_eps = ccol(1.0e-5)
        ones_col, h_oc = ccol(1.0)
        for i in range(8):
            t0 = i * 512
            r_mu = ROWS[0:1, 130:642] if False else AR[0:1, 2048 + 0:2048 + 512]
            r_mu = AR[0:1, 2048:2560]
            r_var = AR[0:1, 2560:3072]
            r_rst = AR[0:1, 3072:3584]
            sq = AR[:, 3584 + (i % 2) * 512: 3584 + (i % 2 + 1) * 512]
            hsum = []
            hsq = []
            sqg = prev.get(("sq", i % 2))
            for c in range(CT):
                w = list(phase_wait) if i < 2 else []
                if c == 0 and ("b6" in prev):
                    w.append(prev["b6"])
                hsum.append(t.matmul(PB[6][0:1, 0:512], ones_col, ST[:, c * S + t0:c * S + t0 + 512],
                                     start=(c == 0), stop=(c == CT - 1),
                                     wait=w + [h_oc], tag="clnsum"))
                wq = w + ([sqg] if sqg and c < 2 else [])
                hq = s.activation(out=sq, in_=ST[:, c * S + t0:c * S + t0 + 512], func=AF.Square,
                                  wait=wq + ([hsq[-1]] if hsq else []), tag="clnsq")
                w7 = [hq] + ([prev["b7"]] if c == 0 and "b7" in prev else [])
                hsq.append(t.matmul(PB[7][0:1, 0:512], ones_col, sq,
                                    start=(c == 0), stop=(c == CT - 1), wait=w7 + [h_oc], tag="clnssq"))
            hmu = v.tensor_scalar(out=r_mu, in0=PB[6][0:1, 0:512], scalar1=1.0 / D, scalar2=None,
                                  op0=ALU.mult, wait=[hsum[-1]] + ([prev["mu"]] if "mu" in prev else []),
                                  tag="cmu")
            hv1 = v.tensor_tensor(out=r_var, in0=r_mu, in1=r_mu, op=ALU.mult,
                                  wait=[hmu] + ([prev["var"]] if "var" in prev else []), tag="cmu2")
            hv2 = v.scalar_tensor_tensor(out=r_var, in0=PB[7][0:1, 0:512], scalar=1.0 / D,
                                         in1=r_var, op0=ALU.mult, op1=ALU.subtract,
                                         wait=[hsq[-1], hv1], tag="cvar")
            hr1 = s.activation(out=r_rst, in_=r_var, func=AF.Ln, bias=eps_col[0:1, :],
                               wait=[hv2, h_eps] + ([prev["rst"]] if "rst" in prev else []), tag="clnr")
            hr2 = s.activation(out=r_rst, in_=r_rst, func=AF.Exp, scale=-0.5, wait=[hr1], tag="crstd")
            hb6 = t.matmul(PB[6][:, 0:512], onesrow, r_mu, start=True, stop=True,
                           wait=[hmu, hr2, h_onesrow], tag="cbcmu")
            hb7 = t.matmul(PB[7][:, 0:512], onesrow, r_rst, start=True, stop=True,
                           wait=[hr2, h_onesrow], tag="cbcr")
            happ = []
            for c in range(CT):
                sl2 = c % 2
                scr2 = AR[:, 4608 + sl2 * 512:4608 + (sl2 + 1) * 512]
                wa = [hb6] + ([happ[c - 2][0]] if c >= 2 else [])
                ha1 = v.tensor_tensor(out=scr2, in0=ST[:, c * S + t0:c * S + t0 + 512],
                                      in1=PB[6][:, 0:512], op=ALU.subtract, wait=wa, tag="capsub")
                ha2 = v.tensor_tensor(out=ST[:, c * S + t0:c * S + t0 + 512], in0=scr2,
                                      in1=PB[7][:, 0:512], op=ALU.mult, wait=[ha1, hb7, hsq[-1], hsum[-1]],
                                      tag="capmul")
                happ.append((ha1, ha2))
            prev = {"b6": happ[-1][1], "b7": happ[-1][1], "mu": hb6, "var": hv2, "rst": hb7,
                    ("sq", i % 2): hsq[-1]}
            hout = [h[1] for h in happ]
        return hout

    def emit_spill(phase_wait=()):
        hs_ = []
        for c in range(CT):
            hs_.append(sy.dma(xscr[c * 128:(c + 1) * 128, :], ST[:, c * S:(c + 1) * S],
                              wait=list(phase_wait), ch="spill", tag="spill"))
        return [ChanTotal("sync", "spill")]

    def emit_final_combine(phase_wait=()):
        hout = []
        hprev = [[], []]
        for i in range(8):
            sl = i % 2
            xb = AR[:, sl * 4096:(sl + 1) * 4096]
            hld = sy.dma(xb.rearrange("p (c t) -> p c t", c=CT),
                         xscr[:, i * 512:(i + 1) * 512].rearrange("(c p) t -> p c t", p=128),
                         wait=list(phase_wait) + hprev[sl], ch=f"fc{sl}", tag="fcld")
            h = v.scalar_tensor_tensor(out=ST3(0, CT, i * 512, (i + 1) * 512),
                                       in0=ST3(0, CT, i * 512, (i + 1) * 512), scalar=0.5,
                                       in1=xb.rearrange("p (c t) -> p c t", c=CT),
                                       op0=ALU.mult, op1=ALU.add,
                                       wait=[hld] + list(phase_wait), tag="fcomb")
            hprev[sl] = [h]
            hout.append(h)
        return hout[-2:]

    if stage in ("ffn1", "x1", "y1", "x2", "full"):
        for l in (0, 1):
            emit_prestage_attn(l)
            emit_prestage_mlp(l)
        htb = emit_tables()
    hz0 = emit_ln_load_transpose()
    if stage == "z0":
        fin = emit_store_stream(extra_wait=hz0)
        Sd.sync.raw(lambda eng: eng.nop(), wait=fin, tag="final")
        Sd.finalize()
        es.close()
        return nc

    if stage == "devA":
        htb = emit_tables()
        hp = emit_prestage_attn(0)
        ha = emit_ffn_attn(0, phase_wait=hz0 + [ChanTotal("gpsimd", "pa0")] + htb, dbg=True)
        fin = emit_store_stream(extra_wait=ha)
        Sd.sync.raw(lambda eng: eng.nop(), wait=fin, tag="final")
        Sd.finalize()
        es.close()
        return nc

    if stage in ("ffn1", "x1", "y1", "x2", "full"):
        hpb = emit_prestage_big(phase_wait=[]) if stage in ("x2", "full") else []
        hph = hz0 + htb + hpb
        _os = os
        NLAY = int(_os.environ.get("KNL", "2"))
        for l in range(NLAY):
            hph = emit_ffn_attn(l, phase_wait=hph)
            if not _os.environ.get("KSKIPB"):
                hph = emit_mlp(l, phase_wait=hph, dbg=(l == 0 and bool(_os.environ.get("KDBGB"))))
                hph = hph + ([ChanTotal("sync", "dbg")] if l == 0 and _os.environ.get("KDBGB") else [])
        if stage == "x2":
            hph = emit_resid_combine(0.5, phase_wait=hph)
            hph = emit_bigattn(phase_wait=hph + hpb, dbg=bool(_os.environ.get("KDBG")))
            hph = hph + ([ChanTotal("sync", "dbg")] if _os.environ.get("KDBG") else [])
            fin = emit_store_stream(extra_wait=hph)
            Sd.sync.raw(lambda eng: eng.nop(), wait=fin, tag="final")
            Sd.finalize()
            es.close()
            return nc
        if stage in ("x1", "y1"):
            hph = emit_resid_combine(0.5, phase_wait=hph)
            if stage == "y1":
                hph = emit_chan_ln(phase_wait=hph)
            fin = emit_store_stream(extra_wait=hph)
            Sd.sync.raw(lambda eng: eng.nop(), wait=fin, tag="final")
            Sd.finalize()
            es.close()
            return nc
        if stage == "ffn1":
            fin = emit_store_stream(extra_wait=hph)
            Sd.sync.raw(lambda eng: eng.nop(), wait=fin, tag="final")
            Sd.finalize()
            es.close()
            return nc
        if stage == "full":
            hph = emit_resid_combine(0.5, phase_wait=hph)
            hph = emit_bigattn(phase_wait=hph + hpb)
            hsp = emit_spill(phase_wait=hph)
            hph = emit_chan_ln(phase_wait=hph + hsp)
            for l in (0, 1):
                hph = emit_ffn_attn(l, phase_wait=hph)
                hph = emit_mlp(l, phase_wait=hph)
            hph = emit_final_combine(phase_wait=hph + hsp)
            fin = emit_store_stream(extra_wait=hph)
            Sd.sync.raw(lambda eng: eng.nop(), wait=fin, tag="final")
            Sd.finalize()
            es.close()
            return nc

    if stage == "devB":
        hp = emit_prestage_mlp(0)
        hm = emit_mlp(0, phase_wait=hz0 + [ChanTotal("gpsimd", "pm0")])
        fin = emit_store_stream(extra_wait=hm)
        Sd.sync.raw(lambda eng: eng.nop(), wait=fin, tag="final")
        Sd.finalize()
        es.close()
        return nc

    raise NotImplementedError(stage)


def kernel(**inputs):
    stage = inputs.pop("_stage", "full")
    nc = build(stage)
    hsl = np.ascontiguousarray(inputs["L0_attn_hidden_states"])
    names = ["L0_attn_Wq", "L0_attn_Wk", "L0_attn_Wv", "L0_attn_Wo",
             "L0_ffn_wqkv", "L0_ffn_q_scale", "L0_ffn_k_scale", "L0_ffn_wo",
             "L0_ffn_ln_g", "L0_ffn_ln_b", "L0_ffn_w1", "L0_ffn_b1",
             "L0_ffn_w2", "L0_ffn_b2"]
    base = {n: np.ascontiguousarray(inputs[n]) for n in names}
    in_maps = [dict(base, hs=hsl[i]) for i in range(8)]
    res = run_bass_kernel_spmd(nc, in_maps, core_ids=list(range(8)))
    return np.stack([r["out"] for r in res.results], axis=0)
